# version 22
# baseline (speedup 1.0000x reference)
import math
import numpy as np
import concourse.bass as bass
import concourse.mybir as mybir
from concourse.bass_utils import run_bass_kernel_spmd
from contextlib import ExitStack

F32 = mybir.dt.float32
BF16 = mybir.dt.bfloat16
I32 = mybir.dt.int32
AF = mybir.ActivationFunctionType
ALU = mybir.AluOpType
AX = mybir.AxisListType

DM = 1024
S_OWN = 4096
QK_SCALE = 0.125
DEQ = float(1.0 / 127.0)
EPS = 1e-5
ENGINES = ("sync", "gpsimd", "tensor", "scalar", "vector")
NENG = len(ENGINES)


class Sched:

    def __init__(self, cnt, hwm):
        self.cnt = cnt
        self.hwm = hwm
        self.ops = {e: [] for e in ENGINES}

    def n(self, sem):
        return self.cnt.get(sem, 0)

    def add(self, eng, fn, waits=(), inc=None, incn=1, chain=False, deps=()):
        w = [(s, v) for (s, v) in waits if v > 0]
        dl = list(deps)
        if chain and self.ops[eng]:
            dl.append((eng, len(self.ops[eng]) - 1))
        cnt_after = None
        if inc is not None:
            self.cnt[inc] = self.cnt.get(inc, 0) + incn
            cnt_after = self.cnt[inc]
        op = {"fn": fn, "waits": w, "deps": [], "inc": inc, "incn": incn,
              "cnt_after": cnt_after, "selfneed": False}
        for (teng, tidx) in dl:
            top = self.ops[teng][tidx]
            if top["inc"] is None:
                top["selfneed"] = True
            op["deps"].append((teng, tidx))
        self.ops[eng].append(op)
        return (eng, len(self.ops[eng]) - 1)

    def replay(self, nc, sems):
        hwm = self.hwm
        selfcount = {}
        for eng in ENGINES:
            sem = "sf" + eng
            n = self.cnt.get(sem, 0)
            for idx, op in enumerate(self.ops[eng]):
                if op["inc"] is not None:
                    selfcount[(eng, idx)] = (op["inc"], op["cnt_after"])
                elif op["selfneed"]:
                    n += 1
                    selfcount[(eng, idx)] = (sem, n)
            self.cnt[sem] = n
        with nc.Block() as blk:
            for eng_name in ENGINES:
                ops = self.ops[eng_name]

                def make(ops=ops, eng_name=eng_name):
                    def f(e):
                        for op in ops:
                            waits = list(op["waits"])
                            for tok in op["deps"]:
                                waits.append(selfcount[tok])
                            for (s, v) in waits:
                                key = (eng_name, s)
                                if v > 0 and hwm.get(key, 0) < v:
                                    e.wait_ge(sems[s], v)
                                    hwm[key] = v
                            ins = op["fn"](e)
                            if op["inc"] is not None:
                                ins.then_inc(sems[op["inc"]], op["incn"])
                            elif op["selfneed"]:
                                ins.then_inc(sems["sf" + eng_name], 1)
                    return f

                getattr(blk, eng_name)(make())


def build_program(taps=(), stop_after=None):
    _ORDER = ["ATT", "FFN"]

    def _do(phname):
        return stop_after is None or _ORDER.index(phname) <= _ORDER.index(stop_after)

    nc = bass.Bass("TRN2", target_bir_lowering=False)

    x_in = nc.dram_tensor("x", [S_OWN, DM], F32, kind="ExternalInput").ap()
    Wq = nc.dram_tensor("Wq", [DM, DM], F32, kind="ExternalInput").ap()
    Wk = nc.dram_tensor("Wk", [DM, DM], F32, kind="ExternalInput").ap()
    Wv = nc.dram_tensor("Wv", [DM, DM], F32, kind="ExternalInput").ap()
    Wfc = nc.dram_tensor("Wfc", [DM, DM], F32, kind="ExternalInput").ap()
    wg_in = nc.dram_tensor("wg", [3072, DM], I32, kind="ExternalInput").ap()
    wu_in = nc.dram_tensor("wu", [3072, DM], I32, kind="ExternalInput").ap()
    wd_in = nc.dram_tensor("wd", [DM, 3072], I32, kind="ExternalInput").ap()
    y_out = nc.dram_tensor("y", [S_OWN, DM], F32, kind="ExternalOutput").ap()

    d_WqT = nc.dram_tensor("d_WqT", [8, 128, 1024], BF16).ap()
    d_WkT = nc.dram_tensor("d_WkT", [8, 128, 1024], BF16).ap()
    d_WvT = nc.dram_tensor("d_WvT", [8, 128, 1024], BF16).ap()
    d_WfcT = nc.dram_tensor("d_WfcT", [8, 128, 1024], BF16).ap()
    d_wgT = nc.dram_tensor("d_wgT", [24, 128, 1024], BF16).ap()
    d_wuT = nc.dram_tensor("d_wuT", [24, 128, 1024], BF16).ap()
    d_wdT = nc.dram_tensor("d_wdT", [8, 128, 3072], BF16).ap()
    d_x1 = nc.dram_tensor("d_x1", [S_OWN, DM], F32).ap()
    d_xn2T = nc.dram_tensor("d_xn2T", [8, 128, S_OWN], BF16).ap()

    tap_aps = {}
    for name, shape, dt in [
        ("QKV", [3, 4, 128, 1024], BF16),
        ("A", [4, 128, 256], BF16),
        ("O", [4, 128, 1024], F32),
        ("x1", [S_OWN, DM], F32),
        ("xn2T", [8, 128, S_OWN], BF16),
    ]:
        if name in taps:
            tap_aps[name] = nc.dram_tensor("tap_" + name, shape, dt,
                                           kind="ExternalOutput").ap()

    cnt = {}
    hwm = {}
    bar_k = [0]

    with ExitStack() as TOP:
        E = TOP.enter_context
        sem_names = [
            "ph", "ini",
            "sfsync", "sfgpsimd", "sftensor", "sfscalar", "sfvector",
            "ail0", "ail1", "ail2", "ail3", "afl0", "afl1", "afl2", "afl3",
            "a0c", "a0t", "a0p", "as0", "as1", "as2", "as3",
            "aiu0", "aiu1", "aiu2", "aiu3",
            "wl", "axl0", "axl1", "axl2", "axl3", "aln", "alna", "alnx", "alnr", "axt", "axe",
            "apj", "apev", "asx", "asb", "aex", "amx", "agp", "apl", "aav", "aot", "aote",
            "arp", "arev", "aadd", "ax2", "ax2t", "ax2e", "axs", "ax2s",
            "axs0", "axs1", "ax2s0", "ax2s1",
            "dwg0", "dwg1", "dwu0", "dwu1", "dg", "dsil", "dmul", "dwd0", "dwd1", "ddn",
            "dev", "dtp", "dadd", "dxl", "dx2l0", "dx2l1", "dy", "tapd",
        ]
        sems = {s: E(nc.semaphore(s)) for s in sem_names}

        def barrier():
            bar_k[0] += 1
            k = bar_k[0]
            with nc.Block() as blk:
                for en in ENGINES:
                    def f(e, k=k):
                        e.nop().then_inc(sems["ph"], 1)
                        e.wait_ge(sems["ph"], NENG * k)
                    getattr(blk, en)(f)

        def tap_block(items):
            todo = [(nm, fn) for (nm, fn) in items if nm in taps]
            if not todo:
                return
            with nc.Block() as blk:
                @blk.sync
                def _(sy):
                    t0 = cnt.get("tapd", 0)
                    add = [0]
                    def issue(dst, src):
                        sy.dma_start(dst, src).then_inc(sems["tapd"], 16)
                        add[0] += 16
                    for (_, fn) in todo:
                        fn(issue)
                    sy.wait_ge(sems["tapd"], t0 + add[0])
                    cnt["tapd"] = t0 + add[0]
            barrier()

        ident = E(nc.sbuf_tensor("ident", [128, 128], BF16))
        WqJ = [E(nc.sbuf_tensor(f"WqJ{j}", [128, 1024], BF16)) for j in range(8)]
        WkJ = [E(nc.sbuf_tensor(f"WkJ{j}", [128, 1024], BF16)) for j in range(8)]
        WvJ = [E(nc.sbuf_tensor(f"WvJ{j}", [128, 1024], BF16)) for j in range(8)]
        WfJ = [E(nc.sbuf_tensor(f"WfJ{j}", [128, 1024], BF16)) for j in range(8)]

        c_eps = E(nc.sbuf_tensor("c_eps", [128, 1], F32))

        with nc.Block() as blk, \
             nc.sbuf_tensor("io1", [128, 128], F32) as io1, \
             nc.sbuf_tensor("io2", [128, 128], F32) as io2:

            @blk.gpsimd
            def _(g):
                g.iota(io1[:], [[1, 128]], channel_multiplier=0,
                       allow_small_or_imprecise_dtypes=True)
                g.iota(io2[:], [[0, 128]], channel_multiplier=1,
                       allow_small_or_imprecise_dtypes=True)

                g.memset(c_eps[:], EPS).then_inc(sems["ini"], 1)

            @blk.vector
            def _(v):
                v.wait_ge(sems["ini"], 1)
                v.tensor_tensor(ident[:], io1[:], io2[:], ALU.is_equal)
        barrier()

        a0_jobs = []
        for (src, dst, nch, ncol, sm, is_int, ldq) in [
            (Wq, WqJ, 8, 1024, "qk", False, "gpsimd"),
            (Wk, WkJ, 8, 1024, "no", False, "gpsimd"),
            (Wv, WvJ, 8, 1024, "no", False, "gpsimd"),
            (Wfc, WfJ, 8, 1024, "no", False, "gpsimd"),
            (wg_in, d_wgT, 24, 1024, "no", True, "sync"),
            (wu_in, d_wuT, 24, 1024, "no", True, "gpsimd"),
            (wd_in, d_wdT, 8, 3072, "no", True, "sync"),
        ]:
            for c in range(nch):
                a0_jobs.append((src, dst, c, ncol, sm, is_int, ldq))
        NJ = len(a0_jobs)
        nb_of = [ncol // 128 for (_, _, _, ncol, _, _, _) in a0_jobs]
        nb_pre = [sum(nb_of[:j]) for j in range(NJ + 1)]

        with ExitStack() as ph:
            e = ph.enter_context
            wstg_i = [e(nc.sbuf_tensor(f"wstg_i{i}", [128, 3072], I32)) for i in range(4)]
            wstg_b = [e(nc.sbuf_tensor(f"wstg_b{i}", [128, 3072], BF16)) for i in range(4)]
            wpk = [e(nc.sbuf_tensor(f"wpk{i}", [128, 3072], BF16)) for i in range(4)]
            wps = [e(nc.psum_tensor(f"wps{i}", [128, 512], F32)) for i in range(2)]

            S = Sched(cnt, hwm)
            n_ld = {"ail": [0, 0, 0, 0], "aiu": [0, 0, 0, 0]}
            n_fl = [0, 0, 0, 0]
            n_sp = [0, 0, 0, 0]
            for j, (src, dst, c, ncol, sm, is_int, ldq) in enumerate(a0_jobs):
                nb = nb_of[j]
                sl = j % 4
                stg_free = nb_pre[j - 3] if j >= 4 else 0
                if is_int:
                    lsem = "ail" if ldq == "sync" else "aiu"
                    n_ld[lsem][sl] += 1
                    S.add(ldq,
                          lambda e, j=j, src=src, c=c, ncol=ncol:
                          e.dma_start(wstg_i[j % 4][:, 0:ncol],
                                      src[c * 128:(c + 1) * 128, :]),
                          waits=[("a0c", S.n("a0c") - 3)],
                          inc="%s%d" % (lsem, sl), incn=16)
                    S.add("scalar",
                          lambda e, j=j, ncol=ncol:
                          e.activation(wstg_b[j % 4][:, 0:ncol], wstg_i[j % 4][:, 0:ncol],
                                       AF.Copy, bias=0.0, scale=DEQ),
                          waits=[("%s%d" % (lsem, sl), 16 * n_ld[lsem][sl]),
                                 ("a0t", stg_free)],
                          inc="a0c")
                else:
                    n_fl[sl] += 1
                    S.add("gpsimd",
                          lambda e, j=j, src=src, c=c, ncol=ncol:
                          e.dma_start(wstg_b[j % 4][:, 0:ncol],
                                      src[c * 128:(c + 1) * 128, :]),
                          waits=[("a0t", stg_free)],
                          inc="afl%d" % sl, incn=16)
                cast_seen = S.n("a0c")
                for b in range(nb):
                    w = [("a0p", nb_pre[j] + b - 1)]
                    if b == 0:
                        w.append(("a0c", cast_seen) if is_int
                                 else ("afl%d" % sl, 16 * n_fl[sl]))
                    S.add("tensor",
                          lambda e, j=j, b=b:
                          e.matmul(wps[b % 2][:, 0:128],
                                   wstg_b[j % 4][:, b * 128:(b + 1) * 128],
                                   ident[:], start=True, stop=True),
                          waits=w, inc="a0t")
                    pw = [("a0t", S.n("a0t"))]
                    direct = isinstance(dst, list)
                    if b == 0 and not direct:
                        pw.append(("as%d" % sl, 16 * n_sp[sl]))
                    if direct:
                        pdst = (lambda e, dst=dst, b=b, c=c:
                                dst[b][:, c * 128:(c + 1) * 128])
                    else:
                        pdst = (lambda e, j=j, b=b: wpk[j % 4][:, b * 128:(b + 1) * 128])
                    if sm == "qk":
                        S.add("vector",
                              lambda e, pdst=pdst, b=b:
                              e.tensor_scalar(pdst(e), wps[b % 2][:, 0:128],
                                              QK_SCALE, None, ALU.mult),
                              waits=pw, inc="a0p")
                    else:
                        S.add("vector",
                              lambda e, pdst=pdst, b=b:
                              e.tensor_copy(pdst(e), wps[b % 2][:, 0:128]),
                              waits=pw, inc="a0p")
                if not isinstance(dst, list):
                    n_sp[sl] += 1
                    S.add("scalar",
                          lambda e, j=j, dst=dst, c=c, ncol=ncol:
                          e.dma_start(dst[c], wpk[j % 4][:, 0:ncol]),
                          waits=[("a0p", S.n("a0p"))], inc="as%d" % sl, incn=16)
            S.add("sync", lambda e: e.nop(),
                  waits=[("as%d" % i, 16 * n_sp[i]) for i in range(4)])
            S.replay(nc, sems)
        barrier()

        if not _do("ATT"):
            return nc, tap_aps
        with ExitStack() as ph:
            e = ph.enter_context
            xcT = [e(nc.sbuf_tensor(f"xcT{i}", [128, 1024], F32)) for i in range(8)]
            xnm = e(nc.sbuf_tensor("xnm", [128, DM], BF16))
            xnT = [e(nc.sbuf_tensor(f"xnT{j}", [128, 512], BF16)) for j in range(8)]
            Lr = [e(nc.sbuf_tensor(f"Lr{i}", [128, 8], F32)) for i in range(8)]
            Lr2 = [e(nc.sbuf_tensor(f"Lr2_{i}", [128, 8], F32)) for i in range(4)]
            sqd = e(nc.sbuf_tensor("sqd", [128, DM], BF16))
            Qs = [e(nc.sbuf_tensor(f"Qs{t}", [128, 1024], BF16)) for t in range(4)]
            Ks = [e(nc.sbuf_tensor(f"Ks{t}", [128, 1024], BF16)) for t in range(4)]
            Vs = [e(nc.sbuf_tensor(f"Vs{t}", [128, 1024], BF16)) for t in range(8)]
            Ss = e(nc.sbuf_tensor("Ss", [128, 256], F32))
            Sm = e(nc.sbuf_tensor("Sm", [128, 16], F32))
            Se = [e(nc.sbuf_tensor(f"Se{i}", [128, 256], BF16)) for i in range(4)]
            Sl = e(nc.sbuf_tensor("Sl", [128, 16], F32))
            Sr = e(nc.sbuf_tensor("Sr", [128, 16], F32))
            Snr = Sm
            Aa = [e(nc.sbuf_tensor(f"Aa{i}", [128, 256], BF16)) for i in range(4)]
            gramB = [e(nc.sbuf_tensor(f"gramB{i}", [128, 4096], BF16))
                     for i in range(2)]
            avB = [e(nc.sbuf_tensor(f"avB{i}", [128, 4096], BF16)) for i in range(2)]
            gh1 = e(nc.sbuf_tensor("gh1", [128, 2048], BF16))
            OT = [e(nc.sbuf_tensor(f"OT{j}", [128, 512], BF16)) for j in range(8)]
            x1p = [e(nc.sbuf_tensor(f"x1p{i}", [128, 1024], F32)) for i in range(2)]
            xn2m = e(nc.sbuf_tensor("xn2m", [128, DM], BF16))
            xn2s = [e(nc.sbuf_tensor(f"xn2s{j}", [128, 512], BF16)) for j in range(8)]
            pA = [e(nc.psum_tensor(f"pA{i}", [128, 512], F32)) for i in range(2)]
            pB = [e(nc.psum_tensor(f"pB{i}", [128, 512], F32)) for i in range(2)]
            pT = [e(nc.psum_tensor(f"pT{i}", [128, 128], F32)) for i in range(2)]
            pOT = e(nc.psum_tensor("pOT", [128, 1024], F32))

            S = Sched(cnt, hwm)
            cap = {}
            pTw = [None, None]
            pQw = {}

            def ln_acc(Sch, xsrc_fn, Lrt, extra_waits=()):
                Sch.add("scalar",
                        lambda e: e.activation(sqd[:], xsrc_fn(), AF.Copy,
                                               bias=0.0, scale=1.0,
                                               accum_out=Lrt[:, 0:1]),
                        waits=[("alna", Sch.n("alna"))] + list(extra_waits),
                        inc="alna")
                Sch.add("scalar",
                        lambda e: e.activation(sqd[:], xsrc_fn(), AF.Square,
                                               bias=0.0, scale=1.0,
                                               accum_out=Lrt[:, 1:2]),
                        waits=[("alna", Sch.n("alna"))] + list(extra_waits),
                        inc="alna")
                return Sch.n("alna")

            def ln_mid(Sch, Lrt, acc_n):
                Sch.add("vector", lambda e: e.tensor_scalar(
                    Lrt[:, 2:3], Lrt[:, 0:1], -1.0 / 1024.0, None, ALU.mult),
                        waits=[("alna", acc_n)], chain=True)
                Sch.add("vector", lambda e: e.tensor_scalar(
                    Lrt[:, 3:4], Lrt[:, 1:2], 1.0 / 1024.0, None, ALU.mult),
                        chain=True)
                Sch.add("vector", lambda e: e.tensor_mul(
                    Lrt[:, 4:5], Lrt[:, 2:3], Lrt[:, 2:3]), chain=True)
                Sch.add("vector", lambda e: e.tensor_sub(
                    Lrt[:, 5:6], Lrt[:, 3:4], Lrt[:, 4:5]), chain=True, inc="aln")
                Sch.add("scalar",
                        lambda e: e.activation(Lrt[:, 6:7], Lrt[:, 5:6], AF.Ln,
                                               bias=c_eps[:, 0:1], scale=1.0),
                        waits=[("aln", Sch.n("aln")), ("alnx", Sch.n("alnx"))],
                        inc="alnx")
                Sch.add("scalar",
                        lambda e: e.activation(Lrt[:, 7:8], Lrt[:, 6:7], AF.Exp,
                                               bias=0.0, scale=-0.5),
                        waits=[("alnx", Sch.n("alnx"))], inc="alnx")
                return Sch.n("alnx")

            def ln_app(Sch, xsrc_fn, Lrt, out_fn, mid_n, extra_waits=(),
                       out_inc=None):
                Sch.add("vector",
                        lambda e: e.tensor_scalar(out_fn(), xsrc_fn(),
                                                  Lrt[:, 2:3], Lrt[:, 7:8],
                                                  ALU.add, ALU.mult),
                        waits=[("alnx", mid_n)] + list(extra_waits),
                        chain=True, inc=out_inc)

            def load_tile(k):
                w = [("axl%d" % (k % 4), 16 * (k // 4))]
                if k - 8 >= 0:
                    w.append(("arev", 2 * (k - 8 + 1)))
                S.add("sync",
                      lambda e, k=k:
                      e.dma_start(xcT[k % 8][:], x_in[k * 128:(k + 1) * 128, :]),
                      waits=w, inc="axl%d" % (k % 4), incn=16)

            def ln1_a(k):
                w = [("axl%d" % (k % 4), 16 * (k // 4 + 1))]
                if k - 8 >= 0:
                    w.append(("alnr", k - 7))
                cap["l1a", k] = ln_acc(S, (lambda k=k: xcT[k % 8][:]), Lr[k % 8],
                                       extra_waits=w)

            def ln1_m(k):
                cap["l1m", k] = ln_mid(S, Lr[k % 8], cap["l1a", k])

            def ln1_c(k):
                ln_app(S,
                       (lambda k=k: xcT[k % 8][:]),
                       Lr[k % 8],
                       (lambda: xnm[:]),
                       cap["l1m", k],
                       extra_waits=[("axt", 8 * k)],
                       out_inc="alnr")

            def lnt(k):
                i = k % 4
                for b in range(8):
                    w = [("alnr", k + 1)] if b == 0 else []
                    if pTw[b % 2] is not None:
                        w.append(pTw[b % 2])
                    S.add("tensor",
                          lambda e, b=b:
                          e.matmul(pT[b % 2][:], xnm[:, b * 128:(b + 1) * 128],
                                   ident[:], start=True, stop=True),
                          waits=w, inc="axt")
                    ev_w = [("axt", S.n("axt"))]
                    if ("apj", k - 4) in cap:
                        ev_w.append(("apj", cap["apj", k - 4]))
                    S.add("scalar",
                          lambda e, b=b, i=i:
                          e.copy(xnT[b][:, i * 128:(i + 1) * 128], pT[b % 2][:]),
                          waits=ev_w, inc="axe")
                    pTw[b % 2] = ("axe", S.n("axe"))

            def proj(k):
                tt = k % 4
                for (Wjs, Dst, kind) in ((WqJ, Qs[tt], "qk"), (WkJ, Ks[tt], "qk"),
                                         (WvJ, Vs[k % 8], "v")):
                    for half in range(2):
                        pb = (pA if half == 0 else pB)[tt % 2]
                        key = (half, tt % 2)
                        for j in range(8):
                            w = []
                            if j == 0:
                                w = [("axe", 8 * (k + 1))]
                                if pQw.get(key):
                                    w.append(pQw[key])
                            S.add("tensor",
                                  lambda e, Wjs=Wjs, tt=tt, half=half, j=j, pb=pb:
                                  e.matmul(pb[:],
                                           xnT[j][:, tt * 128:(tt + 1) * 128],
                                           Wjs[j][:, half * 512:(half + 1) * 512],
                                           start=(j == 0), stop=(j == 7)),
                                  waits=w, inc=("apj" if j == 7 else None))
                        ev_w = [("apj", S.n("apj"))]
                        if kind == "qk":
                            if ("gram", k - 4) in cap:
                                ev_w.append(("agp", cap["gram", k - 4]))
                        else:
                            if ("pool", k - 8) in cap:
                                ev_w.append(("apl", cap["pool", k - 8]))
                        S.add("scalar",
                              lambda e, Dst=Dst, half=half, pb=pb:
                              e.copy(Dst[:, half * 512:(half + 1) * 512], pb[:]),
                              waits=ev_w, inc="apev")
                        pQw[key] = ("apev", S.n("apev"))
                cap["apev", k] = S.n("apev")
                cap["apj", k] = S.n("apj")

            def front(k):
                tt = k % 4
                e4 = k % 4
                for g in range(4):
                    S.add("vector",
                          lambda e, tt=tt, g=g:
                          e.tensor_tensor(
                              gramB[g % 2][:].rearrange("p (t h d) -> p t h d",
                                                        t=4, d=64),
                              bass.AP(Qs[tt][:].tensor, Qs[tt][:].offset,
                                      [[1024, 128], [0, 4], [64, 16], [1, 64]]),
                              bass.AP(Ks[tt][:].tensor, Ks[tt][:].offset + 256 * g,
                                      [[1024, 128], [64, 4], [0, 16], [1, 64]]),
                              ALU.mult),
                          waits=([("apev", cap["apev", k])] if g == 0 else []),
                          chain=True, inc=None)
                    S.add("vector",
                          lambda e, g=g:
                          e.tensor_tensor(
                              gh1[:].rearrange("p (a b) -> p a b", b=32),
                              bass.AP(gramB[g % 2][:].tensor,
                                      gramB[g % 2][:].offset,
                                      [[4096, 128], [64, 64], [1, 32]]),
                              bass.AP(gramB[g % 2][:].tensor,
                                      gramB[g % 2][:].offset + 32,
                                      [[4096, 128], [64, 64], [1, 32]]),
                              ALU.add),
                          chain=True)
                    S.add("vector",
                          lambda e, g=g:
                          e.tensor_reduce(
                              Ss[:, 64 * g:64 * (g + 1)]
                              .rearrange("p (a b) -> p a b", b=1),
                              gh1[:].rearrange("p (a b) -> p a b", b=32),
                              AX.X, ALU.add),
                          chain=True, inc="agp")
                cap["gram", k] = S.n("agp")
                S.add("vector",
                      lambda e:
                      e.reduce_max(Sm[:, 0:16].rearrange("p (a b) -> p a b", b=1),
                                   bass.AP(Ss[:].tensor, Ss[:].offset,
                                           [[256, 128], [1, 16], [16, 16]]),
                                   AX.X),
                      chain=True)
                sub_w = [("amx", k - 3)] if k - 4 >= 0 else []
                S.add("vector",
                      lambda e, e4=e4:
                      e.tensor_tensor(Se[e4][:].rearrange("p (t h) -> p t h", h=16),
                                      Ss[:].rearrange("p (t h) -> p t h", h=16),
                                      bass.AP(Sm[:].tensor, Sm[:].offset,
                                              [[16, 128], [0, 16], [1, 16]]),
                                      ALU.subtract),
                      waits=sub_w, chain=True, inc="asb")
                S.add("scalar",
                      lambda e, e4=e4: e.activation(Se[e4][:], Se[e4][:], AF.Exp,
                                                    bias=0.0, scale=1.0),
                      waits=[("asb", k + 1)], inc="aex")

            def back_a(k):
                s, tt = divmod(k, 4)
                e4 = k % 4
                S.add("vector",
                      lambda e, e4=e4:
                      e.reduce_sum(Sl[:, 0:16].rearrange("p (a b) -> p a b", b=1),
                                   bass.AP(Se[e4][:].tensor, Se[e4][:].offset,
                                           [[256, 128], [1, 16], [16, 16]]),
                                   AX.X),
                      waits=[("aex", k + 1)])
                S.add("vector", lambda e: e.reciprocal(Sr[:], Sl[:]), chain=True)
                S.add("vector", lambda e: e.tensor_mul(Snr[:], Sl[:], Sr[:]),
                      chain=True)
                S.add("vector",
                      lambda e: e.tensor_scalar(Snr[:], Snr[:], -1.0, 2.0,
                                                ALU.mult, ALU.add),
                      chain=True)
                S.add("vector", lambda e: e.tensor_mul(Sr[:], Sr[:], Snr[:]),
                      chain=True)
                am_w = []
                if ("pool", k - 4) in cap:
                    am_w.append(("apl", cap["pool", k - 4]))
                S.add("vector",
                      lambda e, e4=e4:
                      e.tensor_tensor(Aa[e4][:].rearrange("p (t h) -> p t h", h=16),
                                      Se[e4][:].rearrange("p (t h) -> p t h", h=16),
                                      bass.AP(Sr[:].tensor, Sr[:].offset,
                                              [[16, 128], [0, 16], [1, 16]]),
                                      ALU.mult),
                      waits=am_w, chain=True, inc="amx")
                for g in range(4):
                    S.add("gpsimd",
                          lambda e, e4=e4, g=g, k=k:
                          e.tensor_tensor(
                              avB[g % 2][:].rearrange("p (t h d) -> p t h d",
                                                      t=4, d=64),
                              bass.AP(Aa[e4][:].tensor, Aa[e4][:].offset + 64 * g,
                                      [[256, 128], [16, 4], [1, 16], [0, 64]]),
                              bass.AP(Vs[k % 8][:].tensor,
                                      Vs[k % 8][:].offset + 256 * g,
                                      [[1024, 128], [64, 4], [0, 16], [1, 64]]),
                              ALU.mult),
                          waits=[("amx", k + 1), ("aot", 4 * k + g - 1)],
                          inc="apl")
                cap["pool", k] = S.n("apl")

            def back_a2(k):
                tt = k % 4
                for g in range(4):
                    for i in range(4):
                        for b in range(8):
                            w = []
                            if i == 0 and b == 0:
                                w.append(("apl", 4 * k + g + 1))
                                if g == 0:
                                    w.append(("aote", 8 * k))
                            S.add("tensor",
                                  lambda e, g=g, i=i, b=b:
                                  e.matmul(pOT[:, 128 * b:128 * (b + 1)],
                                           avB[g % 2][:, 1024 * i + 128 * b:
                                                       1024 * i + 128 * (b + 1)],
                                           ident[:],
                                           start=(g == 0 and i == 0
                                                  and b % 4 == 0),
                                           stop=(g == 3 and i == 3
                                                 and b % 4 == 3)),
                                  waits=w,
                                  inc=("aot" if (i == 3 and b == 7) else None))
                ev_w0 = [("aot", 4 * (k + 1))]
                if k - 4 >= 0:
                    ev_w0.append(("arp", 2 * (k - 4 + 1)))
                for b in range(8):
                    S.add("scalar",
                          lambda e, b=b, tt=tt:
                          e.copy(OT[b][:, tt * 128:(tt + 1) * 128],
                                 pOT[:, 128 * b:128 * (b + 1)]),
                          waits=(ev_w0 if b == 0 else []), inc="aote")

            def back_b(k):
                s, tt = divmod(k, 4)
                e4 = k % 4
                for half in range(2):
                    pb = (pA if half == 0 else pB)[tt % 2]
                    key = (half, tt % 2)
                    for j in range(8):
                        w = []
                        if j == 0:
                            w = [("aote", 8 * (k + 1))]
                            if pQw.get(key):
                                w.append(pQw[key])
                        S.add("tensor",
                              lambda e, tt=tt, half=half, j=j, pb=pb:
                              e.matmul(pb[:],
                                       OT[j][:, tt * 128:(tt + 1) * 128],
                                       WfJ[j][:, half * 512:(half + 1) * 512],
                                       start=(j == 0), stop=(j == 7)),
                              waits=w, inc=("arp" if j == 7 else None))
                    r_w = [("arp", S.n("arp"))]
                    if k - 2 >= 0:
                        r_w.append(("axs%d" % (k % 2), 16 * (k // 2)))
                        if ("lnsq", k - 2) in cap:
                            r_w.append(("alna", cap["lnsq", k - 2]))
                    S.add("vector",
                          lambda e, half=half, k=k, pb=pb:
                          e.tensor_add(x1p[k % 2][:, 512 * half:512 * (half + 1)],
                                       pb[:],
                                       xcT[k % 8][:, 512 * half:512 * (half + 1)]),
                          waits=r_w, inc="arev")
                    pQw[key] = ("arev", S.n("arev"))
                S.add("sync",
                      lambda e, k=k:
                      e.dma_start(d_x1[k * 128:(k + 1) * 128, :], x1p[k % 2][:]),
                      waits=[("arev", 2 * (k + 1)),
                             ("axs%d" % (k % 2), 16 * (k // 2))],
                      inc="axs%d" % (k % 2), incn=16)
                w2 = [("arev", 2 * (k + 1))]
                if k - 4 >= 0:
                    w2.append(("ax2", k - 3))
                cap["l2a", k] = ln_acc(S, (lambda k=k: x1p[k % 2][:]), Lr2[e4],
                                       extra_waits=w2)
                cap["lnsq", k] = S.n("alna")

            def back_b2m(k):
                cap["l2m", k] = ln_mid(S, Lr2[k % 4], cap["l2a", k])

            def back_b2c(k):
                s, tt = divmod(k, 4)
                e4 = k % 4
                ln_app(S,
                       (lambda k=k: x1p[k % 2][:]),
                       Lr2[e4],
                       (lambda: xn2m[:]),
                       cap["l2m", k],
                       extra_waits=[("ax2t", 8 * k)],
                       out_inc="ax2")
                for b in range(8):
                    w = [("ax2", k + 1)] if b == 0 else []
                    if pTw[b % 2] is not None:
                        w.append(pTw[b % 2])
                    S.add("tensor",
                          lambda e, b=b:
                          e.matmul(pT[b % 2][:], xn2m[:, b * 128:(b + 1) * 128],
                                   ident[:], start=True, stop=True),
                          waits=w, inc="ax2t")
                    ev_w = [("ax2t", S.n("ax2t"))]
                    if s >= 1:
                        ev_w.append(("ax2s%d" % ((s - 1) % 2),
                                     128 * ((s - 1) // 2 + 1)))
                    S.add("scalar",
                          lambda e, b=b, tt=tt:
                          e.copy(xn2s[b][:, tt * 128:(tt + 1) * 128], pT[b % 2][:]),
                          waits=ev_w, inc="ax2e")
                    pTw[b % 2] = ("ax2e", S.n("ax2e"))
                if tt == 3:
                    for j in range(8):
                        w = [("ax2e", S.n("ax2e"))]
                        if j == 0:
                            w.append(("ax2s%d" % (s % 2), 128 * (s // 2)))
                        S.add("sync",
                              lambda e, j=j, s=s:
                              e.dma_start(d_xn2T[j][:, s * 512:(s + 1) * 512],
                                          xn2s[j][:]),
                              waits=w, inc="ax2s%d" % (s % 2), incn=16)

            def a1(k):
                if 0 <= k <= 31:
                    back_a(k)

            def a2(k):
                if 0 <= k <= 31:
                    back_a2(k)

            def op(k):
                if 0 <= k <= 31:
                    back_b(k)

            def op2m(k):
                if 0 <= k <= 31:
                    back_b2m(k)

            def op2c(k):
                if 0 <= k <= 31:
                    back_b2c(k)

            def g_load(k):
                if 5 <= k <= 31:
                    load_tile(k)

            def g_a(k):
                if 4 <= k <= 31:
                    ln1_a(k)

            def g_m(k):
                if 3 <= k <= 31:
                    ln1_m(k)

            def g_c(k):
                if 2 <= k <= 31:
                    ln1_c(k)
                    lnt(k)
                    proj(k)

            for k in range(5):
                load_tile(k)
            for k in range(4):
                ln1_a(k)
            for k in range(3):
                ln1_m(k)
            for k in range(2):
                ln1_c(k)
                lnt(k)
                proj(k)
            for T in range(36):
                if T <= 31:
                    front(T)
                a1(T - 1)
                op2c(T - 5)
                op(T - 3)
                a2(T - 2)
                op2m(T - 4)
                g_load(T + 5)
                g_a(T + 4)
                g_m(T + 3)
                g_c(T + 2)
            op2c(31)
            S.add("sync", lambda e: e.nop(),
                  waits=[("axs0", 16 * 16), ("axs1", 16 * 16),
                         ("ax2s0", 128 * 4), ("ax2s1", 128 * 4)])
            S.replay(nc, sems)
        barrier()

        tap_block([
            ("x1", lambda issue: issue(tap_aps["x1"], d_x1)),
            ("xn2T", lambda issue: [issue(tap_aps["xn2T"][j], d_xn2T[j])
                                    for j in range(8)]),
        ])

        if not _do("FFN"):
            return nc, tap_aps
        with ExitStack() as ph:
            e4 = ph.enter_context
            wgS = [e4(nc.sbuf_tensor(f"wgS{i}", [128, 1024], BF16)) for i in range(2)]
            wuS = [e4(nc.sbuf_tensor(f"wuS{i}", [128, 1024], BF16)) for i in range(2)]
            wdC = [e4(nc.sbuf_tensor(f"wdC{i}", [128, 3072], BF16)) for i in range(2)]
            hseg = [e4(nc.sbuf_tensor(f"hseg{h}", [128, 512], BF16)) for h in range(24)]
            sg = [e4(nc.sbuf_tensor(f"sg{i}", [128, 512], BF16)) for i in range(2)]
            x2s = [e4(nc.sbuf_tensor(f"x2s{j}_0", [128, 512], BF16)) for j in range(8)]
            x2s2 = [e4(nc.sbuf_tensor(f"x2s{j}_1", [128, 512], BF16)) for j in range(8)]
            x1s = e4(nc.sbuf_tensor("x1s", [128, 4096], F32))
            yst = [e4(nc.sbuf_tensor(f"yst{tb}", [128, 1024], F32)) for tb in range(4)]
            rb2 = [e4(nc.sbuf_tensor(f"rb2{i}", [128, 512], BF16)) for i in range(2)]
            pG = [e4(nc.psum_tensor(f"pG{i}", [128, 512], F32)) for i in range(2)]
            pU = [e4(nc.psum_tensor(f"pU{i}", [128, 512], F32)) for i in range(2)]
            pD = [e4(nc.psum_tensor(f"pD{i}", [128, 512], F32)) for i in range(2)]
            pT2 = [e4(nc.psum_tensor(f"pT2{i}", [128, 128], F32)) for i in range(2)]
            x2buf = [x2s, x2s2]

            S = Sched(cnt, hwm)
            y_pend = []
            for s8 in range(8):
                xp = s8 % 2
                for j in range(8):
                    S.add("sync",
                          lambda e, j=j, s8=s8, xp=xp:
                          e.dma_start(x2buf[xp][j][:],
                                      d_xn2T[j][:, s8 * 512:(s8 + 1) * 512]),
                          waits=[("dg", 2 * 24 * (s8 - 1))],
                          inc="dx2l%d" % xp, incn=16)
                X2L = 16 * (S.n("dx2l%d" % xp) // 16)
                for h in range(24):
                    idx = 24 * s8 + h
                    S.add("sync", lambda e, h=h: e.dma_start(wgS[h % 2][:], d_wgT[h]),
                          waits=[("dg", 2 * (idx - 1))],
                          inc="dwg%d" % (h % 2), incn=16)
                    S.add("gpsimd", lambda e, h=h: e.dma_start(wuS[h % 2][:], d_wuT[h]),
                          waits=[("dg", 2 * (idx - 1))],
                          inc="dwu%d" % (h % 2), incn=16)
                    for j in range(8):
                        w = []
                        if j == 0:
                            w = [("dwg%d" % (h % 2), 16 * (idx // 2 + 1)),
                                 ("dsil", idx - 1),
                                 ("dx2l%d" % xp, X2L)]
                        S.add("tensor",
                              lambda e, h=h, j=j, xp=xp:
                              e.matmul(pG[h % 2][:], wgS[h % 2][:, j * 128:(j + 1) * 128],
                                       x2buf[xp][j][:],
                                       start=(j == 0), stop=(j == 7)),
                              waits=w, inc=("dg" if j == 7 else None))
                    S.add("scalar",
                          lambda e, h=h: e.activation(sg[h % 2][:], pG[h % 2][:],
                                                      AF.Silu, bias=0.0, scale=1.0),
                          waits=[("dg", S.n("dg")), ("dmul", idx - 1)], inc="dsil")
                    for j in range(8):
                        w = [("dmul", idx - 1)] if j == 0 else []
                        if j == 0:
                            w.append(("dwu%d" % (h % 2), 16 * (idx // 2 + 1)))
                        S.add("tensor",
                              lambda e, h=h, j=j, xp=xp:
                              e.matmul(pU[h % 2][:], wuS[h % 2][:, j * 128:(j + 1) * 128],
                                       x2buf[xp][j][:],
                                       start=(j == 0), stop=(j == 7)),
                              waits=w, inc=("dg" if j == 7 else None))
                    S.add("vector",
                          lambda e, h=h: e.tensor_mul(hseg[h][:], sg[h % 2][:],
                                                      pU[h % 2][:]),
                          waits=[("dsil", S.n("dsil")), ("dg", S.n("dg")),
                                 ("ddn", 8 * s8)],
                          inc="dmul")
                    if h == 1:
                        for fn in y_pend:
                            fn()
                        y_pend = []
                S.add("sync",
                      lambda e, s8=s8:
                      e.dma_start(x1s[:].rearrange("p (t d) -> p t d", d=1024),
                                  d_x1[s8 * 512:(s8 + 1) * 512, :]
                                  .rearrange("(t p) d -> p t d", p=128)),
                      waits=[("dadd", 32 * s8)], inc="dxl", incn=16)
                for c in range(8):
                    idx2 = 8 * s8 + c
                    S.add("gpsimd", lambda e, c=c: e.dma_start(wdC[c % 2][:], d_wdT[c]),
                          waits=[("ddn", idx2 - 1)], inc="dwd%d" % (c % 2), incn=16)
                    for k in range(24):
                        w = []
                        if k == 0:
                            w = [("dwd%d" % (c % 2), 16 * (idx2 // 2 + 1)),
                                 ("dmul", 24 * (s8 + 1)),
                                 ("dev", idx2 - 1)]
                        S.add("tensor",
                              lambda e, c=c, k=k:
                              e.matmul(pD[c % 2][:], wdC[c % 2][:, k * 128:(k + 1) * 128],
                                       hseg[k][:], start=(k == 0), stop=(k == 23)),
                              waits=w, inc=("ddn" if k == 23 else None))
                    S.add("scalar",
                          lambda e, c=c: e.copy(rb2[c % 2][:], pD[c % 2][:]),
                          waits=[("ddn", S.n("ddn")), ("dtp", 4 * (idx2 - 1))],
                          inc="dev")
                    for tb in range(4):
                        S.add("tensor",
                              lambda e, c=c, tb=tb:
                              e.matmul(pT2[tb % 2][:],
                                       rb2[c % 2][:, tb * 128:(tb + 1) * 128],
                                       ident[:], start=True, stop=True),
                              waits=[("dev", S.n("dev")), ("dadd", S.n("dadd") - 1)],
                              inc="dtp")
                        S.add("vector",
                              lambda e, c=c, tb=tb:
                              e.tensor_add(yst[tb][:, c * 128:(c + 1) * 128],
                                           pT2[tb % 2][:],
                                           x1s[:, 1024 * tb + 128 * c:
                                               1024 * tb + 128 * c + 128]),
                              waits=[("dtp", S.n("dtp")), ("dxl", 16 * (s8 + 1)),
                                     ("dy", 64 * s8)],
                              inc="dadd")
                for tb in range(4):
                    def mk(s8=s8, tb=tb):
                        def emit():
                            S.add("sync",
                                  lambda e, s8=s8, tb=tb:
                                  e.dma_start(y_out[s8 * 512 + tb * 128:
                                                    s8 * 512 + (tb + 1) * 128, :],
                                              yst[tb][:]),
                                  waits=[("dadd", 32 * (s8 + 1))],
                                  inc="dy", incn=16)
                        return emit
                    y_pend.append(mk())
            for fn in y_pend:
                fn()
            S.add("sync", lambda e: e.nop(), waits=[("dy", 64 * 8)])
            S.replay(nc, sems)
        barrier()

    return nc, tap_aps


def kernel(**inputs):
    x = np.asarray(inputs["L0_attn_x"], dtype=np.float32)
    Wq_ = np.asarray(inputs["L0_attn_Wq"], dtype=np.float32)
    Wk_ = np.asarray(inputs["L0_attn_Wk"], dtype=np.float32)
    Wv_ = np.asarray(inputs["L0_attn_Wv"], dtype=np.float32)
    Wfc_ = np.asarray(inputs["L0_attn_Wfc"], dtype=np.float32)
    wg_ = np.asarray(inputs["L0_ffn_w_gate"], dtype=np.int32)
    wu_ = np.asarray(inputs["L0_ffn_w_up"], dtype=np.int32)
    wd_ = np.asarray(inputs["L0_ffn_w_down"], dtype=np.int32)

    nc, _ = build_program()
    in_maps = []
    for c in range(8):
        b, h = c // 2, c % 2
        xc = np.ascontiguousarray(x[b, h * 4096:(h + 1) * 4096])
        in_maps.append({"x": xc, "Wq": Wq_, "Wk": Wk_, "Wv": Wv_, "Wfc": Wfc_,
                        "wg": wg_, "wu": wu_, "wd": wd_})
    res = run_bass_kernel_spmd(nc, in_maps, core_ids=list(range(8)))
    y = np.empty((4, 8192, 1024), dtype=np.float32)
    for c in range(8):
        b, h = c // 2, c % 2
        y[b, h * 4096:(h + 1) * 4096] = res.results[c]["y"]
    return y


# version 23
# speedup vs baseline: 1.0114x; 1.0114x over previous
import math
import numpy as np
import concourse.bass as bass
import concourse.mybir as mybir
from concourse.bass_utils import run_bass_kernel_spmd
from contextlib import ExitStack

F32 = mybir.dt.float32
BF16 = mybir.dt.bfloat16
I32 = mybir.dt.int32
AF = mybir.ActivationFunctionType
ALU = mybir.AluOpType
AX = mybir.AxisListType

DM = 1024
S_OWN = 4096
QK_SCALE = 0.125
DEQ = float(1.0 / 127.0)
EPS = 1e-5
ENGINES = ("sync", "gpsimd", "tensor", "scalar", "vector")
NENG = len(ENGINES)


class Sched:

    def __init__(self, cnt, hwm):
        self.cnt = cnt
        self.hwm = hwm
        self.ops = {e: [] for e in ENGINES}

    def n(self, sem):
        return self.cnt.get(sem, 0)

    def add(self, eng, fn, waits=(), inc=None, incn=1, chain=False, deps=()):
        w = [(s, v) for (s, v) in waits if v > 0]
        dl = list(deps)
        if chain and self.ops[eng]:
            dl.append((eng, len(self.ops[eng]) - 1))
        cnt_after = None
        if inc is not None:
            self.cnt[inc] = self.cnt.get(inc, 0) + incn
            cnt_after = self.cnt[inc]
        op = {"fn": fn, "waits": w, "deps": [], "inc": inc, "incn": incn,
              "cnt_after": cnt_after, "selfneed": False}
        for (teng, tidx) in dl:
            top = self.ops[teng][tidx]
            if top["inc"] is None:
                top["selfneed"] = True
            op["deps"].append((teng, tidx))
        self.ops[eng].append(op)
        return (eng, len(self.ops[eng]) - 1)

    def replay(self, nc, sems):
        hwm = self.hwm
        selfcount = {}
        for eng in ENGINES:
            sem = "sf" + eng
            n = self.cnt.get(sem, 0)
            for idx, op in enumerate(self.ops[eng]):
                if op["inc"] is not None:
                    selfcount[(eng, idx)] = (op["inc"], op["cnt_after"])
                elif op["selfneed"]:
                    n += 1
                    selfcount[(eng, idx)] = (sem, n)
            self.cnt[sem] = n
        with nc.Block() as blk:
            for eng_name in ENGINES:
                ops = self.ops[eng_name]

                def make(ops=ops, eng_name=eng_name):
                    def f(e):
                        for op in ops:
                            waits = list(op["waits"])
                            for tok in op["deps"]:
                                waits.append(selfcount[tok])
                            for (s, v) in waits:
                                key = (eng_name, s)
                                if v > 0 and hwm.get(key, 0) < v:
                                    e.wait_ge(sems[s], v)
                                    hwm[key] = v
                            ins = op["fn"](e)
                            if op["inc"] is not None:
                                ins.then_inc(sems[op["inc"]], op["incn"])
                            elif op["selfneed"]:
                                ins.then_inc(sems["sf" + eng_name], 1)
                    return f

                getattr(blk, eng_name)(make())


def build_program(taps=(), stop_after=None):
    _ORDER = ["ATT", "FFN"]

    def _do(phname):
        return stop_after is None or _ORDER.index(phname) <= _ORDER.index(stop_after)

    nc = bass.Bass("TRN2", target_bir_lowering=False)

    x_in = nc.dram_tensor("x", [S_OWN, DM], F32, kind="ExternalInput").ap()
    Wq = nc.dram_tensor("Wq", [DM, DM], F32, kind="ExternalInput").ap()
    Wk = nc.dram_tensor("Wk", [DM, DM], F32, kind="ExternalInput").ap()
    Wv = nc.dram_tensor("Wv", [DM, DM], F32, kind="ExternalInput").ap()
    Wfc = nc.dram_tensor("Wfc", [DM, DM], F32, kind="ExternalInput").ap()
    wg_in = nc.dram_tensor("wg", [3072, DM], I32, kind="ExternalInput").ap()
    wu_in = nc.dram_tensor("wu", [3072, DM], I32, kind="ExternalInput").ap()
    wd_in = nc.dram_tensor("wd", [DM, 3072], I32, kind="ExternalInput").ap()
    y_out = nc.dram_tensor("y", [S_OWN, DM], F32, kind="ExternalOutput").ap()

    d_WqT = nc.dram_tensor("d_WqT", [8, 128, 1024], BF16).ap()
    d_WkT = nc.dram_tensor("d_WkT", [8, 128, 1024], BF16).ap()
    d_WvT = nc.dram_tensor("d_WvT", [8, 128, 1024], BF16).ap()
    d_WfcT = nc.dram_tensor("d_WfcT", [8, 128, 1024], BF16).ap()
    d_wgT = nc.dram_tensor("d_wgT", [24, 128, 1024], BF16).ap()
    d_wuT = nc.dram_tensor("d_wuT", [24, 128, 1024], BF16).ap()
    d_wdT = nc.dram_tensor("d_wdT", [8, 128, 3072], BF16).ap()
    d_x1 = nc.dram_tensor("d_x1", [S_OWN, DM], F32).ap()
    d_xn2T = nc.dram_tensor("d_xn2T", [8, 128, S_OWN], BF16).ap()

    tap_aps = {}
    for name, shape, dt in [
        ("QKV", [3, 4, 128, 1024], BF16),
        ("A", [4, 128, 256], BF16),
        ("O", [4, 128, 1024], F32),
        ("x1", [S_OWN, DM], F32),
        ("xn2T", [8, 128, S_OWN], BF16),
    ]:
        if name in taps:
            tap_aps[name] = nc.dram_tensor("tap_" + name, shape, dt,
                                           kind="ExternalOutput").ap()

    cnt = {}
    hwm = {}
    bar_k = [0]

    with ExitStack() as TOP:
        E = TOP.enter_context
        sem_names = [
            "ph", "ini",
            "sfsync", "sfgpsimd", "sftensor", "sfscalar", "sfvector",
            "ail0", "ail1", "ail2", "ail3", "afl0", "afl1", "afl2", "afl3",
            "a0c", "a0t", "a0p", "as0", "as1", "as2", "as3",
            "aiu0", "aiu1", "aiu2", "aiu3",
            "wl", "axl0", "axl1", "axl2", "axl3", "aln", "alna", "alnx", "alnr", "axt", "axe",
            "apj", "apev", "asx", "asb", "aex", "amx", "agp", "apl", "aav", "aot", "aote",
            "arp", "arev", "aadd", "ax2", "ax2t", "ax2e", "axs", "ax2s",
            "axs0", "axs1", "ax2s0", "ax2s1",
            "dwg0", "dwg1", "dwu0", "dwu1", "dg", "dsil", "dmul", "dwd0", "dwd1", "ddn",
            "dev", "dtp", "dadd", "dxl", "dx2l0", "dx2l1", "dy", "tapd",
        ]
        sems = {s: E(nc.semaphore(s)) for s in sem_names}

        def barrier():
            bar_k[0] += 1
            k = bar_k[0]
            with nc.Block() as blk:
                for en in ENGINES:
                    def f(e, k=k):
                        e.nop().then_inc(sems["ph"], 1)
                        e.wait_ge(sems["ph"], NENG * k)
                    getattr(blk, en)(f)

        def tap_block(items):
            todo = [(nm, fn) for (nm, fn) in items if nm in taps]
            if not todo:
                return
            with nc.Block() as blk:
                @blk.sync
                def _(sy):
                    t0 = cnt.get("tapd", 0)
                    add = [0]
                    def issue(dst, src):
                        sy.dma_start(dst, src).then_inc(sems["tapd"], 16)
                        add[0] += 16
                    for (_, fn) in todo:
                        fn(issue)
                    sy.wait_ge(sems["tapd"], t0 + add[0])
                    cnt["tapd"] = t0 + add[0]
            barrier()

        ident = E(nc.sbuf_tensor("ident", [128, 128], BF16))
        WqJ = [E(nc.sbuf_tensor(f"WqJ{j}", [128, 1024], BF16)) for j in range(8)]
        WkJ = [E(nc.sbuf_tensor(f"WkJ{j}", [128, 1024], BF16)) for j in range(8)]
        WvJ = [E(nc.sbuf_tensor(f"WvJ{j}", [128, 1024], BF16)) for j in range(8)]
        WfJ = [E(nc.sbuf_tensor(f"WfJ{j}", [128, 1024], BF16)) for j in range(8)]

        c_eps = E(nc.sbuf_tensor("c_eps", [128, 1], F32))

        with nc.Block() as blk, \
             nc.sbuf_tensor("io1", [128, 128], F32) as io1, \
             nc.sbuf_tensor("io2", [128, 128], F32) as io2:

            @blk.gpsimd
            def _(g):
                g.iota(io1[:], [[1, 128]], channel_multiplier=0,
                       allow_small_or_imprecise_dtypes=True)
                g.iota(io2[:], [[0, 128]], channel_multiplier=1,
                       allow_small_or_imprecise_dtypes=True)

                g.memset(c_eps[:], EPS).then_inc(sems["ini"], 1)

            @blk.vector
            def _(v):
                v.wait_ge(sems["ini"], 1)
                v.tensor_tensor(ident[:], io1[:], io2[:], ALU.is_equal)
        barrier()

        a0_jobs = []
        for (src, dst, nch, ncol, sm, is_int, ldq) in [
            (Wq, WqJ, 8, 1024, "qk", False, "gpsimd"),
            (Wk, WkJ, 8, 1024, "no", False, "gpsimd"),
            (Wv, WvJ, 8, 1024, "no", False, "gpsimd"),
            (Wfc, WfJ, 8, 1024, "no", False, "gpsimd"),
            (wg_in, d_wgT, 24, 1024, "no", True, "sync"),
            (wu_in, d_wuT, 24, 1024, "no", True, "gpsimd"),
            (wd_in, d_wdT, 8, 3072, "no", True, "sync"),
        ]:
            for c in range(nch):
                a0_jobs.append((src, dst, c, ncol, sm, is_int, ldq))
        NJ = len(a0_jobs)
        nb_of = [ncol // 128 for (_, _, _, ncol, _, _, _) in a0_jobs]
        nb_pre = [sum(nb_of[:j]) for j in range(NJ + 1)]

        with ExitStack() as ph:
            e = ph.enter_context
            wstg_i = [e(nc.sbuf_tensor(f"wstg_i{i}", [128, 3072], I32)) for i in range(4)]
            wstg_b = [e(nc.sbuf_tensor(f"wstg_b{i}", [128, 3072], BF16)) for i in range(4)]
            wpk = [e(nc.sbuf_tensor(f"wpk{i}", [128, 3072], BF16)) for i in range(4)]
            wps = [e(nc.psum_tensor(f"wps{i}", [128, 512], F32)) for i in range(2)]

            S = Sched(cnt, hwm)
            n_ld = {"ail": [0, 0, 0, 0], "aiu": [0, 0, 0, 0]}
            n_fl = [0, 0, 0, 0]
            n_sp = [0, 0, 0, 0]
            for j, (src, dst, c, ncol, sm, is_int, ldq) in enumerate(a0_jobs):
                nb = nb_of[j]
                sl = j % 4
                stg_free = nb_pre[j - 3] if j >= 4 else 0
                if is_int:
                    lsem = "ail" if ldq == "sync" else "aiu"
                    n_ld[lsem][sl] += 1
                    S.add(ldq,
                          lambda e, j=j, src=src, c=c, ncol=ncol:
                          e.dma_start(wstg_i[j % 4][:, 0:ncol],
                                      src[c * 128:(c + 1) * 128, :]),
                          waits=[("a0c", S.n("a0c") - 3)],
                          inc="%s%d" % (lsem, sl), incn=16)
                    S.add("scalar",
                          lambda e, j=j, ncol=ncol:
                          e.activation(wstg_b[j % 4][:, 0:ncol], wstg_i[j % 4][:, 0:ncol],
                                       AF.Copy, bias=0.0, scale=DEQ),
                          waits=[("%s%d" % (lsem, sl), 16 * n_ld[lsem][sl]),
                                 ("a0t", stg_free)],
                          inc="a0c")
                else:
                    n_fl[sl] += 1
                    S.add("gpsimd",
                          lambda e, j=j, src=src, c=c, ncol=ncol:
                          e.dma_start(wstg_b[j % 4][:, 0:ncol],
                                      src[c * 128:(c + 1) * 128, :]),
                          waits=[("a0t", stg_free)],
                          inc="afl%d" % sl, incn=16)
                cast_seen = S.n("a0c")
                for b in range(nb):
                    w = [("a0p", nb_pre[j] + b - 1)]
                    if b == 0:
                        w.append(("a0c", cast_seen) if is_int
                                 else ("afl%d" % sl, 16 * n_fl[sl]))
                    S.add("tensor",
                          lambda e, j=j, b=b:
                          e.matmul(wps[b % 2][:, 0:128],
                                   wstg_b[j % 4][:, b * 128:(b + 1) * 128],
                                   ident[:], start=True, stop=True),
                          waits=w, inc="a0t")
                    pw = [("a0t", S.n("a0t"))]
                    direct = isinstance(dst, list)
                    if b == 0 and not direct:
                        pw.append(("as%d" % sl, 16 * n_sp[sl]))
                    if direct:
                        pdst = (lambda e, dst=dst, b=b, c=c:
                                dst[b][:, c * 128:(c + 1) * 128])
                    else:
                        pdst = (lambda e, j=j, b=b: wpk[j % 4][:, b * 128:(b + 1) * 128])
                    if sm == "qk":
                        S.add("vector",
                              lambda e, pdst=pdst, b=b:
                              e.tensor_scalar(pdst(e), wps[b % 2][:, 0:128],
                                              QK_SCALE, None, ALU.mult),
                              waits=pw, inc="a0p")
                    else:
                        S.add("vector",
                              lambda e, pdst=pdst, b=b:
                              e.tensor_copy(pdst(e), wps[b % 2][:, 0:128]),
                              waits=pw, inc="a0p")
                if not isinstance(dst, list):
                    n_sp[sl] += 1
                    S.add("scalar",
                          lambda e, j=j, dst=dst, c=c, ncol=ncol:
                          e.dma_start(dst[c], wpk[j % 4][:, 0:ncol]),
                          waits=[("a0p", S.n("a0p"))], inc="as%d" % sl, incn=16)
            S.add("sync", lambda e: e.nop(),
                  waits=[("as%d" % i, 16 * n_sp[i]) for i in range(4)])
            S.replay(nc, sems)
        barrier()

        if not _do("ATT"):
            return nc, tap_aps
        with ExitStack() as ph:
            e = ph.enter_context
            xcT = [e(nc.sbuf_tensor(f"xcT{i}", [128, 1024], F32)) for i in range(8)]
            xnm = e(nc.sbuf_tensor("xnm", [128, DM], BF16))
            xnT = [e(nc.sbuf_tensor(f"xnT{j}", [128, 512], BF16)) for j in range(8)]
            Lr = [e(nc.sbuf_tensor(f"Lr{i}", [128, 8], F32)) for i in range(8)]
            Lr2 = [e(nc.sbuf_tensor(f"Lr2_{i}", [128, 8], F32)) for i in range(4)]
            sqd = e(nc.sbuf_tensor("sqd", [128, DM], BF16))
            Qs = [e(nc.sbuf_tensor(f"Qs{t}", [128, 1024], BF16)) for t in range(4)]
            Ks = [e(nc.sbuf_tensor(f"Ks{t}", [128, 1024], BF16)) for t in range(4)]
            Vs = [e(nc.sbuf_tensor(f"Vs{t}", [128, 1024], BF16)) for t in range(8)]
            Ss = e(nc.sbuf_tensor("Ss", [128, 256], F32))
            Sm = e(nc.sbuf_tensor("Sm", [128, 16], F32))
            Se = [e(nc.sbuf_tensor(f"Se{i}", [128, 256], BF16)) for i in range(4)]
            Sl = e(nc.sbuf_tensor("Sl", [128, 16], F32))
            Sr = e(nc.sbuf_tensor("Sr", [128, 16], F32))
            Snr = Sm
            Aa = [e(nc.sbuf_tensor(f"Aa{i}", [128, 256], BF16)) for i in range(4)]
            gramB = [e(nc.sbuf_tensor(f"gramB{i}", [128, 4096], BF16))
                     for i in range(2)]
            avB = [e(nc.sbuf_tensor(f"avB{i}", [128, 4096], BF16)) for i in range(2)]
            gh1 = e(nc.sbuf_tensor("gh1", [128, 2048], BF16))
            OT = [e(nc.sbuf_tensor(f"OT{j}", [128, 512], BF16)) for j in range(8)]
            x1p = [e(nc.sbuf_tensor(f"x1p{i}", [128, 1024], F32)) for i in range(2)]
            xn2m = e(nc.sbuf_tensor("xn2m", [128, DM], BF16))
            xn2s = [e(nc.sbuf_tensor(f"xn2s{j}", [128, 512], BF16)) for j in range(8)]
            pA = [e(nc.psum_tensor(f"pA{i}", [128, 512], F32)) for i in range(2)]
            pB = [e(nc.psum_tensor(f"pB{i}", [128, 512], F32)) for i in range(2)]
            pT = [e(nc.psum_tensor(f"pT{i}", [128, 128], F32)) for i in range(2)]
            pOT = e(nc.psum_tensor("pOT", [128, 1024], F32))

            S = Sched(cnt, hwm)
            cap = {}
            pTw = [None, None]
            pQw = {}

            def ln_acc(Sch, xsrc_fn, Lrt, extra_waits=()):
                Sch.add("scalar",
                        lambda e: e.activation(sqd[:], xsrc_fn(), AF.Copy,
                                               bias=0.0, scale=1.0,
                                               accum_out=Lrt[:, 0:1]),
                        waits=[("alna", Sch.n("alna"))] + list(extra_waits),
                        inc="alna")
                Sch.add("scalar",
                        lambda e: e.activation(sqd[:], xsrc_fn(), AF.Square,
                                               bias=0.0, scale=1.0,
                                               accum_out=Lrt[:, 1:2]),
                        waits=[("alna", Sch.n("alna"))] + list(extra_waits),
                        inc="alna")
                return Sch.n("alna")

            def ln_mid(Sch, Lrt, acc_n):
                Sch.add("vector", lambda e: e.tensor_scalar(
                    Lrt[:, 2:3], Lrt[:, 0:1], -1.0 / 1024.0, None, ALU.mult),
                        waits=[("alna", acc_n)], chain=True)
                Sch.add("vector", lambda e: e.tensor_scalar(
                    Lrt[:, 3:4], Lrt[:, 1:2], 1.0 / 1024.0, None, ALU.mult),
                        chain=True)
                Sch.add("vector", lambda e: e.tensor_mul(
                    Lrt[:, 4:5], Lrt[:, 2:3], Lrt[:, 2:3]), chain=True)
                Sch.add("vector", lambda e: e.tensor_sub(
                    Lrt[:, 5:6], Lrt[:, 3:4], Lrt[:, 4:5]), chain=True, inc="aln")
                Sch.add("scalar",
                        lambda e: e.activation(Lrt[:, 6:7], Lrt[:, 5:6], AF.Ln,
                                               bias=c_eps[:, 0:1], scale=1.0),
                        waits=[("aln", Sch.n("aln")), ("alnx", Sch.n("alnx"))],
                        inc="alnx")
                Sch.add("scalar",
                        lambda e: e.activation(Lrt[:, 7:8], Lrt[:, 6:7], AF.Exp,
                                               bias=0.0, scale=-0.5),
                        waits=[("alnx", Sch.n("alnx"))], inc="alnx")
                return Sch.n("alnx")

            def ln_app(Sch, xsrc_fn, Lrt, out_fn, mid_n, extra_waits=(),
                       out_inc=None):
                Sch.add("vector",
                        lambda e: e.tensor_scalar(out_fn(), xsrc_fn(),
                                                  Lrt[:, 2:3], Lrt[:, 7:8],
                                                  ALU.add, ALU.mult),
                        waits=[("alnx", mid_n)] + list(extra_waits),
                        chain=True, inc=out_inc)

            def load_tile(k):
                w = [("axl%d" % (k % 4), 16 * (k // 4))]
                if k - 8 >= 0:
                    w.append(("arev", 2 * (k - 8 + 1)))
                S.add("sync",
                      lambda e, k=k:
                      e.dma_start(xcT[k % 8][:], x_in[k * 128:(k + 1) * 128, :]),
                      waits=w, inc="axl%d" % (k % 4), incn=16)

            def ln1_a(k):
                w = [("axl%d" % (k % 4), 16 * (k // 4 + 1))]
                if k - 8 >= 0:
                    w.append(("alnr", k - 7))
                cap["l1a", k] = ln_acc(S, (lambda k=k: xcT[k % 8][:]), Lr[k % 8],
                                       extra_waits=w)

            def ln1_m(k):
                cap["l1m", k] = ln_mid(S, Lr[k % 8], cap["l1a", k])

            def ln1_c(k):
                ln_app(S,
                       (lambda k=k: xcT[k % 8][:]),
                       Lr[k % 8],
                       (lambda: xnm[:]),
                       cap["l1m", k],
                       extra_waits=[("axt", 8 * k)],
                       out_inc="alnr")

            def lnt(k):
                i = k % 4
                for b in range(8):
                    w = [("alnr", k + 1)] if b == 0 else []
                    if pTw[b % 2] is not None:
                        w.append(pTw[b % 2])
                    S.add("tensor",
                          lambda e, b=b:
                          e.matmul(pT[b % 2][:], xnm[:, b * 128:(b + 1) * 128],
                                   ident[:], start=True, stop=True),
                          waits=w, inc="axt")
                    ev_w = [("axt", S.n("axt"))]
                    if ("apj", k - 4) in cap:
                        ev_w.append(("apj", cap["apj", k - 4]))
                    S.add("scalar",
                          lambda e, b=b, i=i:
                          e.copy(xnT[b][:, i * 128:(i + 1) * 128], pT[b % 2][:]),
                          waits=ev_w, inc="axe")
                    pTw[b % 2] = ("axe", S.n("axe"))

            def proj(k):
                tt = k % 4
                for (Wjs, Dst, kind) in ((WqJ, Qs[tt], "qk"), (WkJ, Ks[tt], "qk"),
                                         (WvJ, Vs[k % 8], "v")):
                    for half in range(2):
                        pb = (pA if half == 0 else pB)[tt % 2]
                        key = (half, tt % 2)
                        for j in range(8):
                            w = []
                            if j == 0:
                                w = [("axe", 8 * (k + 1))]
                                if pQw.get(key):
                                    w.append(pQw[key])
                            S.add("tensor",
                                  lambda e, Wjs=Wjs, tt=tt, half=half, j=j, pb=pb:
                                  e.matmul(pb[:],
                                           xnT[j][:, tt * 128:(tt + 1) * 128],
                                           Wjs[j][:, half * 512:(half + 1) * 512],
                                           start=(j == 0), stop=(j == 7)),
                                  waits=w, inc=("apj" if j == 7 else None))
                        ev_w = [("apj", S.n("apj"))]
                        if kind == "qk":
                            if ("gram", k - 4) in cap:
                                ev_w.append(("agp", cap["gram", k - 4]))
                        else:
                            if ("pool", k - 8) in cap:
                                ev_w.append(("apl", cap["pool", k - 8]))
                        S.add("scalar",
                              lambda e, Dst=Dst, half=half, pb=pb:
                              e.copy(Dst[:, half * 512:(half + 1) * 512], pb[:]),
                              waits=ev_w, inc="apev")
                        pQw[key] = ("apev", S.n("apev"))
                cap["apev", k] = S.n("apev")
                cap["apj", k] = S.n("apj")

            def front(k):
                tt = k % 4
                e4 = k % 4
                for g in range(4):
                    S.add("vector",
                          lambda e, tt=tt, g=g:
                          e.tensor_tensor(
                              gramB[g % 2][:].rearrange("p (t h d) -> p t h d",
                                                        t=4, d=64),
                              bass.AP(Qs[tt][:].tensor, Qs[tt][:].offset,
                                      [[1024, 128], [0, 4], [64, 16], [1, 64]]),
                              bass.AP(Ks[tt][:].tensor, Ks[tt][:].offset + 256 * g,
                                      [[1024, 128], [64, 4], [0, 16], [1, 64]]),
                              ALU.mult),
                          waits=([("apev", cap["apev", k])] if g == 0 else []),
                          chain=True, inc=None)
                    S.add("vector",
                          lambda e, g=g:
                          e.tensor_tensor(
                              gh1[:].rearrange("p (a b) -> p a b", b=32),
                              bass.AP(gramB[g % 2][:].tensor,
                                      gramB[g % 2][:].offset,
                                      [[4096, 128], [64, 64], [1, 32]]),
                              bass.AP(gramB[g % 2][:].tensor,
                                      gramB[g % 2][:].offset + 32,
                                      [[4096, 128], [64, 64], [1, 32]]),
                              ALU.add),
                          chain=True)
                    S.add("vector",
                          lambda e, g=g:
                          e.tensor_reduce(
                              Ss[:, 64 * g:64 * (g + 1)]
                              .rearrange("p (a b) -> p a b", b=1),
                              gh1[:].rearrange("p (a b) -> p a b", b=32),
                              AX.X, ALU.add),
                          chain=True, inc="agp")
                cap["gram", k] = S.n("agp")
                S.add("vector",
                      lambda e:
                      e.reduce_max(Sm[:, 0:16].rearrange("p (a b) -> p a b", b=1),
                                   bass.AP(Ss[:].tensor, Ss[:].offset,
                                           [[256, 128], [1, 16], [16, 16]]),
                                   AX.X),
                      chain=True)
                sub_w = [("amx", k - 3)] if k - 4 >= 0 else []
                S.add("vector",
                      lambda e, e4=e4:
                      e.tensor_tensor(Se[e4][:].rearrange("p (t h) -> p t h", h=16),
                                      Ss[:].rearrange("p (t h) -> p t h", h=16),
                                      bass.AP(Sm[:].tensor, Sm[:].offset,
                                              [[16, 128], [0, 16], [1, 16]]),
                                      ALU.subtract),
                      waits=sub_w, chain=True, inc="asb")
                S.add("scalar",
                      lambda e, e4=e4: e.activation(Se[e4][:], Se[e4][:], AF.Exp,
                                                    bias=0.0, scale=1.0),
                      waits=[("asb", k + 1)], inc="aex")

            def back_a(k):
                s, tt = divmod(k, 4)
                e4 = k % 4
                S.add("vector",
                      lambda e, e4=e4:
                      e.reduce_sum(Sl[:, 0:16].rearrange("p (a b) -> p a b", b=1),
                                   bass.AP(Se[e4][:].tensor, Se[e4][:].offset,
                                           [[256, 128], [1, 16], [16, 16]]),
                                   AX.X),
                      waits=[("aex", k + 1)])
                S.add("vector", lambda e: e.reciprocal(Sr[:], Sl[:]), chain=True)
                S.add("vector", lambda e: e.tensor_mul(Snr[:], Sl[:], Sr[:]),
                      chain=True)
                S.add("vector",
                      lambda e: e.tensor_scalar(Snr[:], Snr[:], -1.0, 2.0,
                                                ALU.mult, ALU.add),
                      chain=True)
                S.add("vector", lambda e: e.tensor_mul(Sr[:], Sr[:], Snr[:]),
                      chain=True)
                am_w = []
                if ("pool", k - 4) in cap:
                    am_w.append(("apl", cap["pool", k - 4]))
                S.add("vector",
                      lambda e, e4=e4:
                      e.tensor_tensor(Aa[e4][:].rearrange("p (t h) -> p t h", h=16),
                                      Se[e4][:].rearrange("p (t h) -> p t h", h=16),
                                      bass.AP(Sr[:].tensor, Sr[:].offset,
                                              [[16, 128], [0, 16], [1, 16]]),
                                      ALU.mult),
                      waits=am_w, chain=True, inc="amx")
                for g in range(4):
                    S.add("gpsimd",
                          lambda e, e4=e4, g=g, k=k:
                          e.tensor_tensor(
                              avB[g % 2][:].rearrange("p (t h d) -> p t h d",
                                                      t=4, d=64),
                              bass.AP(Aa[e4][:].tensor, Aa[e4][:].offset + 64 * g,
                                      [[256, 128], [16, 4], [1, 16], [0, 64]]),
                              bass.AP(Vs[k % 8][:].tensor,
                                      Vs[k % 8][:].offset + 256 * g,
                                      [[1024, 128], [64, 4], [0, 16], [1, 64]]),
                              ALU.mult),
                          waits=[("amx", k + 1), ("aot", 4 * k + g - 1)],
                          inc="apl")
                cap["pool", k] = S.n("apl")

            def back_a2(k, midfn=None):
                tt = k % 4
                for g in range(4):
                    if g == 2 and midfn is not None:
                        midfn()
                    for i in range(4):
                        for b in range(8):
                            w = []
                            if i == 0 and b == 0:
                                w.append(("apl", 4 * k + g + 1))
                                if g == 0:
                                    w.append(("aote", 8 * k))
                            S.add("tensor",
                                  lambda e, g=g, i=i, b=b:
                                  e.matmul(pOT[:, 128 * b:128 * (b + 1)],
                                           avB[g % 2][:, 1024 * i + 128 * b:
                                                       1024 * i + 128 * (b + 1)],
                                           ident[:],
                                           start=(g == 0 and i == 0
                                                  and b % 4 == 0),
                                           stop=(g == 3 and i == 3
                                                 and b % 4 == 3)),
                                  waits=w,
                                  inc=("aot" if (i == 3 and b == 7) else None))
                ev_w0 = [("aot", 4 * (k + 1))]
                if k - 4 >= 0:
                    ev_w0.append(("arp", 2 * (k - 4 + 1)))
                for b in range(8):
                    S.add("scalar",
                          lambda e, b=b, tt=tt:
                          e.copy(OT[b][:, tt * 128:(tt + 1) * 128],
                                 pOT[:, 128 * b:128 * (b + 1)]),
                          waits=(ev_w0 if b == 0 else []), inc="aote")

            def back_b(k):
                s, tt = divmod(k, 4)
                e4 = k % 4
                for half in range(2):
                    pb = (pA if half == 0 else pB)[tt % 2]
                    key = (half, tt % 2)
                    for j in range(8):
                        w = []
                        if j == 0:
                            w = [("aote", 8 * (k + 1))]
                            if pQw.get(key):
                                w.append(pQw[key])
                        S.add("tensor",
                              lambda e, tt=tt, half=half, j=j, pb=pb:
                              e.matmul(pb[:],
                                       OT[j][:, tt * 128:(tt + 1) * 128],
                                       WfJ[j][:, half * 512:(half + 1) * 512],
                                       start=(j == 0), stop=(j == 7)),
                              waits=w, inc=("arp" if j == 7 else None))
                    r_w = [("arp", S.n("arp"))]
                    if k - 2 >= 0:
                        r_w.append(("axs%d" % (k % 2), 16 * (k // 2)))
                        if ("lnsq", k - 2) in cap:
                            r_w.append(("alna", cap["lnsq", k - 2]))
                    S.add("vector",
                          lambda e, half=half, k=k, pb=pb:
                          e.tensor_add(x1p[k % 2][:, 512 * half:512 * (half + 1)],
                                       pb[:],
                                       xcT[k % 8][:, 512 * half:512 * (half + 1)]),
                          waits=r_w, inc="arev")
                    pQw[key] = ("arev", S.n("arev"))
                S.add("sync",
                      lambda e, k=k:
                      e.dma_start(d_x1[k * 128:(k + 1) * 128, :], x1p[k % 2][:]),
                      waits=[("arev", 2 * (k + 1)),
                             ("axs%d" % (k % 2), 16 * (k // 2))],
                      inc="axs%d" % (k % 2), incn=16)
                w2 = [("arev", 2 * (k + 1))]
                if k - 4 >= 0:
                    w2.append(("ax2", k - 3))
                cap["l2a", k] = ln_acc(S, (lambda k=k: x1p[k % 2][:]), Lr2[e4],
                                       extra_waits=w2)
                cap["lnsq", k] = S.n("alna")

            def back_b2m(k):
                cap["l2m", k] = ln_mid(S, Lr2[k % 4], cap["l2a", k])

            def back_b2c(k):
                s, tt = divmod(k, 4)
                e4 = k % 4
                ln_app(S,
                       (lambda k=k: x1p[k % 2][:]),
                       Lr2[e4],
                       (lambda: xn2m[:]),
                       cap["l2m", k],
                       extra_waits=[("ax2t", 8 * k)],
                       out_inc="ax2")
                for b in range(8):
                    w = [("ax2", k + 1)] if b == 0 else []
                    if pTw[b % 2] is not None:
                        w.append(pTw[b % 2])
                    S.add("tensor",
                          lambda e, b=b:
                          e.matmul(pT[b % 2][:], xn2m[:, b * 128:(b + 1) * 128],
                                   ident[:], start=True, stop=True),
                          waits=w, inc="ax2t")
                    ev_w = [("ax2t", S.n("ax2t"))]
                    if s >= 1:
                        ev_w.append(("ax2s%d" % ((s - 1) % 2),
                                     128 * ((s - 1) // 2 + 1)))
                    S.add("scalar",
                          lambda e, b=b, tt=tt:
                          e.copy(xn2s[b][:, tt * 128:(tt + 1) * 128], pT[b % 2][:]),
                          waits=ev_w, inc="ax2e")
                    pTw[b % 2] = ("ax2e", S.n("ax2e"))
                if tt == 3:
                    for j in range(8):
                        w = [("ax2e", S.n("ax2e"))]
                        if j == 0:
                            w.append(("ax2s%d" % (s % 2), 128 * (s // 2)))
                        S.add("sync",
                              lambda e, j=j, s=s:
                              e.dma_start(d_xn2T[j][:, s * 512:(s + 1) * 512],
                                          xn2s[j][:]),
                              waits=w, inc="ax2s%d" % (s % 2), incn=16)

            def a1(k):
                if 0 <= k <= 31:
                    back_a(k)

            def a2(k, midfn=None):
                if 0 <= k <= 31:
                    back_a2(k, midfn)
                elif midfn is not None:
                    midfn()

            def op(k):
                if 0 <= k <= 31:
                    back_b(k)

            def op2m(k):
                if 0 <= k <= 31:
                    back_b2m(k)

            def op2c(k):
                if 0 <= k <= 31:
                    back_b2c(k)

            def g_load(k):
                if 5 <= k <= 31:
                    load_tile(k)

            def g_a(k):
                if 4 <= k <= 31:
                    ln1_a(k)

            def g_m(k):
                if 3 <= k <= 31:
                    ln1_m(k)

            def g_c(k):
                if 2 <= k <= 31:
                    ln1_c(k)
                    lnt(k)
                    proj(k)

            for k in range(5):
                load_tile(k)
            for k in range(4):
                ln1_a(k)
            for k in range(3):
                ln1_m(k)
            for k in range(2):
                ln1_c(k)
                lnt(k)
                proj(k)
            for T in range(36):
                if T <= 31:
                    front(T)
                a1(T - 1)
                op2c(T - 5)
                a2(T - 2, midfn=lambda T=T: op(T - 3))
                op2m(T - 4)
                g_load(T + 5)
                g_a(T + 4)
                g_m(T + 3)
                g_c(T + 2)
            op2c(31)
            S.add("sync", lambda e: e.nop(),
                  waits=[("axs0", 16 * 16), ("axs1", 16 * 16),
                         ("ax2s0", 128 * 4), ("ax2s1", 128 * 4)])
            S.replay(nc, sems)
        barrier()

        tap_block([
            ("x1", lambda issue: issue(tap_aps["x1"], d_x1)),
            ("xn2T", lambda issue: [issue(tap_aps["xn2T"][j], d_xn2T[j])
                                    for j in range(8)]),
        ])

        if not _do("FFN"):
            return nc, tap_aps
        with ExitStack() as ph:
            e4 = ph.enter_context
            wgS = [e4(nc.sbuf_tensor(f"wgS{i}", [128, 1024], BF16)) for i in range(2)]
            wuS = [e4(nc.sbuf_tensor(f"wuS{i}", [128, 1024], BF16)) for i in range(2)]
            wdC = [e4(nc.sbuf_tensor(f"wdC{i}", [128, 3072], BF16)) for i in range(2)]
            hseg = [e4(nc.sbuf_tensor(f"hseg{h}", [128, 512], BF16)) for h in range(24)]
            sg = [e4(nc.sbuf_tensor(f"sg{i}", [128, 512], BF16)) for i in range(2)]
            x2s = [e4(nc.sbuf_tensor(f"x2s{j}_0", [128, 512], BF16)) for j in range(8)]
            x2s2 = [e4(nc.sbuf_tensor(f"x2s{j}_1", [128, 512], BF16)) for j in range(8)]
            x1s = e4(nc.sbuf_tensor("x1s", [128, 4096], F32))
            yst = [e4(nc.sbuf_tensor(f"yst{tb}", [128, 1024], F32)) for tb in range(4)]
            rb2 = [e4(nc.sbuf_tensor(f"rb2{i}", [128, 512], BF16)) for i in range(2)]
            pG = [e4(nc.psum_tensor(f"pG{i}", [128, 512], F32)) for i in range(2)]
            pU = [e4(nc.psum_tensor(f"pU{i}", [128, 512], F32)) for i in range(2)]
            pD = [e4(nc.psum_tensor(f"pD{i}", [128, 512], F32)) for i in range(2)]
            pT2 = [e4(nc.psum_tensor(f"pT2{i}", [128, 128], F32)) for i in range(2)]
            x2buf = [x2s, x2s2]

            S = Sched(cnt, hwm)
            y_pend = []
            for s8 in range(8):
                xp = s8 % 2
                for j in range(8):
                    S.add("sync",
                          lambda e, j=j, s8=s8, xp=xp:
                          e.dma_start(x2buf[xp][j][:],
                                      d_xn2T[j][:, s8 * 512:(s8 + 1) * 512]),
                          waits=[("dg", 2 * 24 * (s8 - 1))],
                          inc="dx2l%d" % xp, incn=16)
                X2L = 16 * (S.n("dx2l%d" % xp) // 16)
                for h in range(24):
                    idx = 24 * s8 + h
                    S.add("sync", lambda e, h=h: e.dma_start(wgS[h % 2][:], d_wgT[h]),
                          waits=[("dg", 2 * (idx - 1))],
                          inc="dwg%d" % (h % 2), incn=16)
                    S.add("gpsimd", lambda e, h=h: e.dma_start(wuS[h % 2][:], d_wuT[h]),
                          waits=[("dg", 2 * (idx - 1))],
                          inc="dwu%d" % (h % 2), incn=16)
                    for j in range(8):
                        w = []
                        if j == 0:
                            w = [("dwg%d" % (h % 2), 16 * (idx // 2 + 1)),
                                 ("dsil", idx - 1),
                                 ("dx2l%d" % xp, X2L)]
                        S.add("tensor",
                              lambda e, h=h, j=j, xp=xp:
                              e.matmul(pG[h % 2][:], wgS[h % 2][:, j * 128:(j + 1) * 128],
                                       x2buf[xp][j][:],
                                       start=(j == 0), stop=(j == 7)),
                              waits=w, inc=("dg" if j == 7 else None))
                    S.add("scalar",
                          lambda e, h=h: e.activation(sg[h % 2][:], pG[h % 2][:],
                                                      AF.Silu, bias=0.0, scale=1.0),
                          waits=[("dg", S.n("dg")), ("dmul", idx - 1)], inc="dsil")
                    for j in range(8):
                        w = [("dmul", idx - 1)] if j == 0 else []
                        if j == 0:
                            w.append(("dwu%d" % (h % 2), 16 * (idx // 2 + 1)))
                        S.add("tensor",
                              lambda e, h=h, j=j, xp=xp:
                              e.matmul(pU[h % 2][:], wuS[h % 2][:, j * 128:(j + 1) * 128],
                                       x2buf[xp][j][:],
                                       start=(j == 0), stop=(j == 7)),
                              waits=w, inc=("dg" if j == 7 else None))
                    S.add("vector",
                          lambda e, h=h: e.tensor_mul(hseg[h][:], sg[h % 2][:],
                                                      pU[h % 2][:]),
                          waits=[("dsil", S.n("dsil")), ("dg", S.n("dg")),
                                 ("ddn", 8 * s8)],
                          inc="dmul")
                    if h == 1:
                        for fn in y_pend:
                            fn()
                        y_pend = []
                S.add("sync",
                      lambda e, s8=s8:
                      e.dma_start(x1s[:].rearrange("p (t d) -> p t d", d=1024),
                                  d_x1[s8 * 512:(s8 + 1) * 512, :]
                                  .rearrange("(t p) d -> p t d", p=128)),
                      waits=[("dadd", 32 * s8)], inc="dxl", incn=16)
                for c in range(8):
                    idx2 = 8 * s8 + c
                    S.add("gpsimd", lambda e, c=c: e.dma_start(wdC[c % 2][:], d_wdT[c]),
                          waits=[("ddn", idx2 - 1)], inc="dwd%d" % (c % 2), incn=16)
                    for k in range(24):
                        w = []
                        if k == 0:
                            w = [("dwd%d" % (c % 2), 16 * (idx2 // 2 + 1)),
                                 ("dmul", 24 * (s8 + 1)),
                                 ("dev", idx2 - 1)]
                        S.add("tensor",
                              lambda e, c=c, k=k:
                              e.matmul(pD[c % 2][:], wdC[c % 2][:, k * 128:(k + 1) * 128],
                                       hseg[k][:], start=(k == 0), stop=(k == 23)),
                              waits=w, inc=("ddn" if k == 23 else None))
                    S.add("scalar",
                          lambda e, c=c: e.copy(rb2[c % 2][:], pD[c % 2][:]),
                          waits=[("ddn", S.n("ddn")), ("dtp", 4 * (idx2 - 1))],
                          inc="dev")
                    for tb in range(4):
                        S.add("tensor",
                              lambda e, c=c, tb=tb:
                              e.matmul(pT2[tb % 2][:],
                                       rb2[c % 2][:, tb * 128:(tb + 1) * 128],
                                       ident[:], start=True, stop=True),
                              waits=[("dev", S.n("dev")), ("dadd", S.n("dadd") - 1)],
                              inc="dtp")
                        S.add("vector",
                              lambda e, c=c, tb=tb:
                              e.tensor_add(yst[tb][:, c * 128:(c + 1) * 128],
                                           pT2[tb % 2][:],
                                           x1s[:, 1024 * tb + 128 * c:
                                               1024 * tb + 128 * c + 128]),
                              waits=[("dtp", S.n("dtp")), ("dxl", 16 * (s8 + 1)),
                                     ("dy", 64 * s8)],
                              inc="dadd")
                for tb in range(4):
                    def mk(s8=s8, tb=tb):
                        def emit():
                            S.add("sync",
                                  lambda e, s8=s8, tb=tb:
                                  e.dma_start(y_out[s8 * 512 + tb * 128:
                                                    s8 * 512 + (tb + 1) * 128, :],
                                              yst[tb][:]),
                                  waits=[("dadd", 32 * (s8 + 1))],
                                  inc="dy", incn=16)
                        return emit
                    y_pend.append(mk())
            for fn in y_pend:
                fn()
            S.add("sync", lambda e: e.nop(), waits=[("dy", 64 * 8)])
            S.replay(nc, sems)
        barrier()

    return nc, tap_aps


def kernel(**inputs):
    x = np.asarray(inputs["L0_attn_x"], dtype=np.float32)
    Wq_ = np.asarray(inputs["L0_attn_Wq"], dtype=np.float32)
    Wk_ = np.asarray(inputs["L0_attn_Wk"], dtype=np.float32)
    Wv_ = np.asarray(inputs["L0_attn_Wv"], dtype=np.float32)
    Wfc_ = np.asarray(inputs["L0_attn_Wfc"], dtype=np.float32)
    wg_ = np.asarray(inputs["L0_ffn_w_gate"], dtype=np.int32)
    wu_ = np.asarray(inputs["L0_ffn_w_up"], dtype=np.int32)
    wd_ = np.asarray(inputs["L0_ffn_w_down"], dtype=np.int32)

    nc, _ = build_program()
    in_maps = []
    for c in range(8):
        b, h = c // 2, c % 2
        xc = np.ascontiguousarray(x[b, h * 4096:(h + 1) * 4096])
        in_maps.append({"x": xc, "Wq": Wq_, "Wk": Wk_, "Wv": Wv_, "Wfc": Wfc_,
                        "wg": wg_, "wu": wu_, "wd": wd_})
    res = run_bass_kernel_spmd(nc, in_maps, core_ids=list(range(8)))
    y = np.empty((4, 8192, 1024), dtype=np.float32)
    for c in range(8):
        b, h = c // 2, c % 2
        y[b, h * 4096:(h + 1) * 4096] = res.results[c]["y"]
    return y


# version 24
# speedup vs baseline: 1.0122x; 1.0009x over previous
import math
import numpy as np
import concourse.bass as bass
import concourse.mybir as mybir
from concourse.bass_utils import run_bass_kernel_spmd
from contextlib import ExitStack

F32 = mybir.dt.float32
BF16 = mybir.dt.bfloat16
I32 = mybir.dt.int32
AF = mybir.ActivationFunctionType
ALU = mybir.AluOpType
AX = mybir.AxisListType

DM = 1024
S_OWN = 4096
QK_SCALE = 0.125
DEQ = float(1.0 / 127.0)
EPS = 1e-5
ENGINES = ("sync", "gpsimd", "tensor", "scalar", "vector")
NENG = len(ENGINES)


class Sched:

    def __init__(self, cnt, hwm):
        self.cnt = cnt
        self.hwm = hwm
        self.ops = {e: [] for e in ENGINES}

    def n(self, sem):
        return self.cnt.get(sem, 0)

    def add(self, eng, fn, waits=(), inc=None, incn=1, chain=False, deps=()):
        w = [(s, v) for (s, v) in waits if v > 0]
        dl = list(deps)
        if chain and self.ops[eng]:
            dl.append((eng, len(self.ops[eng]) - 1))
        cnt_after = None
        if inc is not None:
            self.cnt[inc] = self.cnt.get(inc, 0) + incn
            cnt_after = self.cnt[inc]
        op = {"fn": fn, "waits": w, "deps": [], "inc": inc, "incn": incn,
              "cnt_after": cnt_after, "selfneed": False}
        for (teng, tidx) in dl:
            top = self.ops[teng][tidx]
            if top["inc"] is None:
                top["selfneed"] = True
            op["deps"].append((teng, tidx))
        self.ops[eng].append(op)
        return (eng, len(self.ops[eng]) - 1)

    def replay(self, nc, sems):
        hwm = self.hwm
        selfcount = {}
        for eng in ENGINES:
            sem = "sf" + eng
            n = self.cnt.get(sem, 0)
            for idx, op in enumerate(self.ops[eng]):
                if op["inc"] is not None:
                    selfcount[(eng, idx)] = (op["inc"], op["cnt_after"])
                elif op["selfneed"]:
                    n += 1
                    selfcount[(eng, idx)] = (sem, n)
            self.cnt[sem] = n
        with nc.Block() as blk:
            for eng_name in ENGINES:
                ops = self.ops[eng_name]

                def make(ops=ops, eng_name=eng_name):
                    def f(e):
                        for op in ops:
                            waits = list(op["waits"])
                            for tok in op["deps"]:
                                waits.append(selfcount[tok])
                            for (s, v) in waits:
                                key = (eng_name, s)
                                if v > 0 and hwm.get(key, 0) < v:
                                    e.wait_ge(sems[s], v)
                                    hwm[key] = v
                            ins = op["fn"](e)
                            if op["inc"] is not None:
                                ins.then_inc(sems[op["inc"]], op["incn"])
                            elif op["selfneed"]:
                                ins.then_inc(sems["sf" + eng_name], 1)
                    return f

                getattr(blk, eng_name)(make())


def build_program(taps=(), stop_after=None):
    _ORDER = ["ATT", "FFN"]

    def _do(phname):
        return stop_after is None or _ORDER.index(phname) <= _ORDER.index(stop_after)

    nc = bass.Bass("TRN2", target_bir_lowering=False)

    x_in = nc.dram_tensor("x", [S_OWN, DM], F32, kind="ExternalInput").ap()
    Wq = nc.dram_tensor("Wq", [DM, DM], F32, kind="ExternalInput").ap()
    Wk = nc.dram_tensor("Wk", [DM, DM], F32, kind="ExternalInput").ap()
    Wv = nc.dram_tensor("Wv", [DM, DM], F32, kind="ExternalInput").ap()
    Wfc = nc.dram_tensor("Wfc", [DM, DM], F32, kind="ExternalInput").ap()
    wg_in = nc.dram_tensor("wg", [3072, DM], I32, kind="ExternalInput").ap()
    wu_in = nc.dram_tensor("wu", [3072, DM], I32, kind="ExternalInput").ap()
    wd_in = nc.dram_tensor("wd", [DM, 3072], I32, kind="ExternalInput").ap()
    y_out = nc.dram_tensor("y", [S_OWN, DM], F32, kind="ExternalOutput").ap()

    d_WqT = nc.dram_tensor("d_WqT", [8, 128, 1024], BF16).ap()
    d_WkT = nc.dram_tensor("d_WkT", [8, 128, 1024], BF16).ap()
    d_WvT = nc.dram_tensor("d_WvT", [8, 128, 1024], BF16).ap()
    d_WfcT = nc.dram_tensor("d_WfcT", [8, 128, 1024], BF16).ap()
    d_wgT = nc.dram_tensor("d_wgT", [24, 128, 1024], BF16).ap()
    d_wuT = nc.dram_tensor("d_wuT", [24, 128, 1024], BF16).ap()
    d_wdT = nc.dram_tensor("d_wdT", [8, 128, 3072], BF16).ap()
    d_x1 = nc.dram_tensor("d_x1", [S_OWN, DM], F32).ap()
    d_xn2T = nc.dram_tensor("d_xn2T", [8, 128, S_OWN], BF16).ap()

    tap_aps = {}
    for name, shape, dt in [
        ("QKV", [3, 4, 128, 1024], BF16),
        ("A", [4, 128, 256], BF16),
        ("O", [4, 128, 1024], F32),
        ("x1", [S_OWN, DM], F32),
        ("xn2T", [8, 128, S_OWN], BF16),
    ]:
        if name in taps:
            tap_aps[name] = nc.dram_tensor("tap_" + name, shape, dt,
                                           kind="ExternalOutput").ap()

    cnt = {}
    hwm = {}
    bar_k = [0]

    with ExitStack() as TOP:
        E = TOP.enter_context
        sem_names = [
            "ph", "ini",
            "sfsync", "sfgpsimd", "sftensor", "sfscalar", "sfvector",
            "ail0", "ail1", "ail2", "ail3", "afl0", "afl1", "afl2", "afl3",
            "a0c", "a0t", "a0p", "as0", "as1", "as2", "as3",
            "aiu0", "aiu1", "aiu2", "aiu3",
            "wl", "axl0", "axl1", "axl2", "axl3", "aln", "alna", "alnx", "alnr", "axt", "axe",
            "apj", "apev", "asx", "asb", "aex", "amx", "agp", "apl", "aav", "aot", "aote",
            "arp", "arev", "aadd", "ax2", "ax2t", "ax2e", "axs", "ax2s",
            "axs0", "axs1", "ax2s0", "ax2s1",
            "dwg0", "dwg1", "dwu0", "dwu1", "dg", "dsil", "dmul", "dwd0", "dwd1", "ddn",
            "dev", "dtp", "dadd", "dxl", "dx2l0", "dx2l1", "dy", "tapd",
        ]
        sems = {s: E(nc.semaphore(s)) for s in sem_names}

        def barrier():
            bar_k[0] += 1
            k = bar_k[0]
            with nc.Block() as blk:
                for en in ENGINES:
                    def f(e, k=k):
                        e.nop().then_inc(sems["ph"], 1)
                        e.wait_ge(sems["ph"], NENG * k)
                    getattr(blk, en)(f)

        def tap_block(items):
            todo = [(nm, fn) for (nm, fn) in items if nm in taps]
            if not todo:
                return
            with nc.Block() as blk:
                @blk.sync
                def _(sy):
                    t0 = cnt.get("tapd", 0)
                    add = [0]
                    def issue(dst, src):
                        sy.dma_start(dst, src).then_inc(sems["tapd"], 16)
                        add[0] += 16
                    for (_, fn) in todo:
                        fn(issue)
                    sy.wait_ge(sems["tapd"], t0 + add[0])
                    cnt["tapd"] = t0 + add[0]
            barrier()

        ident = E(nc.sbuf_tensor("ident", [128, 128], BF16))
        WqJ = [E(nc.sbuf_tensor(f"WqJ{j}", [128, 1024], BF16)) for j in range(8)]
        WkJ = [E(nc.sbuf_tensor(f"WkJ{j}", [128, 1024], BF16)) for j in range(8)]
        WvJ = [E(nc.sbuf_tensor(f"WvJ{j}", [128, 1024], BF16)) for j in range(8)]
        WfJ = [E(nc.sbuf_tensor(f"WfJ{j}", [128, 1024], BF16)) for j in range(8)]

        c_eps = E(nc.sbuf_tensor("c_eps", [128, 1], F32))

        with nc.Block() as blk, \
             nc.sbuf_tensor("io1", [128, 128], F32) as io1, \
             nc.sbuf_tensor("io2", [128, 128], F32) as io2:

            @blk.gpsimd
            def _(g):
                g.iota(io1[:], [[1, 128]], channel_multiplier=0,
                       allow_small_or_imprecise_dtypes=True)
                g.iota(io2[:], [[0, 128]], channel_multiplier=1,
                       allow_small_or_imprecise_dtypes=True)

                g.memset(c_eps[:], EPS).then_inc(sems["ini"], 1)

            @blk.vector
            def _(v):
                v.wait_ge(sems["ini"], 1)
                v.tensor_tensor(ident[:], io1[:], io2[:], ALU.is_equal)
        barrier()

        a0_jobs = []
        for (src, dst, nch, ncol, sm, is_int, ldq) in [
            (Wq, WqJ, 8, 1024, "qk", False, "gpsimd"),
            (Wk, WkJ, 8, 1024, "no", False, "gpsimd"),
            (Wv, WvJ, 8, 1024, "no", False, "gpsimd"),
            (Wfc, WfJ, 8, 1024, "no", False, "gpsimd"),
            (wg_in, d_wgT, 24, 1024, "no", True, "sync"),
            (wu_in, d_wuT, 24, 1024, "no", True, "gpsimd"),
            (wd_in, d_wdT, 8, 3072, "no", True, "sync"),
        ]:
            for c in range(nch):
                a0_jobs.append((src, dst, c, ncol, sm, is_int, ldq))
        NJ = len(a0_jobs)
        nb_of = [ncol // 128 for (_, _, _, ncol, _, _, _) in a0_jobs]
        nb_pre = [sum(nb_of[:j]) for j in range(NJ + 1)]

        with ExitStack() as ph:
            e = ph.enter_context
            wstg_i = [e(nc.sbuf_tensor(f"wstg_i{i}", [128, 3072], I32)) for i in range(4)]
            wstg_b = [e(nc.sbuf_tensor(f"wstg_b{i}", [128, 3072], BF16)) for i in range(4)]
            wpk = [e(nc.sbuf_tensor(f"wpk{i}", [128, 3072], BF16)) for i in range(4)]
            wps = [e(nc.psum_tensor(f"wps{i}", [128, 512], F32)) for i in range(2)]

            S = Sched(cnt, hwm)
            n_ld = {"ail": [0, 0, 0, 0], "aiu": [0, 0, 0, 0]}
            n_fl = [0, 0, 0, 0]
            n_sp = [0, 0, 0, 0]
            for j, (src, dst, c, ncol, sm, is_int, ldq) in enumerate(a0_jobs):
                nb = nb_of[j]
                sl = j % 4
                stg_free = nb_pre[j - 3] if j >= 4 else 0
                if is_int:
                    lsem = "ail" if ldq == "sync" else "aiu"
                    n_ld[lsem][sl] += 1
                    S.add(ldq,
                          lambda e, j=j, src=src, c=c, ncol=ncol:
                          e.dma_start(wstg_i[j % 4][:, 0:ncol],
                                      src[c * 128:(c + 1) * 128, :]),
                          waits=[("a0c", S.n("a0c") - 3)],
                          inc="%s%d" % (lsem, sl), incn=16)
                    S.add("scalar",
                          lambda e, j=j, ncol=ncol:
                          e.activation(wstg_b[j % 4][:, 0:ncol], wstg_i[j % 4][:, 0:ncol],
                                       AF.Copy, bias=0.0, scale=DEQ),
                          waits=[("%s%d" % (lsem, sl), 16 * n_ld[lsem][sl]),
                                 ("a0t", stg_free)],
                          inc="a0c")
                else:
                    n_fl[sl] += 1
                    S.add("gpsimd",
                          lambda e, j=j, src=src, c=c, ncol=ncol:
                          e.dma_start(wstg_b[j % 4][:, 0:ncol],
                                      src[c * 128:(c + 1) * 128, :]),
                          waits=[("a0t", stg_free)],
                          inc="afl%d" % sl, incn=16)
                cast_seen = S.n("a0c")
                for b in range(nb):
                    w = [("a0p", nb_pre[j] + b - 1)]
                    if b == 0:
                        w.append(("a0c", cast_seen) if is_int
                                 else ("afl%d" % sl, 16 * n_fl[sl]))
                    S.add("tensor",
                          lambda e, j=j, b=b:
                          e.matmul(wps[b % 2][:, 0:128],
                                   wstg_b[j % 4][:, b * 128:(b + 1) * 128],
                                   ident[:], start=True, stop=True),
                          waits=w, inc="a0t")
                    pw = [("a0t", S.n("a0t"))]
                    direct = isinstance(dst, list)
                    if b == 0 and not direct:
                        pw.append(("as%d" % sl, 16 * n_sp[sl]))
                    if direct:
                        pdst = (lambda e, dst=dst, b=b, c=c:
                                dst[b][:, c * 128:(c + 1) * 128])
                    else:
                        pdst = (lambda e, j=j, b=b: wpk[j % 4][:, b * 128:(b + 1) * 128])
                    if sm == "qk":
                        S.add("vector",
                              lambda e, pdst=pdst, b=b:
                              e.tensor_scalar(pdst(e), wps[b % 2][:, 0:128],
                                              QK_SCALE, None, ALU.mult),
                              waits=pw, inc="a0p")
                    else:
                        S.add("vector",
                              lambda e, pdst=pdst, b=b:
                              e.tensor_copy(pdst(e), wps[b % 2][:, 0:128]),
                              waits=pw, inc="a0p")
                if not isinstance(dst, list):
                    n_sp[sl] += 1
                    S.add("scalar",
                          lambda e, j=j, dst=dst, c=c, ncol=ncol:
                          e.dma_start(dst[c], wpk[j % 4][:, 0:ncol]),
                          waits=[("a0p", S.n("a0p"))], inc="as%d" % sl, incn=16)
            S.add("sync", lambda e: e.nop(),
                  waits=[("as%d" % i, 16 * n_sp[i]) for i in range(4)])
            S.replay(nc, sems)
        barrier()

        if not _do("ATT"):
            return nc, tap_aps
        with ExitStack() as ph:
            e = ph.enter_context
            xcT = [e(nc.sbuf_tensor(f"xcT{i}", [128, 1024], F32)) for i in range(8)]
            xnm = e(nc.sbuf_tensor("xnm", [128, DM], BF16))
            xnT = [e(nc.sbuf_tensor(f"xnT{j}", [128, 512], BF16)) for j in range(8)]
            Lr = [e(nc.sbuf_tensor(f"Lr{i}", [128, 8], F32)) for i in range(8)]
            Lr2 = [e(nc.sbuf_tensor(f"Lr2_{i}", [128, 8], F32)) for i in range(4)]
            sqd = e(nc.sbuf_tensor("sqd", [128, DM], BF16))
            Qs = [e(nc.sbuf_tensor(f"Qs{t}", [128, 1024], BF16)) for t in range(4)]
            Ks = [e(nc.sbuf_tensor(f"Ks{t}", [128, 1024], BF16)) for t in range(4)]
            Vs = [e(nc.sbuf_tensor(f"Vs{t}", [128, 1024], BF16)) for t in range(8)]
            Ss = e(nc.sbuf_tensor("Ss", [128, 256], F32))
            Sm = e(nc.sbuf_tensor("Sm", [128, 16], F32))
            Se = [e(nc.sbuf_tensor(f"Se{i}", [128, 256], BF16)) for i in range(4)]
            Sl = e(nc.sbuf_tensor("Sl", [128, 16], F32))
            Sr = e(nc.sbuf_tensor("Sr", [128, 16], F32))
            Snr = Sm
            Aa = [e(nc.sbuf_tensor(f"Aa{i}", [128, 256], BF16)) for i in range(4)]
            gramB = [e(nc.sbuf_tensor(f"gramB{i}", [128, 4096], BF16))
                     for i in range(2)]
            avB = [e(nc.sbuf_tensor(f"avB{i}", [128, 4096], BF16)) for i in range(2)]
            gh1 = e(nc.sbuf_tensor("gh1", [128, 2048], BF16))
            OT = [e(nc.sbuf_tensor(f"OT{j}", [128, 512], BF16)) for j in range(8)]
            x1p = [e(nc.sbuf_tensor(f"x1p{i}", [128, 1024], F32)) for i in range(2)]
            xn2m = e(nc.sbuf_tensor("xn2m", [128, DM], BF16))
            xn2s = [e(nc.sbuf_tensor(f"xn2s{j}", [128, 512], BF16)) for j in range(8)]
            pA = [e(nc.psum_tensor(f"pA{i}", [128, 512], F32)) for i in range(2)]
            pB = [e(nc.psum_tensor(f"pB{i}", [128, 512], F32)) for i in range(2)]
            pT = [e(nc.psum_tensor(f"pT{i}", [128, 128], F32)) for i in range(2)]
            pOT = e(nc.psum_tensor("pOT", [128, 1024], F32))

            S = Sched(cnt, hwm)
            cap = {}
            pTw = [None, None]
            pQw = {}

            def ln_acc(Sch, xsrc_fn, Lrt, extra_waits=()):
                Sch.add("scalar",
                        lambda e: e.activation(sqd[:], xsrc_fn(), AF.Copy,
                                               bias=0.0, scale=1.0,
                                               accum_out=Lrt[:, 0:1]),
                        waits=[("alna", Sch.n("alna"))] + list(extra_waits),
                        inc="alna")
                Sch.add("scalar",
                        lambda e: e.activation(sqd[:], xsrc_fn(), AF.Square,
                                               bias=0.0, scale=1.0,
                                               accum_out=Lrt[:, 1:2]),
                        waits=[("alna", Sch.n("alna"))] + list(extra_waits),
                        inc="alna")
                return Sch.n("alna")

            def ln_mid(Sch, Lrt, acc_n):
                Sch.add("vector", lambda e: e.tensor_scalar(
                    Lrt[:, 2:3], Lrt[:, 0:1], -1.0 / 1024.0, None, ALU.mult),
                        waits=[("alna", acc_n)], chain=True)
                Sch.add("vector", lambda e: e.tensor_scalar(
                    Lrt[:, 3:4], Lrt[:, 1:2], 1.0 / 1024.0, None, ALU.mult),
                        chain=True)
                Sch.add("vector", lambda e: e.tensor_mul(
                    Lrt[:, 4:5], Lrt[:, 2:3], Lrt[:, 2:3]), chain=True)
                Sch.add("vector", lambda e: e.tensor_sub(
                    Lrt[:, 5:6], Lrt[:, 3:4], Lrt[:, 4:5]), chain=True, inc="aln")
                Sch.add("scalar",
                        lambda e: e.activation(Lrt[:, 6:7], Lrt[:, 5:6], AF.Ln,
                                               bias=c_eps[:, 0:1], scale=1.0),
                        waits=[("aln", Sch.n("aln")), ("alnx", Sch.n("alnx"))],
                        inc="alnx")
                Sch.add("scalar",
                        lambda e: e.activation(Lrt[:, 7:8], Lrt[:, 6:7], AF.Exp,
                                               bias=0.0, scale=-0.5),
                        waits=[("alnx", Sch.n("alnx"))], inc="alnx")
                return Sch.n("alnx")

            def ln_app(Sch, xsrc_fn, Lrt, out_fn, mid_n, extra_waits=(),
                       out_inc=None):
                Sch.add("vector",
                        lambda e: e.tensor_scalar(out_fn(), xsrc_fn(),
                                                  Lrt[:, 2:3], Lrt[:, 7:8],
                                                  ALU.add, ALU.mult),
                        waits=[("alnx", mid_n)] + list(extra_waits),
                        chain=True, inc=out_inc)

            def load_tile(k):
                w = [("axl%d" % (k % 4), 16 * (k // 4))]
                if k - 8 >= 0:
                    w.append(("arev", 2 * (k - 8 + 1)))
                S.add("sync",
                      lambda e, k=k:
                      e.dma_start(xcT[k % 8][:], x_in[k * 128:(k + 1) * 128, :]),
                      waits=w, inc="axl%d" % (k % 4), incn=16)

            def ln1_a(k):
                w = [("axl%d" % (k % 4), 16 * (k // 4 + 1))]
                if k - 8 >= 0:
                    w.append(("alnr", k - 7))
                cap["l1a", k] = ln_acc(S, (lambda k=k: xcT[k % 8][:]), Lr[k % 8],
                                       extra_waits=w)

            def ln1_m(k):
                cap["l1m", k] = ln_mid(S, Lr[k % 8], cap["l1a", k])

            def ln1_c(k):
                ln_app(S,
                       (lambda k=k: xcT[k % 8][:]),
                       Lr[k % 8],
                       (lambda: xnm[:]),
                       cap["l1m", k],
                       extra_waits=[("axt", 8 * k)],
                       out_inc="alnr")

            def lnt(k):
                i = k % 4
                for b in range(8):
                    w = [("alnr", k + 1)] if b == 0 else []
                    if pTw[b % 2] is not None:
                        w.append(pTw[b % 2])
                    S.add("tensor",
                          lambda e, b=b:
                          e.matmul(pT[b % 2][:], xnm[:, b * 128:(b + 1) * 128],
                                   ident[:], start=True, stop=True),
                          waits=w, inc="axt")
                    ev_w = [("axt", S.n("axt"))]
                    if ("apj", k - 4) in cap:
                        ev_w.append(("apj", cap["apj", k - 4]))
                    S.add("scalar",
                          lambda e, b=b, i=i:
                          e.copy(xnT[b][:, i * 128:(i + 1) * 128], pT[b % 2][:]),
                          waits=ev_w, inc="axe")
                    pTw[b % 2] = ("axe", S.n("axe"))

            def proj(k):
                tt = k % 4
                for (Wjs, Dst, kind) in ((WqJ, Qs[tt], "qk"), (WkJ, Ks[tt], "qk"),
                                         (WvJ, Vs[k % 8], "v")):
                    for half in range(2):
                        pb = (pA if half == 0 else pB)[tt % 2]
                        key = (half, tt % 2)
                        for j in range(8):
                            w = []
                            if j == 0:
                                w = [("axe", 8 * (k + 1))]
                                if pQw.get(key):
                                    w.append(pQw[key])
                            S.add("tensor",
                                  lambda e, Wjs=Wjs, tt=tt, half=half, j=j, pb=pb:
                                  e.matmul(pb[:],
                                           xnT[j][:, tt * 128:(tt + 1) * 128],
                                           Wjs[j][:, half * 512:(half + 1) * 512],
                                           start=(j == 0), stop=(j == 7)),
                                  waits=w, inc=("apj" if j == 7 else None))
                        ev_w = [("apj", S.n("apj"))]
                        if kind == "qk":
                            if ("gram", k - 4) in cap:
                                ev_w.append(("agp", cap["gram", k - 4]))
                        else:
                            if ("pool", k - 8) in cap:
                                ev_w.append(("apl", cap["pool", k - 8]))
                        S.add("scalar",
                              lambda e, Dst=Dst, half=half, pb=pb:
                              e.copy(Dst[:, half * 512:(half + 1) * 512], pb[:]),
                              waits=ev_w, inc="apev")
                        pQw[key] = ("apev", S.n("apev"))
                cap["apev", k] = S.n("apev")
                cap["apj", k] = S.n("apj")

            def front(k):
                tt = k % 4
                e4 = k % 4
                for g in range(4):
                    S.add("vector",
                          lambda e, tt=tt, g=g:
                          e.tensor_tensor(
                              gramB[g % 2][:].rearrange("p (t h d) -> p t h d",
                                                        t=4, d=64),
                              bass.AP(Qs[tt][:].tensor, Qs[tt][:].offset,
                                      [[1024, 128], [0, 4], [64, 16], [1, 64]]),
                              bass.AP(Ks[tt][:].tensor, Ks[tt][:].offset + 256 * g,
                                      [[1024, 128], [64, 4], [0, 16], [1, 64]]),
                              ALU.mult),
                          waits=([("apev", cap["apev", k])] if g == 0 else []),
                          chain=True, inc=None)
                    S.add("vector",
                          lambda e, g=g:
                          e.tensor_tensor(
                              gh1[:].rearrange("p (a b) -> p a b", b=32),
                              bass.AP(gramB[g % 2][:].tensor,
                                      gramB[g % 2][:].offset,
                                      [[4096, 128], [64, 64], [1, 32]]),
                              bass.AP(gramB[g % 2][:].tensor,
                                      gramB[g % 2][:].offset + 32,
                                      [[4096, 128], [64, 64], [1, 32]]),
                              ALU.add),
                          chain=True)
                    S.add("vector",
                          lambda e, g=g:
                          e.tensor_reduce(
                              Ss[:, 64 * g:64 * (g + 1)]
                              .rearrange("p (a b) -> p a b", b=1),
                              gh1[:].rearrange("p (a b) -> p a b", b=32),
                              AX.X, ALU.add),
                          chain=True, inc="agp")
                cap["gram", k] = S.n("agp")
                S.add("vector",
                      lambda e:
                      e.reduce_max(Sm[:, 0:16].rearrange("p (a b) -> p a b", b=1),
                                   bass.AP(Ss[:].tensor, Ss[:].offset,
                                           [[256, 128], [1, 16], [16, 16]]),
                                   AX.X),
                      chain=True)
                sub_w = [("amx", k - 3)] if k - 4 >= 0 else []
                S.add("vector",
                      lambda e, e4=e4:
                      e.tensor_tensor(Se[e4][:].rearrange("p (t h) -> p t h", h=16),
                                      Ss[:].rearrange("p (t h) -> p t h", h=16),
                                      bass.AP(Sm[:].tensor, Sm[:].offset,
                                              [[16, 128], [0, 16], [1, 16]]),
                                      ALU.subtract),
                      waits=sub_w, chain=True, inc="asb")
                S.add("scalar",
                      lambda e, e4=e4: e.activation(Se[e4][:], Se[e4][:], AF.Exp,
                                                    bias=0.0, scale=1.0),
                      waits=[("asb", k + 1)], inc="aex")

            def back_a(k):
                s, tt = divmod(k, 4)
                e4 = k % 4
                S.add("vector",
                      lambda e, e4=e4:
                      e.reduce_sum(Sl[:, 0:16].rearrange("p (a b) -> p a b", b=1),
                                   bass.AP(Se[e4][:].tensor, Se[e4][:].offset,
                                           [[256, 128], [1, 16], [16, 16]]),
                                   AX.X),
                      waits=[("aex", k + 1)])
                S.add("vector", lambda e: e.reciprocal(Sr[:], Sl[:]), chain=True)
                S.add("vector", lambda e: e.tensor_mul(Snr[:], Sl[:], Sr[:]),
                      chain=True)
                S.add("vector",
                      lambda e: e.tensor_scalar(Snr[:], Snr[:], -1.0, 2.0,
                                                ALU.mult, ALU.add),
                      chain=True)
                S.add("vector", lambda e: e.tensor_mul(Sr[:], Sr[:], Snr[:]),
                      chain=True)
                am_w = []
                if ("pool", k - 4) in cap:
                    am_w.append(("apl", cap["pool", k - 4]))
                S.add("vector",
                      lambda e, e4=e4:
                      e.tensor_tensor(Aa[e4][:].rearrange("p (t h) -> p t h", h=16),
                                      Se[e4][:].rearrange("p (t h) -> p t h", h=16),
                                      bass.AP(Sr[:].tensor, Sr[:].offset,
                                              [[16, 128], [0, 16], [1, 16]]),
                                      ALU.mult),
                      waits=am_w, chain=True, inc="amx")
                for g in range(4):
                    S.add("gpsimd",
                          lambda e, e4=e4, g=g, k=k:
                          e.tensor_tensor(
                              avB[g % 2][:].rearrange("p (t h d) -> p t h d",
                                                      t=4, d=64),
                              bass.AP(Aa[e4][:].tensor, Aa[e4][:].offset + 64 * g,
                                      [[256, 128], [16, 4], [1, 16], [0, 64]]),
                              bass.AP(Vs[k % 8][:].tensor,
                                      Vs[k % 8][:].offset + 256 * g,
                                      [[1024, 128], [64, 4], [0, 16], [1, 64]]),
                              ALU.mult),
                          waits=[("amx", k + 1), ("aot", 4 * k + g - 1)],
                          inc="apl")
                cap["pool", k] = S.n("apl")

            def back_a2(k):
                tt = k % 4
                for g in range(4):
                    for i in range(4):
                        for b in range(8):
                            w = []
                            if i == 0 and b == 0:
                                w.append(("apl", 4 * k + g + 1))
                                if g == 0:
                                    w.append(("aote", 8 * k))
                                    if k >= 1:
                                        w.append(("arev", 2 * k))
                            S.add("tensor",
                                  lambda e, g=g, i=i, b=b:
                                  e.matmul(pOT[:, 128 * b:128 * (b + 1)],
                                           avB[g % 2][:, 1024 * i + 128 * b:
                                                       1024 * i + 128 * (b + 1)],
                                           ident[:],
                                           start=(g == 0 and i == 0
                                                  and b % 4 == 0),
                                           stop=(g == 3 and i == 3
                                                 and b % 4 == 3)),
                                  waits=w,
                                  inc=("aot" if (i == 3 and b == 7) else None))
                ev_w0 = [("aot", 4 * (k + 1))]
                if k - 4 >= 0:
                    ev_w0.append(("arp", 2 * (k - 4 + 1)))
                for b in range(8):
                    S.add("scalar",
                          lambda e, b=b, tt=tt:
                          e.copy(OT[b][:, tt * 128:(tt + 1) * 128],
                                 pOT[:, 128 * b:128 * (b + 1)]),
                          waits=(ev_w0 if b == 0 else []), inc="aote")
                for half in range(2):
                    for j in range(8):
                        w = []
                        if j == 0 and half == 0:
                            w = [("aote", 8 * (k + 1))]
                        S.add("tensor",
                              lambda e, tt=tt, half=half, j=j:
                              e.matmul(pOT[:, 512 * half:512 * (half + 1)],
                                       OT[j][:, tt * 128:(tt + 1) * 128],
                                       WfJ[j][:, half * 512:(half + 1) * 512],
                                       start=(j == 0), stop=(j == 7)),
                              waits=w, inc=("arp" if j == 7 else None))
                    r_w = [("arp", S.n("arp"))]
                    if k - 2 >= 0:
                        r_w.append(("axs%d" % (k % 2), 16 * (k // 2)))
                        if ("lnsq", k - 2) in cap:
                            r_w.append(("alna", cap["lnsq", k - 2]))
                    S.add("vector",
                          lambda e, half=half, k=k:
                          e.tensor_add(x1p[k % 2][:, 512 * half:512 * (half + 1)],
                                       pOT[:, 512 * half:512 * (half + 1)],
                                       xcT[k % 8][:, 512 * half:512 * (half + 1)]),
                          waits=r_w, inc="arev")
                S.add("sync",
                      lambda e, k=k:
                      e.dma_start(d_x1[k * 128:(k + 1) * 128, :], x1p[k % 2][:]),
                      waits=[("arev", 2 * (k + 1)),
                             ("axs%d" % (k % 2), 16 * (k // 2))],
                      inc="axs%d" % (k % 2), incn=16)
                w2 = [("arev", 2 * (k + 1))]
                if k - 4 >= 0:
                    w2.append(("ax2", k - 3))
                cap["l2a", k] = ln_acc(S, (lambda k=k: x1p[k % 2][:]), Lr2[k % 4],
                                       extra_waits=w2)
                cap["lnsq", k] = S.n("alna")

            def back_b2m(k):
                cap["l2m", k] = ln_mid(S, Lr2[k % 4], cap["l2a", k])

            def back_b2c(k):
                s, tt = divmod(k, 4)
                e4 = k % 4
                ln_app(S,
                       (lambda k=k: x1p[k % 2][:]),
                       Lr2[e4],
                       (lambda: xn2m[:]),
                       cap["l2m", k],
                       extra_waits=[("ax2t", 8 * k)],
                       out_inc="ax2")
                for b in range(8):
                    w = [("ax2", k + 1)] if b == 0 else []
                    if pTw[b % 2] is not None:
                        w.append(pTw[b % 2])
                    S.add("tensor",
                          lambda e, b=b:
                          e.matmul(pT[b % 2][:], xn2m[:, b * 128:(b + 1) * 128],
                                   ident[:], start=True, stop=True),
                          waits=w, inc="ax2t")
                    ev_w = [("ax2t", S.n("ax2t"))]
                    if s >= 1:
                        ev_w.append(("ax2s%d" % ((s - 1) % 2),
                                     128 * ((s - 1) // 2 + 1)))
                    S.add("scalar",
                          lambda e, b=b, tt=tt:
                          e.copy(xn2s[b][:, tt * 128:(tt + 1) * 128], pT[b % 2][:]),
                          waits=ev_w, inc="ax2e")
                    pTw[b % 2] = ("ax2e", S.n("ax2e"))
                if tt == 3:
                    for j in range(8):
                        w = [("ax2e", S.n("ax2e"))]
                        if j == 0:
                            w.append(("ax2s%d" % (s % 2), 128 * (s // 2)))
                        S.add("sync",
                              lambda e, j=j, s=s:
                              e.dma_start(d_xn2T[j][:, s * 512:(s + 1) * 512],
                                          xn2s[j][:]),
                              waits=w, inc="ax2s%d" % (s % 2), incn=16)

            def a1(k):
                if 0 <= k <= 31:
                    back_a(k)

            def a2(k):
                if 0 <= k <= 31:
                    back_a2(k)

            def op2m(k):
                if 0 <= k <= 31:
                    back_b2m(k)

            def op2c(k):
                if 0 <= k <= 31:
                    back_b2c(k)

            def g_load(k):
                if 5 <= k <= 31:
                    load_tile(k)

            def g_a(k):
                if 4 <= k <= 31:
                    ln1_a(k)

            def g_m(k):
                if 3 <= k <= 31:
                    ln1_m(k)

            def g_c(k):
                if 2 <= k <= 31:
                    ln1_c(k)
                    lnt(k)
                    proj(k)

            for k in range(5):
                load_tile(k)
            for k in range(4):
                ln1_a(k)
            for k in range(3):
                ln1_m(k)
            for k in range(2):
                ln1_c(k)
                lnt(k)
                proj(k)
            for T in range(36):
                if T <= 31:
                    front(T)
                a1(T - 1)
                op2c(T - 4)
                a2(T - 2)
                op2m(T - 3)
                g_load(T + 5)
                g_a(T + 4)
                g_m(T + 3)
                g_c(T + 2)

            S.add("sync", lambda e: e.nop(),
                  waits=[("axs0", 16 * 16), ("axs1", 16 * 16),
                         ("ax2s0", 128 * 4), ("ax2s1", 128 * 4)])
            S.replay(nc, sems)
        barrier()

        tap_block([
            ("x1", lambda issue: issue(tap_aps["x1"], d_x1)),
            ("xn2T", lambda issue: [issue(tap_aps["xn2T"][j], d_xn2T[j])
                                    for j in range(8)]),
        ])

        if not _do("FFN"):
            return nc, tap_aps
        with ExitStack() as ph:
            e4 = ph.enter_context
            wgS = [e4(nc.sbuf_tensor(f"wgS{i}", [128, 1024], BF16)) for i in range(2)]
            wuS = [e4(nc.sbuf_tensor(f"wuS{i}", [128, 1024], BF16)) for i in range(2)]
            wdC = [e4(nc.sbuf_tensor(f"wdC{i}", [128, 3072], BF16)) for i in range(2)]
            hseg = [e4(nc.sbuf_tensor(f"hseg{h}", [128, 512], BF16)) for h in range(24)]
            sg = [e4(nc.sbuf_tensor(f"sg{i}", [128, 512], BF16)) for i in range(2)]
            x2s = [e4(nc.sbuf_tensor(f"x2s{j}_0", [128, 512], BF16)) for j in range(8)]
            x2s2 = [e4(nc.sbuf_tensor(f"x2s{j}_1", [128, 512], BF16)) for j in range(8)]
            x1s = e4(nc.sbuf_tensor("x1s", [128, 4096], F32))
            yst = [e4(nc.sbuf_tensor(f"yst{tb}", [128, 1024], F32)) for tb in range(4)]
            rb2 = [e4(nc.sbuf_tensor(f"rb2{i}", [128, 512], BF16)) for i in range(2)]
            pG = [e4(nc.psum_tensor(f"pG{i}", [128, 512], F32)) for i in range(2)]
            pU = [e4(nc.psum_tensor(f"pU{i}", [128, 512], F32)) for i in range(2)]
            pD = [e4(nc.psum_tensor(f"pD{i}", [128, 512], F32)) for i in range(2)]
            pT2 = [e4(nc.psum_tensor(f"pT2{i}", [128, 128], F32)) for i in range(2)]
            x2buf = [x2s, x2s2]

            S = Sched(cnt, hwm)
            y_pend = []
            for s8 in range(8):
                xp = s8 % 2
                for j in range(8):
                    S.add("sync",
                          lambda e, j=j, s8=s8, xp=xp:
                          e.dma_start(x2buf[xp][j][:],
                                      d_xn2T[j][:, s8 * 512:(s8 + 1) * 512]),
                          waits=[("dg", 2 * 24 * (s8 - 1))],
                          inc="dx2l%d" % xp, incn=16)
                X2L = 16 * (S.n("dx2l%d" % xp) // 16)
                for h in range(24):
                    idx = 24 * s8 + h
                    S.add("sync", lambda e, h=h: e.dma_start(wgS[h % 2][:], d_wgT[h]),
                          waits=[("dg", 2 * (idx - 1))],
                          inc="dwg%d" % (h % 2), incn=16)
                    S.add("gpsimd", lambda e, h=h: e.dma_start(wuS[h % 2][:], d_wuT[h]),
                          waits=[("dg", 2 * (idx - 1))],
                          inc="dwu%d" % (h % 2), incn=16)
                    for j in range(8):
                        w = []
                        if j == 0:
                            w = [("dwg%d" % (h % 2), 16 * (idx // 2 + 1)),
                                 ("dsil", idx - 1),
                                 ("dx2l%d" % xp, X2L)]
                        S.add("tensor",
                              lambda e, h=h, j=j, xp=xp:
                              e.matmul(pG[h % 2][:], wgS[h % 2][:, j * 128:(j + 1) * 128],
                                       x2buf[xp][j][:],
                                       start=(j == 0), stop=(j == 7)),
                              waits=w, inc=("dg" if j == 7 else None))
                    S.add("scalar",
                          lambda e, h=h: e.activation(sg[h % 2][:], pG[h % 2][:],
                                                      AF.Silu, bias=0.0, scale=1.0),
                          waits=[("dg", S.n("dg")), ("dmul", idx - 1)], inc="dsil")
                    for j in range(8):
                        w = [("dmul", idx - 1)] if j == 0 else []
                        if j == 0:
                            w.append(("dwu%d" % (h % 2), 16 * (idx // 2 + 1)))
                        S.add("tensor",
                              lambda e, h=h, j=j, xp=xp:
                              e.matmul(pU[h % 2][:], wuS[h % 2][:, j * 128:(j + 1) * 128],
                                       x2buf[xp][j][:],
                                       start=(j == 0), stop=(j == 7)),
                              waits=w, inc=("dg" if j == 7 else None))
                    S.add("vector",
                          lambda e, h=h: e.tensor_mul(hseg[h][:], sg[h % 2][:],
                                                      pU[h % 2][:]),
                          waits=[("dsil", S.n("dsil")), ("dg", S.n("dg")),
                                 ("ddn", 8 * s8)],
                          inc="dmul")
                    if h == 1:
                        for fn in y_pend:
                            fn()
                        y_pend = []
                S.add("sync",
                      lambda e, s8=s8:
                      e.dma_start(x1s[:].rearrange("p (t d) -> p t d", d=1024),
                                  d_x1[s8 * 512:(s8 + 1) * 512, :]
                                  .rearrange("(t p) d -> p t d", p=128)),
                      waits=[("dadd", 32 * s8)], inc="dxl", incn=16)
                for c in range(8):
                    idx2 = 8 * s8 + c
                    S.add("gpsimd", lambda e, c=c: e.dma_start(wdC[c % 2][:], d_wdT[c]),
                          waits=[("ddn", idx2 - 1)], inc="dwd%d" % (c % 2), incn=16)
                    for k in range(24):
                        w = []
                        if k == 0:
                            w = [("dwd%d" % (c % 2), 16 * (idx2 // 2 + 1)),
                                 ("dmul", 24 * (s8 + 1)),
                                 ("dev", idx2 - 1)]
                        S.add("tensor",
                              lambda e, c=c, k=k:
                              e.matmul(pD[c % 2][:], wdC[c % 2][:, k * 128:(k + 1) * 128],
                                       hseg[k][:], start=(k == 0), stop=(k == 23)),
                              waits=w, inc=("ddn" if k == 23 else None))
                    S.add("scalar",
                          lambda e, c=c: e.copy(rb2[c % 2][:], pD[c % 2][:]),
                          waits=[("ddn", S.n("ddn")), ("dtp", 4 * (idx2 - 1))],
                          inc="dev")
                    for tb in range(4):
                        S.add("tensor",
                              lambda e, c=c, tb=tb:
                              e.matmul(pT2[tb % 2][:],
                                       rb2[c % 2][:, tb * 128:(tb + 1) * 128],
                                       ident[:], start=True, stop=True),
                              waits=[("dev", S.n("dev")), ("dadd", S.n("dadd") - 1)],
                              inc="dtp")
                        S.add("vector",
                              lambda e, c=c, tb=tb:
                              e.tensor_add(yst[tb][:, c * 128:(c + 1) * 128],
                                           pT2[tb % 2][:],
                                           x1s[:, 1024 * tb + 128 * c:
                                               1024 * tb + 128 * c + 128]),
                              waits=[("dtp", S.n("dtp")), ("dxl", 16 * (s8 + 1)),
                                     ("dy", 64 * s8)],
                              inc="dadd")
                for tb in range(4):
                    def mk(s8=s8, tb=tb):
                        def emit():
                            S.add("sync",
                                  lambda e, s8=s8, tb=tb:
                                  e.dma_start(y_out[s8 * 512 + tb * 128:
                                                    s8 * 512 + (tb + 1) * 128, :],
                                              yst[tb][:]),
                                  waits=[("dadd", 32 * (s8 + 1))],
                                  inc="dy", incn=16)
                        return emit
                    y_pend.append(mk())
            for fn in y_pend:
                fn()
            S.add("sync", lambda e: e.nop(), waits=[("dy", 64 * 8)])
            S.replay(nc, sems)
        barrier()

    return nc, tap_aps


def kernel(**inputs):
    x = np.asarray(inputs["L0_attn_x"], dtype=np.float32)
    Wq_ = np.asarray(inputs["L0_attn_Wq"], dtype=np.float32)
    Wk_ = np.asarray(inputs["L0_attn_Wk"], dtype=np.float32)
    Wv_ = np.asarray(inputs["L0_attn_Wv"], dtype=np.float32)
    Wfc_ = np.asarray(inputs["L0_attn_Wfc"], dtype=np.float32)
    wg_ = np.asarray(inputs["L0_ffn_w_gate"], dtype=np.int32)
    wu_ = np.asarray(inputs["L0_ffn_w_up"], dtype=np.int32)
    wd_ = np.asarray(inputs["L0_ffn_w_down"], dtype=np.int32)

    nc, _ = build_program()
    in_maps = []
    for c in range(8):
        b, h = c // 2, c % 2
        xc = np.ascontiguousarray(x[b, h * 4096:(h + 1) * 4096])
        in_maps.append({"x": xc, "Wq": Wq_, "Wk": Wk_, "Wv": Wv_, "Wfc": Wfc_,
                        "wg": wg_, "wu": wu_, "wd": wd_})
    res = run_bass_kernel_spmd(nc, in_maps, core_ids=list(range(8)))
    y = np.empty((4, 8192, 1024), dtype=np.float32)
    for c in range(8):
        b, h = c // 2, c % 2
        y[b, h * 4096:(h + 1) * 4096] = res.results[c]["y"]
    return y


# version 25
# speedup vs baseline: 1.1203x; 1.1068x over previous
import math
import numpy as np
import concourse.bass as bass
import concourse.mybir as mybir
from concourse.bass_utils import run_bass_kernel_spmd
from contextlib import ExitStack

F32 = mybir.dt.float32
BF16 = mybir.dt.bfloat16
I32 = mybir.dt.int32
AF = mybir.ActivationFunctionType
ALU = mybir.AluOpType
AX = mybir.AxisListType

DM = 1024
S_OWN = 4096
QK_SCALE = 0.125
DEQ = float(1.0 / 127.0)
EPS = 1e-5
ENGINES = ("sync", "gpsimd", "tensor", "scalar", "vector")
NENG = len(ENGINES)


class Sched:

    def __init__(self, cnt, hwm):
        self.cnt = cnt
        self.hwm = hwm
        self.ops = {e: [] for e in ENGINES}

    def n(self, sem):
        return self.cnt.get(sem, 0)

    def add(self, eng, fn, waits=(), inc=None, incn=1, chain=False, deps=()):
        w = [(s, v) for (s, v) in waits if v > 0]
        dl = list(deps)
        if chain and self.ops[eng]:
            dl.append((eng, len(self.ops[eng]) - 1))
        cnt_after = None
        if inc is not None:
            self.cnt[inc] = self.cnt.get(inc, 0) + incn
            cnt_after = self.cnt[inc]
        op = {"fn": fn, "waits": w, "deps": [], "inc": inc, "incn": incn,
              "cnt_after": cnt_after, "selfneed": False}
        for (teng, tidx) in dl:
            top = self.ops[teng][tidx]
            if top["inc"] is None:
                top["selfneed"] = True
            op["deps"].append((teng, tidx))
        self.ops[eng].append(op)
        return (eng, len(self.ops[eng]) - 1)

    def replay(self, nc, sems):
        hwm = self.hwm
        selfcount = {}
        for eng in ENGINES:
            sem = "sf" + eng
            n = self.cnt.get(sem, 0)
            for idx, op in enumerate(self.ops[eng]):
                if op["inc"] is not None:
                    selfcount[(eng, idx)] = (op["inc"], op["cnt_after"])
                elif op["selfneed"]:
                    n += 1
                    selfcount[(eng, idx)] = (sem, n)
            self.cnt[sem] = n
        with nc.Block() as blk:
            for eng_name in ENGINES:
                ops = self.ops[eng_name]

                def make(ops=ops, eng_name=eng_name):
                    def f(e):
                        for op in ops:
                            waits = list(op["waits"])
                            for tok in op["deps"]:
                                waits.append(selfcount[tok])
                            for (s, v) in waits:
                                key = (eng_name, s)
                                if v > 0 and hwm.get(key, 0) < v:
                                    e.wait_ge(sems[s], v)
                                    hwm[key] = v
                            ins = op["fn"](e)
                            if op["inc"] is not None:
                                ins.then_inc(sems[op["inc"]], op["incn"])
                            elif op["selfneed"]:
                                ins.then_inc(sems["sf" + eng_name], 1)
                    return f

                getattr(blk, eng_name)(make())


def build_program(taps=(), stop_after=None):
    _ORDER = ["ATT", "FFN"]

    def _do(phname):
        return stop_after is None or _ORDER.index(phname) <= _ORDER.index(stop_after)

    nc = bass.Bass("TRN2", target_bir_lowering=False)

    x_in = nc.dram_tensor("x", [S_OWN, DM], F32, kind="ExternalInput").ap()
    Wq = nc.dram_tensor("Wq", [DM, DM], F32, kind="ExternalInput").ap()
    Wk = nc.dram_tensor("Wk", [DM, DM], F32, kind="ExternalInput").ap()
    Wv = nc.dram_tensor("Wv", [DM, DM], F32, kind="ExternalInput").ap()
    Wfc = nc.dram_tensor("Wfc", [DM, DM], F32, kind="ExternalInput").ap()
    wg_in = nc.dram_tensor("wg", [3072, DM], I32, kind="ExternalInput").ap()
    wu_in = nc.dram_tensor("wu", [3072, DM], I32, kind="ExternalInput").ap()
    wd_in = nc.dram_tensor("wd", [DM, 3072], I32, kind="ExternalInput").ap()
    y_out = nc.dram_tensor("y", [S_OWN, DM], F32, kind="ExternalOutput").ap()

    d_WqT = nc.dram_tensor("d_WqT", [8, 128, 1024], BF16).ap()
    d_WkT = nc.dram_tensor("d_WkT", [8, 128, 1024], BF16).ap()
    d_WvT = nc.dram_tensor("d_WvT", [8, 128, 1024], BF16).ap()
    d_WfcT = nc.dram_tensor("d_WfcT", [8, 128, 1024], BF16).ap()
    d_wgT = nc.dram_tensor("d_wgT", [24, 128, 1024], BF16).ap()
    d_wuT = nc.dram_tensor("d_wuT", [24, 128, 1024], BF16).ap()
    d_wdT = nc.dram_tensor("d_wdT", [8, 128, 3072], BF16).ap()
    d_x1 = nc.dram_tensor("d_x1", [S_OWN, DM], F32).ap()
    d_xn2T = nc.dram_tensor("d_xn2T", [8, 128, S_OWN], BF16).ap()

    tap_aps = {}
    for name, shape, dt in [
        ("QKV", [3, 4, 128, 1024], BF16),
        ("A", [4, 128, 256], BF16),
        ("O", [4, 128, 1024], F32),
        ("x1", [S_OWN, DM], F32),
        ("xn2T", [8, 128, S_OWN], BF16),
    ]:
        if name in taps:
            tap_aps[name] = nc.dram_tensor("tap_" + name, shape, dt,
                                           kind="ExternalOutput").ap()

    cnt = {}
    hwm = {}
    bar_k = [0]

    with ExitStack() as TOP:
        E = TOP.enter_context
        sem_names = [
            "ph", "ini",
            "sfsync", "sfgpsimd", "sftensor", "sfscalar", "sfvector",
            "ail0", "ail1", "ail2", "ail3", "afl0", "afl1", "afl2", "afl3",
            "a0c", "a0t", "a0p", "as0", "as1", "as2", "as3",
            "aiu0", "aiu1", "aiu2", "aiu3",
            "wl", "axl0", "axl1", "axl2", "axl3", "aln", "alna", "alnx", "alnr", "axt", "axe",
            "apj", "apev", "asx", "asb", "aex", "amx", "agp", "apl", "aav", "aot", "aote",
            "arp", "arev", "aadd", "ax2", "ax2t", "ax2e", "axs", "ax2s",
            "axs0", "axs1", "ax2s0", "ax2s1",
            "dwg0", "dwg1", "dwg2", "dwg3", "dwu0", "dwu1", "dwu2", "dwu3",
            "dg", "dsil", "dmul", "dwd0", "dwd1", "dwd2", "dwd3", "ddn",
            "dev", "dtp", "dadd", "dxl", "dx2l0", "dx2l1", "dy", "tapd",
        ]
        sems = {s: E(nc.semaphore(s)) for s in sem_names}

        def barrier():
            bar_k[0] += 1
            k = bar_k[0]
            with nc.Block() as blk:
                for en in ENGINES:
                    def f(e, k=k):
                        e.nop().then_inc(sems["ph"], 1)
                        e.wait_ge(sems["ph"], NENG * k)
                    getattr(blk, en)(f)

        def tap_block(items):
            todo = [(nm, fn) for (nm, fn) in items if nm in taps]
            if not todo:
                return
            with nc.Block() as blk:
                @blk.sync
                def _(sy):
                    t0 = cnt.get("tapd", 0)
                    add = [0]
                    def issue(dst, src):
                        sy.dma_start(dst, src).then_inc(sems["tapd"], 16)
                        add[0] += 16
                    for (_, fn) in todo:
                        fn(issue)
                    sy.wait_ge(sems["tapd"], t0 + add[0])
                    cnt["tapd"] = t0 + add[0]
            barrier()

        ident = E(nc.sbuf_tensor("ident", [128, 128], BF16))
        WqJ = [E(nc.sbuf_tensor(f"WqJ{j}", [128, 1024], BF16)) for j in range(8)]
        WkJ = [E(nc.sbuf_tensor(f"WkJ{j}", [128, 1024], BF16)) for j in range(8)]
        WvJ = [E(nc.sbuf_tensor(f"WvJ{j}", [128, 1024], BF16)) for j in range(8)]
        WfJ = [E(nc.sbuf_tensor(f"WfJ{j}", [128, 1024], BF16)) for j in range(8)]

        c_eps = E(nc.sbuf_tensor("c_eps", [128, 1], F32))

        with nc.Block() as blk, \
             nc.sbuf_tensor("io1", [128, 128], F32) as io1, \
             nc.sbuf_tensor("io2", [128, 128], F32) as io2:

            @blk.gpsimd
            def _(g):
                g.iota(io1[:], [[1, 128]], channel_multiplier=0,
                       allow_small_or_imprecise_dtypes=True)
                g.iota(io2[:], [[0, 128]], channel_multiplier=1,
                       allow_small_or_imprecise_dtypes=True)

                g.memset(c_eps[:], EPS).then_inc(sems["ini"], 1)

            @blk.vector
            def _(v):
                v.wait_ge(sems["ini"], 1)
                v.tensor_tensor(ident[:], io1[:], io2[:], ALU.is_equal)
        barrier()

        a0_jobs = []
        for (src, dst, nch, ncol, sm, is_int, ldq) in [
            (Wq, WqJ, 8, 1024, "qk", False, "gpsimd"),
            (Wk, WkJ, 8, 1024, "no", False, "gpsimd"),
            (Wv, WvJ, 8, 1024, "no", False, "gpsimd"),
            (Wfc, WfJ, 8, 1024, "no", False, "gpsimd"),
            (wg_in, d_wgT, 24, 1024, "no", True, "sync"),
            (wu_in, d_wuT, 24, 1024, "no", True, "gpsimd"),
            (wd_in, d_wdT, 8, 3072, "no", True, "sync"),
        ]:
            for c in range(nch):
                a0_jobs.append((src, dst, c, ncol, sm, is_int, ldq))
        NJ = len(a0_jobs)
        nb_of = [ncol // 128 for (_, _, _, ncol, _, _, _) in a0_jobs]
        nb_pre = [sum(nb_of[:j]) for j in range(NJ + 1)]

        with ExitStack() as ph:
            e = ph.enter_context
            wstg_i = [e(nc.sbuf_tensor(f"wstg_i{i}", [128, 3072], I32)) for i in range(4)]
            wstg_b = [e(nc.sbuf_tensor(f"wstg_b{i}", [128, 3072], BF16)) for i in range(4)]
            wpk = [e(nc.sbuf_tensor(f"wpk{i}", [128, 3072], BF16)) for i in range(4)]
            wps = [e(nc.psum_tensor(f"wps{i}", [128, 512], F32)) for i in range(2)]

            S = Sched(cnt, hwm)
            n_ld = {"ail": [0, 0, 0, 0], "aiu": [0, 0, 0, 0]}
            n_fl = [0, 0, 0, 0]
            n_sp = [0, 0, 0, 0]
            for j, (src, dst, c, ncol, sm, is_int, ldq) in enumerate(a0_jobs):
                nb = nb_of[j]
                sl = j % 4
                stg_free = nb_pre[j - 3] if j >= 4 else 0
                if is_int:
                    lsem = "ail" if ldq == "sync" else "aiu"
                    n_ld[lsem][sl] += 1
                    S.add(ldq,
                          lambda e, j=j, src=src, c=c, ncol=ncol:
                          e.dma_start(wstg_i[j % 4][:, 0:ncol],
                                      src[c * 128:(c + 1) * 128, :]),
                          waits=[("a0c", S.n("a0c") - 3)],
                          inc="%s%d" % (lsem, sl), incn=16)
                    S.add("scalar",
                          lambda e, j=j, ncol=ncol:
                          e.activation(wstg_b[j % 4][:, 0:ncol], wstg_i[j % 4][:, 0:ncol],
                                       AF.Copy, bias=0.0, scale=DEQ),
                          waits=[("%s%d" % (lsem, sl), 16 * n_ld[lsem][sl]),
                                 ("a0t", stg_free)],
                          inc="a0c")
                else:
                    n_fl[sl] += 1
                    S.add("gpsimd",
                          lambda e, j=j, src=src, c=c, ncol=ncol:
                          e.dma_start(wstg_b[j % 4][:, 0:ncol],
                                      src[c * 128:(c + 1) * 128, :]),
                          waits=[("a0t", stg_free)],
                          inc="afl%d" % sl, incn=16)
                cast_seen = S.n("a0c")
                for b in range(nb):
                    w = [("a0p", nb_pre[j] + b - 1)]
                    if b == 0:
                        w.append(("a0c", cast_seen) if is_int
                                 else ("afl%d" % sl, 16 * n_fl[sl]))
                    S.add("tensor",
                          lambda e, j=j, b=b:
                          e.matmul(wps[b % 2][:, 0:128],
                                   wstg_b[j % 4][:, b * 128:(b + 1) * 128],
                                   ident[:], start=True, stop=True),
                          waits=w, inc="a0t")
                    pw = [("a0t", S.n("a0t"))]
                    direct = isinstance(dst, list)
                    if b == 0 and not direct:
                        pw.append(("as%d" % sl, 16 * n_sp[sl]))
                    if direct:
                        pdst = (lambda e, dst=dst, b=b, c=c:
                                dst[b][:, c * 128:(c + 1) * 128])
                    else:
                        pdst = (lambda e, j=j, b=b: wpk[j % 4][:, b * 128:(b + 1) * 128])
                    if sm == "qk":
                        S.add("vector",
                              lambda e, pdst=pdst, b=b:
                              e.tensor_scalar(pdst(e), wps[b % 2][:, 0:128],
                                              QK_SCALE, None, ALU.mult),
                              waits=pw, inc="a0p")
                    else:
                        S.add("vector",
                              lambda e, pdst=pdst, b=b:
                              e.tensor_copy(pdst(e), wps[b % 2][:, 0:128]),
                              waits=pw, inc="a0p")
                if not isinstance(dst, list):
                    n_sp[sl] += 1
                    S.add("scalar",
                          lambda e, j=j, dst=dst, c=c, ncol=ncol:
                          e.dma_start(dst[c], wpk[j % 4][:, 0:ncol]),
                          waits=[("a0p", S.n("a0p"))], inc="as%d" % sl, incn=16)
            S.add("sync", lambda e: e.nop(),
                  waits=[("as%d" % i, 16 * n_sp[i]) for i in range(4)])
            S.replay(nc, sems)
        barrier()

        if not _do("ATT"):
            return nc, tap_aps
        with ExitStack() as ph:
            e = ph.enter_context
            xcT = [e(nc.sbuf_tensor(f"xcT{i}", [128, 1024], F32)) for i in range(8)]
            xnm = e(nc.sbuf_tensor("xnm", [128, DM], BF16))
            xnT = [e(nc.sbuf_tensor(f"xnT{j}", [128, 512], BF16)) for j in range(8)]
            Lr = [e(nc.sbuf_tensor(f"Lr{i}", [128, 8], F32)) for i in range(8)]
            Lr2 = [e(nc.sbuf_tensor(f"Lr2_{i}", [128, 8], F32)) for i in range(4)]
            sqd = e(nc.sbuf_tensor("sqd", [128, DM], BF16))
            Qs = [e(nc.sbuf_tensor(f"Qs{t}", [128, 1024], BF16)) for t in range(4)]
            Ks = [e(nc.sbuf_tensor(f"Ks{t}", [128, 1024], BF16)) for t in range(4)]
            Vs = [e(nc.sbuf_tensor(f"Vs{t}", [128, 1024], BF16)) for t in range(8)]
            Ss = e(nc.sbuf_tensor("Ss", [128, 256], F32))
            Sm = e(nc.sbuf_tensor("Sm", [128, 16], F32))
            Se = [e(nc.sbuf_tensor(f"Se{i}", [128, 256], BF16)) for i in range(4)]
            Sl = e(nc.sbuf_tensor("Sl", [128, 16], F32))
            Sr = e(nc.sbuf_tensor("Sr", [128, 16], F32))
            Snr = Sm
            Aa = [e(nc.sbuf_tensor(f"Aa{i}", [128, 256], BF16)) for i in range(4)]
            gramB = [e(nc.sbuf_tensor(f"gramB{i}", [128, 4096], BF16))
                     for i in range(2)]
            avB = [e(nc.sbuf_tensor(f"avB{i}", [128, 4096], BF16)) for i in range(2)]
            gh1 = e(nc.sbuf_tensor("gh1", [128, 2048], BF16))
            OT = [e(nc.sbuf_tensor(f"OT{j}", [128, 512], BF16)) for j in range(8)]
            x1p = [e(nc.sbuf_tensor(f"x1p{i}", [128, 1024], F32)) for i in range(2)]
            xn2m = e(nc.sbuf_tensor("xn2m", [128, DM], BF16))
            xn2s = [e(nc.sbuf_tensor(f"xn2s{j}", [128, 512], BF16)) for j in range(8)]
            pA = [e(nc.psum_tensor(f"pA{i}", [128, 512], F32)) for i in range(2)]
            pB = [e(nc.psum_tensor(f"pB{i}", [128, 512], F32)) for i in range(2)]
            pT = [e(nc.psum_tensor(f"pT{i}", [128, 128], F32)) for i in range(2)]
            pOT = e(nc.psum_tensor("pOT", [128, 1024], F32))

            S = Sched(cnt, hwm)
            cap = {}
            pTw = [None, None]
            pQw = {}

            def ln_acc(Sch, xsrc_fn, Lrt, extra_waits=()):
                Sch.add("scalar",
                        lambda e: e.activation(sqd[:], xsrc_fn(), AF.Copy,
                                               bias=0.0, scale=1.0,
                                               accum_out=Lrt[:, 0:1]),
                        waits=[("alna", Sch.n("alna"))] + list(extra_waits),
                        inc="alna")
                Sch.add("scalar",
                        lambda e: e.activation(sqd[:], xsrc_fn(), AF.Square,
                                               bias=0.0, scale=1.0,
                                               accum_out=Lrt[:, 1:2]),
                        waits=[("alna", Sch.n("alna"))] + list(extra_waits),
                        inc="alna")
                return Sch.n("alna")

            def ln_mid(Sch, Lrt, acc_n):
                Sch.add("vector", lambda e: e.tensor_scalar(
                    Lrt[:, 2:3], Lrt[:, 0:1], -1.0 / 1024.0, None, ALU.mult),
                        waits=[("alna", acc_n)], chain=True)
                Sch.add("vector", lambda e: e.tensor_scalar(
                    Lrt[:, 3:4], Lrt[:, 1:2], 1.0 / 1024.0, None, ALU.mult),
                        chain=True)
                Sch.add("vector", lambda e: e.tensor_mul(
                    Lrt[:, 4:5], Lrt[:, 2:3], Lrt[:, 2:3]), chain=True)
                Sch.add("vector", lambda e: e.tensor_sub(
                    Lrt[:, 5:6], Lrt[:, 3:4], Lrt[:, 4:5]), chain=True, inc="aln")
                Sch.add("scalar",
                        lambda e: e.activation(Lrt[:, 6:7], Lrt[:, 5:6], AF.Ln,
                                               bias=c_eps[:, 0:1], scale=1.0),
                        waits=[("aln", Sch.n("aln")), ("alnx", Sch.n("alnx"))],
                        inc="alnx")
                Sch.add("scalar",
                        lambda e: e.activation(Lrt[:, 7:8], Lrt[:, 6:7], AF.Exp,
                                               bias=0.0, scale=-0.5),
                        waits=[("alnx", Sch.n("alnx"))], inc="alnx")
                return Sch.n("alnx")

            def ln_app(Sch, xsrc_fn, Lrt, out_fn, mid_n, extra_waits=(),
                       out_inc=None):
                Sch.add("vector",
                        lambda e: e.tensor_scalar(out_fn(), xsrc_fn(),
                                                  Lrt[:, 2:3], Lrt[:, 7:8],
                                                  ALU.add, ALU.mult),
                        waits=[("alnx", mid_n)] + list(extra_waits),
                        chain=True, inc=out_inc)

            def load_tile(k):
                w = [("axl%d" % (k % 4), 16 * (k // 4))]
                if k - 8 >= 0:
                    w.append(("arev", 2 * (k - 8 + 1)))
                S.add("sync",
                      lambda e, k=k:
                      e.dma_start(xcT[k % 8][:], x_in[k * 128:(k + 1) * 128, :]),
                      waits=w, inc="axl%d" % (k % 4), incn=16)

            def ln1_a(k):
                w = [("axl%d" % (k % 4), 16 * (k // 4 + 1))]
                if k - 8 >= 0:
                    w.append(("alnr", k - 7))
                cap["l1a", k] = ln_acc(S, (lambda k=k: xcT[k % 8][:]), Lr[k % 8],
                                       extra_waits=w)

            def ln1_m(k):
                cap["l1m", k] = ln_mid(S, Lr[k % 8], cap["l1a", k])

            def ln1_c(k):
                ln_app(S,
                       (lambda k=k: xcT[k % 8][:]),
                       Lr[k % 8],
                       (lambda: xnm[:]),
                       cap["l1m", k],
                       extra_waits=[("axt", 8 * k)],
                       out_inc="alnr")

            def lnt(k):
                i = k % 4
                for b in range(8):
                    w = [("alnr", k + 1)] if b == 0 else []
                    if pTw[b % 2] is not None:
                        w.append(pTw[b % 2])
                    S.add("tensor",
                          lambda e, b=b:
                          e.matmul(pT[b % 2][:], xnm[:, b * 128:(b + 1) * 128],
                                   ident[:], start=True, stop=True),
                          waits=w, inc="axt")
                    ev_w = [("axt", S.n("axt"))]
                    if ("apj", k - 4) in cap:
                        ev_w.append(("apj", cap["apj", k - 4]))
                    S.add("scalar",
                          lambda e, b=b, i=i:
                          e.copy(xnT[b][:, i * 128:(i + 1) * 128], pT[b % 2][:]),
                          waits=ev_w, inc="axe")
                    pTw[b % 2] = ("axe", S.n("axe"))

            def proj(k):
                tt = k % 4
                for (Wjs, Dst, kind) in ((WqJ, Qs[tt], "qk"), (WkJ, Ks[tt], "qk"),
                                         (WvJ, Vs[k % 8], "v")):
                    for half in range(2):
                        pb = (pA if half == 0 else pB)[tt % 2]
                        key = (half, tt % 2)
                        for j in range(8):
                            w = []
                            if j == 0:
                                w = [("axe", 8 * (k + 1))]
                                if pQw.get(key):
                                    w.append(pQw[key])
                            S.add("tensor",
                                  lambda e, Wjs=Wjs, tt=tt, half=half, j=j, pb=pb:
                                  e.matmul(pb[:],
                                           xnT[j][:, tt * 128:(tt + 1) * 128],
                                           Wjs[j][:, half * 512:(half + 1) * 512],
                                           start=(j == 0), stop=(j == 7)),
                                  waits=w, inc=("apj" if j == 7 else None))
                        ev_w = [("apj", S.n("apj"))]
                        if kind == "qk":
                            if ("gram", k - 4) in cap:
                                ev_w.append(("agp", cap["gram", k - 4]))
                        else:
                            if ("pool", k - 8) in cap:
                                ev_w.append(("apl", cap["pool", k - 8]))
                        S.add("scalar",
                              lambda e, Dst=Dst, half=half, pb=pb:
                              e.copy(Dst[:, half * 512:(half + 1) * 512], pb[:]),
                              waits=ev_w, inc="apev")
                        pQw[key] = ("apev", S.n("apev"))
                cap["apev", k] = S.n("apev")
                cap["apj", k] = S.n("apj")

            def front(k):
                tt = k % 4
                e4 = k % 4
                for g in range(4):
                    S.add("vector",
                          lambda e, tt=tt, g=g:
                          e.tensor_tensor(
                              gramB[g % 2][:].rearrange("p (t h d) -> p t h d",
                                                        t=4, d=64),
                              bass.AP(Qs[tt][:].tensor, Qs[tt][:].offset,
                                      [[1024, 128], [0, 4], [64, 16], [1, 64]]),
                              bass.AP(Ks[tt][:].tensor, Ks[tt][:].offset + 256 * g,
                                      [[1024, 128], [64, 4], [0, 16], [1, 64]]),
                              ALU.mult),
                          waits=([("apev", cap["apev", k])] if g == 0 else []),
                          chain=True, inc=None)
                    S.add("vector",
                          lambda e, g=g:
                          e.tensor_tensor(
                              gh1[:].rearrange("p (a b) -> p a b", b=32),
                              bass.AP(gramB[g % 2][:].tensor,
                                      gramB[g % 2][:].offset,
                                      [[4096, 128], [64, 64], [1, 32]]),
                              bass.AP(gramB[g % 2][:].tensor,
                                      gramB[g % 2][:].offset + 32,
                                      [[4096, 128], [64, 64], [1, 32]]),
                              ALU.add),
                          chain=True)
                    S.add("vector",
                          lambda e, g=g:
                          e.tensor_reduce(
                              Ss[:, 64 * g:64 * (g + 1)]
                              .rearrange("p (a b) -> p a b", b=1),
                              gh1[:].rearrange("p (a b) -> p a b", b=32),
                              AX.X, ALU.add),
                          chain=True, inc="agp")
                cap["gram", k] = S.n("agp")
                S.add("vector",
                      lambda e:
                      e.reduce_max(Sm[:, 0:16].rearrange("p (a b) -> p a b", b=1),
                                   bass.AP(Ss[:].tensor, Ss[:].offset,
                                           [[256, 128], [1, 16], [16, 16]]),
                                   AX.X),
                      chain=True)
                sub_w = [("amx", k - 3)] if k - 4 >= 0 else []
                S.add("vector",
                      lambda e, e4=e4:
                      e.tensor_tensor(Se[e4][:].rearrange("p (t h) -> p t h", h=16),
                                      Ss[:].rearrange("p (t h) -> p t h", h=16),
                                      bass.AP(Sm[:].tensor, Sm[:].offset,
                                              [[16, 128], [0, 16], [1, 16]]),
                                      ALU.subtract),
                      waits=sub_w, chain=True, inc="asb")
                S.add("scalar",
                      lambda e, e4=e4: e.activation(Se[e4][:], Se[e4][:], AF.Exp,
                                                    bias=0.0, scale=1.0),
                      waits=[("asb", k + 1)], inc="aex")

            def back_a(k):
                s, tt = divmod(k, 4)
                e4 = k % 4
                S.add("vector",
                      lambda e, e4=e4:
                      e.reduce_sum(Sl[:, 0:16].rearrange("p (a b) -> p a b", b=1),
                                   bass.AP(Se[e4][:].tensor, Se[e4][:].offset,
                                           [[256, 128], [1, 16], [16, 16]]),
                                   AX.X),
                      waits=[("aex", k + 1)])
                S.add("vector", lambda e: e.reciprocal(Sr[:], Sl[:]), chain=True)
                S.add("vector", lambda e: e.tensor_mul(Snr[:], Sl[:], Sr[:]),
                      chain=True)
                S.add("vector",
                      lambda e: e.tensor_scalar(Snr[:], Snr[:], -1.0, 2.0,
                                                ALU.mult, ALU.add),
                      chain=True)
                S.add("vector", lambda e: e.tensor_mul(Sr[:], Sr[:], Snr[:]),
                      chain=True)
                am_w = []
                if ("pool", k - 4) in cap:
                    am_w.append(("apl", cap["pool", k - 4]))
                S.add("vector",
                      lambda e, e4=e4:
                      e.tensor_tensor(Aa[e4][:].rearrange("p (t h) -> p t h", h=16),
                                      Se[e4][:].rearrange("p (t h) -> p t h", h=16),
                                      bass.AP(Sr[:].tensor, Sr[:].offset,
                                              [[16, 128], [0, 16], [1, 16]]),
                                      ALU.mult),
                      waits=am_w, chain=True, inc="amx")
                for g in range(4):
                    S.add("gpsimd",
                          lambda e, e4=e4, g=g, k=k:
                          e.tensor_tensor(
                              avB[g % 2][:].rearrange("p (t h d) -> p t h d",
                                                      t=4, d=64),
                              bass.AP(Aa[e4][:].tensor, Aa[e4][:].offset + 64 * g,
                                      [[256, 128], [16, 4], [1, 16], [0, 64]]),
                              bass.AP(Vs[k % 8][:].tensor,
                                      Vs[k % 8][:].offset + 256 * g,
                                      [[1024, 128], [64, 4], [0, 16], [1, 64]]),
                              ALU.mult),
                          waits=[("amx", k + 1), ("aot", 4 * k + g - 1)],
                          inc="apl")
                cap["pool", k] = S.n("apl")

            def back_a2(k):
                tt = k % 4
                for g in range(4):
                    for i in range(4):
                        for b in range(8):
                            w = []
                            if i == 0 and b == 0:
                                w.append(("apl", 4 * k + g + 1))
                                if g == 0:
                                    w.append(("aote", 8 * k))
                            S.add("tensor",
                                  lambda e, g=g, i=i, b=b:
                                  e.matmul(pOT[:, 128 * b:128 * (b + 1)],
                                           avB[g % 2][:, 1024 * i + 128 * b:
                                                       1024 * i + 128 * (b + 1)],
                                           ident[:],
                                           start=(g == 0 and i == 0
                                                  and b % 4 == 0),
                                           stop=(g == 3 and i == 3
                                                 and b % 4 == 3)),
                                  waits=w,
                                  inc=("aot" if (i == 3 and b == 7) else None))
                ev_w0 = [("aot", 4 * (k + 1))]
                if k - 4 >= 0:
                    ev_w0.append(("arp", 2 * (k - 4 + 1)))
                for b in range(8):
                    S.add("scalar",
                          lambda e, b=b, tt=tt:
                          e.copy(OT[b][:, tt * 128:(tt + 1) * 128],
                                 pOT[:, 128 * b:128 * (b + 1)]),
                          waits=(ev_w0 if b == 0 else []), inc="aote")

            def back_b(k):
                s, tt = divmod(k, 4)
                e4 = k % 4
                for half in range(2):
                    pb = (pA if half == 0 else pB)[tt % 2]
                    key = (half, tt % 2)
                    for j in range(8):
                        w = []
                        if j == 0:
                            w = [("aote", 8 * (k + 1))]
                            if pQw.get(key):
                                w.append(pQw[key])
                        S.add("tensor",
                              lambda e, tt=tt, half=half, j=j, pb=pb:
                              e.matmul(pb[:],
                                       OT[j][:, tt * 128:(tt + 1) * 128],
                                       WfJ[j][:, half * 512:(half + 1) * 512],
                                       start=(j == 0), stop=(j == 7)),
                              waits=w, inc=("arp" if j == 7 else None))
                    r_w = [("arp", S.n("arp"))]
                    if k - 2 >= 0:
                        r_w.append(("axs%d" % (k % 2), 16 * (k // 2)))
                        if ("lnsq", k - 2) in cap:
                            r_w.append(("alna", cap["lnsq", k - 2]))
                    S.add("vector",
                          lambda e, half=half, k=k, pb=pb:
                          e.tensor_add(x1p[k % 2][:, 512 * half:512 * (half + 1)],
                                       pb[:],
                                       xcT[k % 8][:, 512 * half:512 * (half + 1)]),
                          waits=r_w, inc="arev")
                    pQw[key] = ("arev", S.n("arev"))
                S.add("sync",
                      lambda e, k=k:
                      e.dma_start(d_x1[k * 128:(k + 1) * 128, :], x1p[k % 2][:]),
                      waits=[("arev", 2 * (k + 1)),
                             ("axs%d" % (k % 2), 16 * (k // 2))],
                      inc="axs%d" % (k % 2), incn=16)
                w2 = [("arev", 2 * (k + 1))]
                if k - 4 >= 0:
                    w2.append(("ax2", k - 3))
                cap["l2a", k] = ln_acc(S, (lambda k=k: x1p[k % 2][:]), Lr2[e4],
                                       extra_waits=w2)
                cap["lnsq", k] = S.n("alna")

            def back_b2m(k):
                cap["l2m", k] = ln_mid(S, Lr2[k % 4], cap["l2a", k])

            def back_b2c(k):
                s, tt = divmod(k, 4)
                e4 = k % 4
                ln_app(S,
                       (lambda k=k: x1p[k % 2][:]),
                       Lr2[e4],
                       (lambda: xn2m[:]),
                       cap["l2m", k],
                       extra_waits=[("ax2t", 8 * k)],
                       out_inc="ax2")
                for b in range(8):
                    w = [("ax2", k + 1)] if b == 0 else []
                    if pTw[b % 2] is not None:
                        w.append(pTw[b % 2])
                    S.add("tensor",
                          lambda e, b=b:
                          e.matmul(pT[b % 2][:], xn2m[:, b * 128:(b + 1) * 128],
                                   ident[:], start=True, stop=True),
                          waits=w, inc="ax2t")
                    ev_w = [("ax2t", S.n("ax2t"))]
                    if s >= 1:
                        ev_w.append(("ax2s%d" % ((s - 1) % 2),
                                     128 * ((s - 1) // 2 + 1)))
                    S.add("scalar",
                          lambda e, b=b, tt=tt:
                          e.copy(xn2s[b][:, tt * 128:(tt + 1) * 128], pT[b % 2][:]),
                          waits=ev_w, inc="ax2e")
                    pTw[b % 2] = ("ax2e", S.n("ax2e"))
                if tt == 3:
                    for j in range(8):
                        w = [("ax2e", S.n("ax2e"))]
                        if j == 0:
                            w.append(("ax2s%d" % (s % 2), 128 * (s // 2)))
                        S.add("sync",
                              lambda e, j=j, s=s:
                              e.dma_start(d_xn2T[j][:, s * 512:(s + 1) * 512],
                                          xn2s[j][:]),
                              waits=w, inc="ax2s%d" % (s % 2), incn=16)

            def a1(k):
                if 0 <= k <= 31:
                    back_a(k)

            def a2(k):
                if 0 <= k <= 31:
                    back_a2(k)

            def op(k):
                if 0 <= k <= 31:
                    back_b(k)

            def op2m(k):
                if 0 <= k <= 31:
                    back_b2m(k)

            def op2c(k):
                if 0 <= k <= 31:
                    back_b2c(k)

            def g_load(k):
                if 5 <= k <= 31:
                    load_tile(k)

            def g_a(k):
                if 4 <= k <= 31:
                    ln1_a(k)

            def g_m(k):
                if 3 <= k <= 31:
                    ln1_m(k)

            def g_c(k):
                if 2 <= k <= 31:
                    ln1_c(k)
                    lnt(k)
                    proj(k)

            for k in range(5):
                load_tile(k)
            for k in range(4):
                ln1_a(k)
            for k in range(3):
                ln1_m(k)
            for k in range(2):
                ln1_c(k)
                lnt(k)
                proj(k)
            for T in range(36):
                if T <= 31:
                    front(T)
                a1(T - 1)
                a2(T - 2)
                op2c(T - 5)
                op(T - 3)
                op2m(T - 4)
                g_load(T + 5)
                g_a(T + 4)
                g_m(T + 3)
                g_c(T + 2)
            op2c(31)
            S.add("sync", lambda e: e.nop(),
                  waits=[("axs0", 16 * 16), ("axs1", 16 * 16),
                         ("ax2s0", 128 * 4), ("ax2s1", 128 * 4)])
            S.replay(nc, sems)
        barrier()

        tap_block([
            ("x1", lambda issue: issue(tap_aps["x1"], d_x1)),
            ("xn2T", lambda issue: [issue(tap_aps["xn2T"][j], d_xn2T[j])
                                    for j in range(8)]),
        ])

        if not _do("FFN"):
            return nc, tap_aps
        with ExitStack() as ph:
            e4 = ph.enter_context
            wgS = [e4(nc.sbuf_tensor(f"wgS{i}", [128, 1024], BF16)) for i in range(4)]
            wuS = [e4(nc.sbuf_tensor(f"wuS{i}", [128, 1024], BF16)) for i in range(4)]
            wdC = [e4(nc.sbuf_tensor(f"wdC{i}", [128, 3072], BF16)) for i in range(4)]
            hseg = [e4(nc.sbuf_tensor(f"hseg{h}", [128, 512], BF16)) for h in range(24)]
            sg = [e4(nc.sbuf_tensor(f"sg{i}", [128, 512], BF16)) for i in range(2)]
            x2s = [e4(nc.sbuf_tensor(f"x2s{j}_0", [128, 512], BF16)) for j in range(8)]
            x2s2 = [e4(nc.sbuf_tensor(f"x2s{j}_1", [128, 512], BF16)) for j in range(8)]
            x1s = e4(nc.sbuf_tensor("x1s", [128, 4096], F32))
            yst = [e4(nc.sbuf_tensor(f"yst{tb}", [128, 1024], F32)) for tb in range(4)]
            rb2 = [e4(nc.sbuf_tensor(f"rb2{i}", [128, 512], BF16)) for i in range(2)]
            pG = [e4(nc.psum_tensor(f"pG{i}", [128, 512], F32)) for i in range(2)]
            pU = [e4(nc.psum_tensor(f"pU{i}", [128, 512], F32)) for i in range(2)]
            pD = [e4(nc.psum_tensor(f"pD{i}", [128, 512], F32)) for i in range(2)]
            pT2 = [e4(nc.psum_tensor(f"pT2{i}", [128, 128], F32)) for i in range(2)]
            x2buf = [x2s, x2s2]

            S = Sched(cnt, hwm)
            y_pend = []
            for s8 in range(8):
                xp = s8 % 2
                for j in range(8):
                    S.add("sync",
                          lambda e, j=j, s8=s8, xp=xp:
                          e.dma_start(x2buf[xp][j][:],
                                      d_xn2T[j][:, s8 * 512:(s8 + 1) * 512]),
                          waits=[("dg", 2 * 24 * (s8 - 1))],
                          inc="dx2l%d" % xp, incn=16)
                X2L = 16 * (S.n("dx2l%d" % xp) // 16)
                for h in range(24):
                    idx = 24 * s8 + h
                    S.add("sync", lambda e, h=h: e.dma_start(wgS[h % 4][:], d_wgT[h]),
                          waits=[("dg", 2 * (idx - 3)),
                                 ("dwg%d" % (h % 4), 16 * (idx // 4))],
                          inc="dwg%d" % (h % 4), incn=16)
                    S.add("gpsimd", lambda e, h=h: e.dma_start(wuS[h % 4][:], d_wuT[h]),
                          waits=[("dg", 2 * (idx - 3)),
                                 ("dwu%d" % (h % 4), 16 * (idx // 4))],
                          inc="dwu%d" % (h % 4), incn=16)
                    for j in range(8):
                        w = []
                        if j == 0:
                            w = [("dwg%d" % (h % 4), 16 * (idx // 4 + 1)),
                                 ("dsil", idx - 1),
                                 ("dx2l%d" % xp, X2L)]
                        S.add("tensor",
                              lambda e, h=h, j=j, xp=xp:
                              e.matmul(pG[h % 2][:], wgS[h % 4][:, j * 128:(j + 1) * 128],
                                       x2buf[xp][j][:],
                                       start=(j == 0), stop=(j == 7)),
                              waits=w, inc=("dg" if j == 7 else None))
                    S.add("scalar",
                          lambda e, h=h: e.activation(sg[h % 2][:], pG[h % 2][:],
                                                      AF.Silu, bias=0.0, scale=1.0),
                          waits=[("dg", S.n("dg")), ("dmul", idx - 1)], inc="dsil")
                    for j in range(8):
                        w = [("dmul", idx - 1)] if j == 0 else []
                        if j == 0:
                            w.append(("dwu%d" % (h % 4), 16 * (idx // 4 + 1)))
                        S.add("tensor",
                              lambda e, h=h, j=j, xp=xp:
                              e.matmul(pU[h % 2][:], wuS[h % 4][:, j * 128:(j + 1) * 128],
                                       x2buf[xp][j][:],
                                       start=(j == 0), stop=(j == 7)),
                              waits=w, inc=("dg" if j == 7 else None))
                    S.add("vector",
                          lambda e, h=h: e.tensor_mul(hseg[h][:], sg[h % 2][:],
                                                      pU[h % 2][:]),
                          waits=[("dsil", S.n("dsil")), ("dg", S.n("dg")),
                                 ("ddn", 8 * s8)],
                          inc="dmul")
                    if h == 1:
                        for fn in y_pend:
                            fn()
                        y_pend = []
                S.add("sync",
                      lambda e, s8=s8:
                      e.dma_start(x1s[:].rearrange("p (t d) -> p t d", d=1024),
                                  d_x1[s8 * 512:(s8 + 1) * 512, :]
                                  .rearrange("(t p) d -> p t d", p=128)),
                      waits=[("dadd", 32 * s8)], inc="dxl", incn=16)
                for c in range(8):
                    idx2 = 8 * s8 + c
                    S.add("gpsimd", lambda e, c=c: e.dma_start(wdC[c % 4][:], d_wdT[c]),
                          waits=[("ddn", idx2 - 3),
                                 ("dwd%d" % (c % 4), 16 * (idx2 // 4))],
                          inc="dwd%d" % (c % 4), incn=16)
                    for k in range(24):
                        w = []
                        if k == 0:
                            w = [("dwd%d" % (c % 4), 16 * (idx2 // 4 + 1)),
                                 ("dmul", 24 * (s8 + 1)),
                                 ("dev", idx2 - 1)]
                        S.add("tensor",
                              lambda e, c=c, k=k:
                              e.matmul(pD[c % 2][:], wdC[c % 4][:, k * 128:(k + 1) * 128],
                                       hseg[k][:], start=(k == 0), stop=(k == 23)),
                              waits=w, inc=("ddn" if k == 23 else None))
                    S.add("scalar",
                          lambda e, c=c: e.copy(rb2[c % 2][:], pD[c % 2][:]),
                          waits=[("ddn", S.n("ddn")), ("dtp", 4 * (idx2 - 1))],
                          inc="dev")
                    for tb in range(4):
                        S.add("tensor",
                              lambda e, c=c, tb=tb:
                              e.matmul(pT2[tb % 2][:],
                                       rb2[c % 2][:, tb * 128:(tb + 1) * 128],
                                       ident[:], start=True, stop=True),
                              waits=[("dev", S.n("dev")), ("dadd", S.n("dadd") - 1)],
                              inc="dtp")
                        S.add("vector",
                              lambda e, c=c, tb=tb:
                              e.tensor_add(yst[tb][:, c * 128:(c + 1) * 128],
                                           pT2[tb % 2][:],
                                           x1s[:, 1024 * tb + 128 * c:
                                               1024 * tb + 128 * c + 128]),
                              waits=[("dtp", S.n("dtp")), ("dxl", 16 * (s8 + 1)),
                                     ("dy", 64 * s8)],
                              inc="dadd")
                for tb in range(4):
                    def mk(s8=s8, tb=tb):
                        def emit():
                            S.add("sync",
                                  lambda e, s8=s8, tb=tb:
                                  e.dma_start(y_out[s8 * 512 + tb * 128:
                                                    s8 * 512 + (tb + 1) * 128, :],
                                              yst[tb][:]),
                                  waits=[("dadd", 32 * (s8 + 1))],
                                  inc="dy", incn=16)
                        return emit
                    y_pend.append(mk())
            for fn in y_pend:
                fn()
            S.add("sync", lambda e: e.nop(), waits=[("dy", 64 * 8)])
            S.replay(nc, sems)
        barrier()

    return nc, tap_aps


def kernel(**inputs):
    x = np.asarray(inputs["L0_attn_x"], dtype=np.float32)
    Wq_ = np.asarray(inputs["L0_attn_Wq"], dtype=np.float32)
    Wk_ = np.asarray(inputs["L0_attn_Wk"], dtype=np.float32)
    Wv_ = np.asarray(inputs["L0_attn_Wv"], dtype=np.float32)
    Wfc_ = np.asarray(inputs["L0_attn_Wfc"], dtype=np.float32)
    wg_ = np.asarray(inputs["L0_ffn_w_gate"], dtype=np.int32)
    wu_ = np.asarray(inputs["L0_ffn_w_up"], dtype=np.int32)
    wd_ = np.asarray(inputs["L0_ffn_w_down"], dtype=np.int32)

    nc, _ = build_program()
    in_maps = []
    for c in range(8):
        b, h = c // 2, c % 2
        xc = np.ascontiguousarray(x[b, h * 4096:(h + 1) * 4096])
        in_maps.append({"x": xc, "Wq": Wq_, "Wk": Wk_, "Wv": Wv_, "Wfc": Wfc_,
                        "wg": wg_, "wu": wu_, "wd": wd_})
    res = run_bass_kernel_spmd(nc, in_maps, core_ids=list(range(8)))
    y = np.empty((4, 8192, 1024), dtype=np.float32)
    for c in range(8):
        b, h = c // 2, c % 2
        y[b, h * 4096:(h + 1) * 4096] = res.results[c]["y"]
    return y


# version 26 (final)
# speedup vs baseline: 1.1228x; 1.0023x over previous
import math
import numpy as np
import concourse.bass as bass
import concourse.mybir as mybir
from concourse.bass_utils import run_bass_kernel_spmd
from contextlib import ExitStack

F32 = mybir.dt.float32
BF16 = mybir.dt.bfloat16
I32 = mybir.dt.int32
AF = mybir.ActivationFunctionType
ALU = mybir.AluOpType
AX = mybir.AxisListType

DM = 1024
S_OWN = 4096
QK_SCALE = 0.125
DEQ = float(1.0 / 127.0)
EPS = 1e-5
ENGINES = ("sync", "gpsimd", "tensor", "scalar", "vector")
NENG = len(ENGINES)


class Sched:

    def __init__(self, cnt, hwm):
        self.cnt = cnt
        self.hwm = hwm
        self.ops = {e: [] for e in ENGINES}

    def n(self, sem):
        return self.cnt.get(sem, 0)

    def add(self, eng, fn, waits=(), inc=None, incn=1, chain=False, deps=()):
        w = [(s, v) for (s, v) in waits if v > 0]
        dl = list(deps)
        if chain and self.ops[eng]:
            dl.append((eng, len(self.ops[eng]) - 1))
        cnt_after = None
        if inc is not None:
            self.cnt[inc] = self.cnt.get(inc, 0) + incn
            cnt_after = self.cnt[inc]
        op = {"fn": fn, "waits": w, "deps": [], "inc": inc, "incn": incn,
              "cnt_after": cnt_after, "selfneed": False}
        for (teng, tidx) in dl:
            top = self.ops[teng][tidx]
            if top["inc"] is None:
                top["selfneed"] = True
            op["deps"].append((teng, tidx))
        self.ops[eng].append(op)
        return (eng, len(self.ops[eng]) - 1)

    def replay(self, nc, sems):
        hwm = self.hwm
        selfcount = {}
        for eng in ENGINES:
            sem = "sf" + eng
            n = self.cnt.get(sem, 0)
            for idx, op in enumerate(self.ops[eng]):
                if op["inc"] is not None:
                    selfcount[(eng, idx)] = (op["inc"], op["cnt_after"])
                elif op["selfneed"]:
                    n += 1
                    selfcount[(eng, idx)] = (sem, n)
            self.cnt[sem] = n
        with nc.Block() as blk:
            for eng_name in ENGINES:
                ops = self.ops[eng_name]

                def make(ops=ops, eng_name=eng_name):
                    def f(e):
                        for op in ops:
                            waits = list(op["waits"])
                            for tok in op["deps"]:
                                waits.append(selfcount[tok])
                            for (s, v) in waits:
                                key = (eng_name, s)
                                if v > 0 and hwm.get(key, 0) < v:
                                    e.wait_ge(sems[s], v)
                                    hwm[key] = v
                            ins = op["fn"](e)
                            if op["inc"] is not None:
                                ins.then_inc(sems[op["inc"]], op["incn"])
                            elif op["selfneed"]:
                                ins.then_inc(sems["sf" + eng_name], 1)
                    return f

                getattr(blk, eng_name)(make())


def build_program(taps=(), stop_after=None):
    _ORDER = ["ATT", "FFN"]

    def _do(phname):
        return stop_after is None or _ORDER.index(phname) <= _ORDER.index(stop_after)

    nc = bass.Bass("TRN2", target_bir_lowering=False)

    x_in = nc.dram_tensor("x", [S_OWN, DM], F32, kind="ExternalInput").ap()
    Wq = nc.dram_tensor("Wq", [DM, DM], F32, kind="ExternalInput").ap()
    Wk = nc.dram_tensor("Wk", [DM, DM], F32, kind="ExternalInput").ap()
    Wv = nc.dram_tensor("Wv", [DM, DM], F32, kind="ExternalInput").ap()
    Wfc = nc.dram_tensor("Wfc", [DM, DM], F32, kind="ExternalInput").ap()
    wg_in = nc.dram_tensor("wg", [3072, DM], I32, kind="ExternalInput").ap()
    wu_in = nc.dram_tensor("wu", [3072, DM], I32, kind="ExternalInput").ap()
    wd_in = nc.dram_tensor("wd", [DM, 3072], I32, kind="ExternalInput").ap()
    y_out = nc.dram_tensor("y", [S_OWN, DM], F32, kind="ExternalOutput").ap()

    d_WqT = nc.dram_tensor("d_WqT", [8, 128, 1024], BF16).ap()
    d_WkT = nc.dram_tensor("d_WkT", [8, 128, 1024], BF16).ap()
    d_WvT = nc.dram_tensor("d_WvT", [8, 128, 1024], BF16).ap()
    d_WfcT = nc.dram_tensor("d_WfcT", [8, 128, 1024], BF16).ap()
    d_wgT = nc.dram_tensor("d_wgT", [24, 128, 1024], BF16).ap()
    d_wuT = nc.dram_tensor("d_wuT", [24, 128, 1024], BF16).ap()
    d_wdT = nc.dram_tensor("d_wdT", [8, 128, 3072], BF16).ap()
    d_x1 = nc.dram_tensor("d_x1", [S_OWN, DM], F32).ap()
    d_xn2T = nc.dram_tensor("d_xn2T", [8, 128, S_OWN], BF16).ap()

    tap_aps = {}
    for name, shape, dt in [
        ("QKV", [3, 4, 128, 1024], BF16),
        ("A", [4, 128, 256], BF16),
        ("O", [4, 128, 1024], F32),
        ("x1", [S_OWN, DM], F32),
        ("xn2T", [8, 128, S_OWN], BF16),
    ]:
        if name in taps:
            tap_aps[name] = nc.dram_tensor("tap_" + name, shape, dt,
                                           kind="ExternalOutput").ap()

    cnt = {}
    hwm = {}
    bar_k = [0]

    with ExitStack() as TOP:
        E = TOP.enter_context
        sem_names = [
            "ph", "ini",
            "sfsync", "sfgpsimd", "sftensor", "sfscalar", "sfvector",
            "ail0", "ail1", "ail2", "ail3", "afl0", "afl1", "afl2", "afl3",
            "a0c", "a0t", "a0p", "as0", "as1", "as2", "as3",
            "aiu0", "aiu1", "aiu2", "aiu3",
            "wl", "axl0", "axl1", "axl2", "axl3", "aln", "alna", "alnx", "alnr", "axt", "axe",
            "apj", "apev", "asx", "asb", "aex", "amx", "agp", "apl", "aav", "aot", "aote",
            "arp", "arev", "aadd", "ax2", "ax2t", "ax2e", "axs", "ax2s",
            "axs0", "axs1", "ax2s0", "ax2s1",
            "dwg0", "dwg1", "dwg2", "dwg3", "dwu0", "dwu1", "dwu2", "dwu3",
            "dg", "dsil", "dmul", "dwd0", "dwd1", "dwd2", "dwd3", "ddn",
            "dev", "dtp", "dadd", "dxl", "dx2l0", "dx2l1", "dy", "tapd",
        ]
        sems = {s: E(nc.semaphore(s)) for s in sem_names}

        def barrier():
            bar_k[0] += 1
            k = bar_k[0]
            with nc.Block() as blk:
                for en in ENGINES:
                    def f(e, k=k):
                        e.nop().then_inc(sems["ph"], 1)
                        e.wait_ge(sems["ph"], NENG * k)
                    getattr(blk, en)(f)

        def tap_block(items):
            todo = [(nm, fn) for (nm, fn) in items if nm in taps]
            if not todo:
                return
            with nc.Block() as blk:
                @blk.sync
                def _(sy):
                    t0 = cnt.get("tapd", 0)
                    add = [0]
                    def issue(dst, src):
                        sy.dma_start(dst, src).then_inc(sems["tapd"], 16)
                        add[0] += 16
                    for (_, fn) in todo:
                        fn(issue)
                    sy.wait_ge(sems["tapd"], t0 + add[0])
                    cnt["tapd"] = t0 + add[0]
            barrier()

        ident = E(nc.sbuf_tensor("ident", [128, 128], BF16))
        WqJ = [E(nc.sbuf_tensor(f"WqJ{j}", [128, 1024], BF16)) for j in range(8)]
        WkJ = [E(nc.sbuf_tensor(f"WkJ{j}", [128, 1024], BF16)) for j in range(8)]
        WvJ = [E(nc.sbuf_tensor(f"WvJ{j}", [128, 1024], BF16)) for j in range(8)]
        WfJ = [E(nc.sbuf_tensor(f"WfJ{j}", [128, 1024], BF16)) for j in range(8)]

        c_eps = E(nc.sbuf_tensor("c_eps", [128, 1], F32))

        with nc.Block() as blk, \
             nc.sbuf_tensor("io1", [128, 128], F32) as io1, \
             nc.sbuf_tensor("io2", [128, 128], F32) as io2:

            @blk.gpsimd
            def _(g):
                g.iota(io1[:], [[1, 128]], channel_multiplier=0,
                       allow_small_or_imprecise_dtypes=True)
                g.iota(io2[:], [[0, 128]], channel_multiplier=1,
                       allow_small_or_imprecise_dtypes=True)

                g.memset(c_eps[:], EPS).then_inc(sems["ini"], 1)

            @blk.vector
            def _(v):
                v.wait_ge(sems["ini"], 1)
                v.tensor_tensor(ident[:], io1[:], io2[:], ALU.is_equal)
        barrier()

        a0_jobs = []
        for (src, dst, nch, ncol, sm, is_int, ldq) in [
            (Wq, WqJ, 8, 1024, "qk", False, "gpsimd"),
            (Wk, WkJ, 8, 1024, "no", False, "gpsimd"),
            (Wv, WvJ, 8, 1024, "no", False, "gpsimd"),
            (Wfc, WfJ, 8, 1024, "no", False, "gpsimd"),
            (wg_in, d_wgT, 24, 1024, "no", True, "sync"),
            (wu_in, d_wuT, 24, 1024, "no", True, "gpsimd"),
            (wd_in, d_wdT, 8, 3072, "no", True, "sync"),
        ]:
            for c in range(nch):
                a0_jobs.append((src, dst, c, ncol, sm, is_int, ldq))
        NJ = len(a0_jobs)
        nb_of = [ncol // 128 for (_, _, _, ncol, _, _, _) in a0_jobs]
        nb_pre = [sum(nb_of[:j]) for j in range(NJ + 1)]

        with ExitStack() as ph:
            e = ph.enter_context
            wstg_i = [e(nc.sbuf_tensor(f"wstg_i{i}", [128, 3072], I32)) for i in range(4)]
            wstg_b = [e(nc.sbuf_tensor(f"wstg_b{i}", [128, 3072], BF16)) for i in range(4)]
            wpk = [e(nc.sbuf_tensor(f"wpk{i}", [128, 3072], BF16)) for i in range(4)]
            wps = [e(nc.psum_tensor(f"wps{i}", [128, 512], F32)) for i in range(2)]

            S = Sched(cnt, hwm)
            n_ld = {"ail": [0, 0, 0, 0], "aiu": [0, 0, 0, 0]}
            n_fl = [0, 0, 0, 0]
            n_sp = [0, 0, 0, 0]
            for j, (src, dst, c, ncol, sm, is_int, ldq) in enumerate(a0_jobs):
                nb = nb_of[j]
                sl = j % 4
                stg_free = nb_pre[j - 3] if j >= 4 else 0
                if is_int:
                    lsem = "ail" if ldq == "sync" else "aiu"
                    n_ld[lsem][sl] += 1
                    S.add(ldq,
                          lambda e, j=j, src=src, c=c, ncol=ncol:
                          e.dma_start(wstg_i[j % 4][:, 0:ncol],
                                      src[c * 128:(c + 1) * 128, :]),
                          waits=[("a0c", S.n("a0c") - 3)],
                          inc="%s%d" % (lsem, sl), incn=16)
                    S.add("scalar",
                          lambda e, j=j, ncol=ncol:
                          e.activation(wstg_b[j % 4][:, 0:ncol], wstg_i[j % 4][:, 0:ncol],
                                       AF.Copy, bias=0.0, scale=DEQ),
                          waits=[("%s%d" % (lsem, sl), 16 * n_ld[lsem][sl]),
                                 ("a0t", stg_free)],
                          inc="a0c")
                else:
                    n_fl[sl] += 1
                    S.add("gpsimd",
                          lambda e, j=j, src=src, c=c, ncol=ncol:
                          e.dma_start(wstg_b[j % 4][:, 0:ncol],
                                      src[c * 128:(c + 1) * 128, :]),
                          waits=[("a0t", stg_free)],
                          inc="afl%d" % sl, incn=16)
                cast_seen = S.n("a0c")
                for b in range(nb):
                    w = [("a0p", nb_pre[j] + b - 1)]
                    if b == 0:
                        w.append(("a0c", cast_seen) if is_int
                                 else ("afl%d" % sl, 16 * n_fl[sl]))
                    S.add("tensor",
                          lambda e, j=j, b=b:
                          e.matmul(wps[b % 2][:, 0:128],
                                   wstg_b[j % 4][:, b * 128:(b + 1) * 128],
                                   ident[:], start=True, stop=True),
                          waits=w, inc="a0t")
                    pw = [("a0t", S.n("a0t"))]
                    direct = isinstance(dst, list)
                    if b == 0 and not direct:
                        pw.append(("as%d" % sl, 16 * n_sp[sl]))
                    if direct:
                        pdst = (lambda e, dst=dst, b=b, c=c:
                                dst[b][:, c * 128:(c + 1) * 128])
                    else:
                        pdst = (lambda e, j=j, b=b: wpk[j % 4][:, b * 128:(b + 1) * 128])
                    if sm == "qk":
                        S.add("vector",
                              lambda e, pdst=pdst, b=b:
                              e.tensor_scalar(pdst(e), wps[b % 2][:, 0:128],
                                              QK_SCALE, None, ALU.mult),
                              waits=pw, inc="a0p")
                    else:
                        S.add("vector",
                              lambda e, pdst=pdst, b=b:
                              e.tensor_copy(pdst(e), wps[b % 2][:, 0:128]),
                              waits=pw, inc="a0p")
                if not isinstance(dst, list):
                    n_sp[sl] += 1
                    S.add("scalar",
                          lambda e, j=j, dst=dst, c=c, ncol=ncol:
                          e.dma_start(dst[c], wpk[j % 4][:, 0:ncol]),
                          waits=[("a0p", S.n("a0p"))], inc="as%d" % sl, incn=16)
            S.add("sync", lambda e: e.nop(),
                  waits=[("as%d" % i, 16 * n_sp[i]) for i in range(4)])
            S.replay(nc, sems)
        barrier()

        if not _do("ATT"):
            return nc, tap_aps
        with ExitStack() as ph:
            e = ph.enter_context
            xcT = [e(nc.sbuf_tensor(f"xcT{i}", [128, 1024], F32)) for i in range(8)]
            xnm = e(nc.sbuf_tensor("xnm", [128, DM], BF16))
            xnT = [e(nc.sbuf_tensor(f"xnT{j}", [128, 512], BF16)) for j in range(8)]
            Lr = [e(nc.sbuf_tensor(f"Lr{i}", [128, 8], F32)) for i in range(8)]
            Lr2 = [e(nc.sbuf_tensor(f"Lr2_{i}", [128, 8], F32)) for i in range(4)]
            sqd = e(nc.sbuf_tensor("sqd", [128, DM], BF16))
            Qs = [e(nc.sbuf_tensor(f"Qs{t}", [128, 1024], BF16)) for t in range(4)]
            Ks = [e(nc.sbuf_tensor(f"Ks{t}", [128, 1024], BF16)) for t in range(4)]
            Vs = [e(nc.sbuf_tensor(f"Vs{t}", [128, 1024], BF16)) for t in range(8)]
            Ss = e(nc.sbuf_tensor("Ss", [128, 256], F32))
            Sm = e(nc.sbuf_tensor("Sm", [128, 16], F32))
            Se = [e(nc.sbuf_tensor(f"Se{i}", [128, 256], BF16)) for i in range(4)]
            Sl = e(nc.sbuf_tensor("Sl", [128, 16], F32))
            Sr = e(nc.sbuf_tensor("Sr", [128, 16], F32))
            Snr = Sm
            Aa = [e(nc.sbuf_tensor(f"Aa{i}", [128, 256], BF16)) for i in range(4)]
            gramB = [e(nc.sbuf_tensor(f"gramB{i}", [128, 4096], BF16))
                     for i in range(2)]
            avB = [e(nc.sbuf_tensor(f"avB{i}", [128, 4096], BF16)) for i in range(2)]
            gh1 = e(nc.sbuf_tensor("gh1", [128, 2048], BF16))
            OT = [e(nc.sbuf_tensor(f"OT{j}", [128, 512], BF16)) for j in range(8)]
            x1p = [e(nc.sbuf_tensor(f"x1p{i}", [128, 1024], F32)) for i in range(2)]
            xn2m = e(nc.sbuf_tensor("xn2m", [128, DM], BF16))
            xn2s = [e(nc.sbuf_tensor(f"xn2s{j}", [128, 512], BF16)) for j in range(8)]
            pA = [e(nc.psum_tensor(f"pA{i}", [128, 512], F32)) for i in range(2)]
            pB = [e(nc.psum_tensor(f"pB{i}", [128, 512], F32)) for i in range(2)]
            pT = [e(nc.psum_tensor(f"pT{i}", [128, 128], F32)) for i in range(2)]
            pOT = e(nc.psum_tensor("pOT", [128, 1024], F32))

            S = Sched(cnt, hwm)
            cap = {}
            pTw = [None, None]
            pQw = {}

            def ln_acc(Sch, xsrc_fn, Lrt, extra_waits=()):
                Sch.add("scalar",
                        lambda e: e.activation(sqd[:], xsrc_fn(), AF.Copy,
                                               bias=0.0, scale=1.0,
                                               accum_out=Lrt[:, 0:1]),
                        waits=[("alna", Sch.n("alna"))] + list(extra_waits),
                        inc="alna")
                Sch.add("scalar",
                        lambda e: e.activation(sqd[:], xsrc_fn(), AF.Square,
                                               bias=0.0, scale=1.0,
                                               accum_out=Lrt[:, 1:2]),
                        waits=[("alna", Sch.n("alna"))] + list(extra_waits),
                        inc="alna")
                return Sch.n("alna")

            def ln_mid(Sch, Lrt, acc_n):
                Sch.add("vector", lambda e: e.tensor_scalar(
                    Lrt[:, 2:3], Lrt[:, 0:1], -1.0 / 1024.0, None, ALU.mult),
                        waits=[("alna", acc_n)], chain=True)
                Sch.add("vector", lambda e: e.tensor_scalar(
                    Lrt[:, 3:4], Lrt[:, 1:2], 1.0 / 1024.0, None, ALU.mult),
                        chain=True)
                Sch.add("vector", lambda e: e.tensor_mul(
                    Lrt[:, 4:5], Lrt[:, 2:3], Lrt[:, 2:3]), chain=True)
                Sch.add("vector", lambda e: e.tensor_sub(
                    Lrt[:, 5:6], Lrt[:, 3:4], Lrt[:, 4:5]), chain=True, inc="aln")
                Sch.add("scalar",
                        lambda e: e.activation(Lrt[:, 6:7], Lrt[:, 5:6], AF.Ln,
                                               bias=c_eps[:, 0:1], scale=1.0),
                        waits=[("aln", Sch.n("aln")), ("alnx", Sch.n("alnx"))],
                        inc="alnx")
                Sch.add("scalar",
                        lambda e: e.activation(Lrt[:, 7:8], Lrt[:, 6:7], AF.Exp,
                                               bias=0.0, scale=-0.5),
                        waits=[("alnx", Sch.n("alnx"))], inc="alnx")
                return Sch.n("alnx")

            def ln_app(Sch, xsrc_fn, Lrt, out_fn, mid_n, extra_waits=(),
                       out_inc=None):
                Sch.add("vector",
                        lambda e: e.tensor_scalar(out_fn(), xsrc_fn(),
                                                  Lrt[:, 2:3], Lrt[:, 7:8],
                                                  ALU.add, ALU.mult),
                        waits=[("alnx", mid_n)] + list(extra_waits),
                        chain=True, inc=out_inc)

            def load_tile(k):
                w = [("axl%d" % (k % 4), 16 * (k // 4))]
                if k - 8 >= 0:
                    w.append(("arev", 2 * (k - 8 + 1)))
                S.add("sync",
                      lambda e, k=k:
                      e.dma_start(xcT[k % 8][:], x_in[k * 128:(k + 1) * 128, :]),
                      waits=w, inc="axl%d" % (k % 4), incn=16)

            def ln1_a(k):
                w = [("axl%d" % (k % 4), 16 * (k // 4 + 1))]
                if k - 8 >= 0:
                    w.append(("alnr", k - 7))
                cap["l1a", k] = ln_acc(S, (lambda k=k: xcT[k % 8][:]), Lr[k % 8],
                                       extra_waits=w)

            def ln1_m(k):
                cap["l1m", k] = ln_mid(S, Lr[k % 8], cap["l1a", k])

            def ln1_c(k):
                ln_app(S,
                       (lambda k=k: xcT[k % 8][:]),
                       Lr[k % 8],
                       (lambda: xnm[:]),
                       cap["l1m", k],
                       extra_waits=[("axt", 8 * k)],
                       out_inc="alnr")

            def lnt(k):
                i = k % 4
                for b in range(8):
                    w = [("alnr", k + 1)] if b == 0 else []
                    if pTw[b % 2] is not None:
                        w.append(pTw[b % 2])
                    S.add("tensor",
                          lambda e, b=b:
                          e.matmul(pT[b % 2][:], xnm[:, b * 128:(b + 1) * 128],
                                   ident[:], start=True, stop=True),
                          waits=w, inc="axt")
                    ev_w = [("axt", S.n("axt"))]
                    if ("apj", k - 4) in cap:
                        ev_w.append(("apj", cap["apj", k - 4]))
                    S.add("scalar",
                          lambda e, b=b, i=i:
                          e.copy(xnT[b][:, i * 128:(i + 1) * 128], pT[b % 2][:]),
                          waits=ev_w, inc="axe")
                    pTw[b % 2] = ("axe", S.n("axe"))

            def proj(k):
                tt = k % 4
                for (Wjs, Dst, kind) in ((WqJ, Qs[tt], "qk"), (WkJ, Ks[tt], "qk"),
                                         (WvJ, Vs[k % 8], "v")):
                    for half in range(2):
                        pb = (pA if half == 0 else pB)[tt % 2]
                        key = (half, tt % 2)
                        for j in range(8):
                            w = []
                            if j == 0:
                                w = [("axe", 8 * (k + 1))]
                                if pQw.get(key):
                                    w.append(pQw[key])
                            S.add("tensor",
                                  lambda e, Wjs=Wjs, tt=tt, half=half, j=j, pb=pb:
                                  e.matmul(pb[:],
                                           xnT[j][:, tt * 128:(tt + 1) * 128],
                                           Wjs[j][:, half * 512:(half + 1) * 512],
                                           start=(j == 0), stop=(j == 7)),
                                  waits=w, inc=("apj" if j == 7 else None))
                        ev_w = [("apj", S.n("apj"))]
                        if kind == "qk":
                            if ("gram", k - 4) in cap:
                                ev_w.append(("agp", cap["gram", k - 4]))
                        else:
                            if ("pool", k - 8) in cap:
                                ev_w.append(("apl", cap["pool", k - 8]))
                        S.add("scalar",
                              lambda e, Dst=Dst, half=half, pb=pb:
                              e.copy(Dst[:, half * 512:(half + 1) * 512], pb[:]),
                              waits=ev_w, inc="apev")
                        pQw[key] = ("apev", S.n("apev"))
                cap["apev", k] = S.n("apev")
                cap["apj", k] = S.n("apj")

            def front(k):
                tt = k % 4
                e4 = k % 4
                for g in range(4):
                    S.add("vector",
                          lambda e, tt=tt, g=g:
                          e.tensor_tensor(
                              gramB[g % 2][:].rearrange("p (t h d) -> p t h d",
                                                        t=4, d=64),
                              bass.AP(Qs[tt][:].tensor, Qs[tt][:].offset,
                                      [[1024, 128], [0, 4], [64, 16], [1, 64]]),
                              bass.AP(Ks[tt][:].tensor, Ks[tt][:].offset + 256 * g,
                                      [[1024, 128], [64, 4], [0, 16], [1, 64]]),
                              ALU.mult),
                          waits=([("apev", cap["apev", k])] if g == 0 else []),
                          chain=True, inc=None)
                    S.add("vector",
                          lambda e, g=g:
                          e.tensor_tensor(
                              gh1[:].rearrange("p (a b) -> p a b", b=32),
                              bass.AP(gramB[g % 2][:].tensor,
                                      gramB[g % 2][:].offset,
                                      [[4096, 128], [64, 64], [1, 32]]),
                              bass.AP(gramB[g % 2][:].tensor,
                                      gramB[g % 2][:].offset + 32,
                                      [[4096, 128], [64, 64], [1, 32]]),
                              ALU.add),
                          chain=True)
                    S.add("vector",
                          lambda e, g=g:
                          e.tensor_reduce(
                              Ss[:, 64 * g:64 * (g + 1)]
                              .rearrange("p (a b) -> p a b", b=1),
                              gh1[:].rearrange("p (a b) -> p a b", b=32),
                              AX.X, ALU.add),
                          chain=True, inc="agp")
                cap["gram", k] = S.n("agp")
                S.add("vector",
                      lambda e:
                      e.reduce_max(Sm[:, 0:16].rearrange("p (a b) -> p a b", b=1),
                                   bass.AP(Ss[:].tensor, Ss[:].offset,
                                           [[256, 128], [1, 16], [16, 16]]),
                                   AX.X),
                      chain=True)
                sub_w = [("amx", k - 3)] if k - 4 >= 0 else []
                S.add("vector",
                      lambda e, e4=e4:
                      e.tensor_tensor(Se[e4][:].rearrange("p (t h) -> p t h", h=16),
                                      Ss[:].rearrange("p (t h) -> p t h", h=16),
                                      bass.AP(Sm[:].tensor, Sm[:].offset,
                                              [[16, 128], [0, 16], [1, 16]]),
                                      ALU.subtract),
                      waits=sub_w, chain=True, inc="asb")
                S.add("scalar",
                      lambda e, e4=e4: e.activation(Se[e4][:], Se[e4][:], AF.Exp,
                                                    bias=0.0, scale=1.0),
                      waits=[("asb", k + 1)], inc="aex")

            def back_a(k):
                s, tt = divmod(k, 4)
                e4 = k % 4
                S.add("vector",
                      lambda e, e4=e4:
                      e.reduce_sum(Sl[:, 0:16].rearrange("p (a b) -> p a b", b=1),
                                   bass.AP(Se[e4][:].tensor, Se[e4][:].offset,
                                           [[256, 128], [1, 16], [16, 16]]),
                                   AX.X),
                      waits=[("aex", k + 1)])
                S.add("vector", lambda e: e.reciprocal(Sr[:], Sl[:]), chain=True)
                S.add("vector", lambda e: e.tensor_mul(Snr[:], Sl[:], Sr[:]),
                      chain=True)
                S.add("vector",
                      lambda e: e.tensor_scalar(Snr[:], Snr[:], -1.0, 2.0,
                                                ALU.mult, ALU.add),
                      chain=True)
                S.add("vector", lambda e: e.tensor_mul(Sr[:], Sr[:], Snr[:]),
                      chain=True)
                am_w = []
                if ("pool", k - 4) in cap:
                    am_w.append(("apl", cap["pool", k - 4]))
                S.add("vector",
                      lambda e, e4=e4:
                      e.tensor_tensor(Aa[e4][:].rearrange("p (t h) -> p t h", h=16),
                                      Se[e4][:].rearrange("p (t h) -> p t h", h=16),
                                      bass.AP(Sr[:].tensor, Sr[:].offset,
                                              [[16, 128], [0, 16], [1, 16]]),
                                      ALU.mult),
                      waits=am_w, chain=True, inc="amx")
                for g in range(4):
                    S.add("gpsimd",
                          lambda e, e4=e4, g=g, k=k:
                          e.tensor_tensor(
                              avB[g % 2][:].rearrange("p (t h d) -> p t h d",
                                                      t=4, d=64),
                              bass.AP(Aa[e4][:].tensor, Aa[e4][:].offset + 64 * g,
                                      [[256, 128], [16, 4], [1, 16], [0, 64]]),
                              bass.AP(Vs[k % 8][:].tensor,
                                      Vs[k % 8][:].offset + 256 * g,
                                      [[1024, 128], [64, 4], [0, 16], [1, 64]]),
                              ALU.mult),
                          waits=[("amx", k + 1), ("aot", 4 * k + g - 1)],
                          inc="apl")
                cap["pool", k] = S.n("apl")

            def back_a2(k):
                tt = k % 4
                for g in range(4):
                    for i in range(4):
                        for b in range(8):
                            w = []
                            if i == 0 and b == 0:
                                w.append(("apl", 4 * k + g + 1))
                                if g == 0:
                                    w.append(("aote", 8 * k))
                            S.add("tensor",
                                  lambda e, g=g, i=i, b=b:
                                  e.matmul(pOT[:, 128 * b:128 * (b + 1)],
                                           avB[g % 2][:, 1024 * i + 128 * b:
                                                       1024 * i + 128 * (b + 1)],
                                           ident[:],
                                           start=(g == 0 and i == 0
                                                  and b % 4 == 0),
                                           stop=(g == 3 and i == 3
                                                 and b % 4 == 3)),
                                  waits=w,
                                  inc=("aot" if (i == 3 and b == 7) else None))
                ev_w0 = [("aot", 4 * (k + 1))]
                if k - 4 >= 0:
                    ev_w0.append(("arp", 2 * (k - 4 + 1)))
                for b in range(8):
                    S.add("scalar",
                          lambda e, b=b, tt=tt:
                          e.copy(OT[b][:, tt * 128:(tt + 1) * 128],
                                 pOT[:, 128 * b:128 * (b + 1)]),
                          waits=(ev_w0 if b == 0 else []), inc="aote")

            def back_b(k):
                s, tt = divmod(k, 4)
                e4 = k % 4
                for half in range(2):
                    pb = (pA if half == 0 else pB)[tt % 2]
                    key = (half, tt % 2)
                    for j in range(8):
                        w = []
                        if j == 0:
                            w = [("aote", 8 * (k + 1))]
                            if pQw.get(key):
                                w.append(pQw[key])
                        S.add("tensor",
                              lambda e, tt=tt, half=half, j=j, pb=pb:
                              e.matmul(pb[:],
                                       OT[j][:, tt * 128:(tt + 1) * 128],
                                       WfJ[j][:, half * 512:(half + 1) * 512],
                                       start=(j == 0), stop=(j == 7)),
                              waits=w, inc=("arp" if j == 7 else None))
                    r_w = [("arp", S.n("arp"))]
                    if k - 2 >= 0:
                        r_w.append(("axs%d" % (k % 2), 16 * (k // 2)))
                        if ("lnsq", k - 2) in cap:
                            r_w.append(("alna", cap["lnsq", k - 2]))
                    S.add("vector",
                          lambda e, half=half, k=k, pb=pb:
                          e.tensor_add(x1p[k % 2][:, 512 * half:512 * (half + 1)],
                                       pb[:],
                                       xcT[k % 8][:, 512 * half:512 * (half + 1)]),
                          waits=r_w, inc="arev")
                    pQw[key] = ("arev", S.n("arev"))
                S.add("sync",
                      lambda e, k=k:
                      e.dma_start(d_x1[k * 128:(k + 1) * 128, :], x1p[k % 2][:]),
                      waits=[("arev", 2 * (k + 1)),
                             ("axs%d" % (k % 2), 16 * (k // 2))],
                      inc="axs%d" % (k % 2), incn=16)
                w2 = [("arev", 2 * (k + 1))]
                if k - 4 >= 0:
                    w2.append(("ax2", k - 3))
                cap["l2a", k] = ln_acc(S, (lambda k=k: x1p[k % 2][:]), Lr2[e4],
                                       extra_waits=w2)
                cap["lnsq", k] = S.n("alna")

            def back_b2m(k):
                cap["l2m", k] = ln_mid(S, Lr2[k % 4], cap["l2a", k])

            def back_b2c(k):
                s, tt = divmod(k, 4)
                e4 = k % 4
                ln_app(S,
                       (lambda k=k: x1p[k % 2][:]),
                       Lr2[e4],
                       (lambda: xn2m[:]),
                       cap["l2m", k],
                       extra_waits=[("ax2t", 8 * k)],
                       out_inc="ax2")
                for b in range(8):
                    w = [("ax2", k + 1)] if b == 0 else []
                    if pTw[b % 2] is not None:
                        w.append(pTw[b % 2])
                    S.add("tensor",
                          lambda e, b=b:
                          e.matmul(pT[b % 2][:], xn2m[:, b * 128:(b + 1) * 128],
                                   ident[:], start=True, stop=True),
                          waits=w, inc="ax2t")
                    ev_w = [("ax2t", S.n("ax2t"))]
                    if s >= 1:
                        ev_w.append(("ax2s%d" % ((s - 1) % 2),
                                     128 * ((s - 1) // 2 + 1)))
                    S.add("scalar",
                          lambda e, b=b, tt=tt:
                          e.copy(xn2s[b][:, tt * 128:(tt + 1) * 128], pT[b % 2][:]),
                          waits=ev_w, inc="ax2e")
                    pTw[b % 2] = ("ax2e", S.n("ax2e"))
                if tt == 3:
                    for j in range(8):
                        w = [("ax2e", S.n("ax2e"))]
                        if j == 0:
                            w.append(("ax2s%d" % (s % 2), 128 * (s // 2)))
                        S.add("sync",
                              lambda e, j=j, s=s:
                              e.dma_start(d_xn2T[j][:, s * 512:(s + 1) * 512],
                                          xn2s[j][:]),
                              waits=w, inc="ax2s%d" % (s % 2), incn=16)

            def a1(k):
                if 0 <= k <= 31:
                    back_a(k)

            def a2(k):
                if 0 <= k <= 31:
                    back_a2(k)

            def op(k):
                if 0 <= k <= 31:
                    back_b(k)

            def op2m(k):
                if 0 <= k <= 31:
                    back_b2m(k)

            def op2c(k):
                if 0 <= k <= 31:
                    back_b2c(k)

            def g_load(k):
                if 5 <= k <= 31:
                    load_tile(k)

            def g_a(k):
                if 4 <= k <= 31:
                    ln1_a(k)

            def g_m(k):
                if 3 <= k <= 31:
                    ln1_m(k)

            def g_c(k):
                if 2 <= k <= 31:
                    ln1_c(k)
                    lnt(k)
                    proj(k)

            for k in range(5):
                load_tile(k)
            for k in range(4):
                ln1_a(k)
            for k in range(3):
                ln1_m(k)
            for k in range(2):
                ln1_c(k)
                lnt(k)
                proj(k)
            for T in range(36):
                if T <= 31:
                    front(T)
                a1(T - 1)
                a2(T - 2)
                op2c(T - 5)
                op(T - 3)
                op2m(T - 4)
                g_load(T + 5)
                g_a(T + 4)
                g_m(T + 3)
                g_c(T + 2)
            op2c(31)
            S.add("sync", lambda e: e.nop(),
                  waits=[("axs0", 16 * 16), ("axs1", 16 * 16),
                         ("ax2s0", 128 * 4), ("ax2s1", 128 * 4)])
            S.replay(nc, sems)
        barrier()

        tap_block([
            ("x1", lambda issue: issue(tap_aps["x1"], d_x1)),
            ("xn2T", lambda issue: [issue(tap_aps["xn2T"][j], d_xn2T[j])
                                    for j in range(8)]),
        ])

        if not _do("FFN"):
            return nc, tap_aps
        with ExitStack() as ph:
            e4 = ph.enter_context
            wgS = [e4(nc.sbuf_tensor(f"wgS{i}", [128, 1024], BF16)) for i in range(4)]
            wuS = [e4(nc.sbuf_tensor(f"wuS{i}", [128, 1024], BF16)) for i in range(4)]
            wdC = [e4(nc.sbuf_tensor(f"wdC{i}", [128, 3072], BF16)) for i in range(4)]
            hseg = [e4(nc.sbuf_tensor(f"hseg{h}", [128, 512], BF16)) for h in range(24)]
            sg = [e4(nc.sbuf_tensor(f"sg{i}", [128, 512], BF16)) for i in range(2)]
            x2s = [e4(nc.sbuf_tensor(f"x2s{j}_0", [128, 512], BF16)) for j in range(8)]
            x2s2 = [e4(nc.sbuf_tensor(f"x2s{j}_1", [128, 512], BF16)) for j in range(8)]
            x1s = [e4(nc.sbuf_tensor(f"x1s{i}", [128, 4096], F32))
                   for i in range(2)]
            yst = [e4(nc.sbuf_tensor(f"yst{tb}", [128, 1024], F32)) for tb in range(4)]
            rb2 = [e4(nc.sbuf_tensor(f"rb2{i}", [128, 512], BF16)) for i in range(2)]
            pG = [e4(nc.psum_tensor(f"pG{i}", [128, 512], F32)) for i in range(2)]
            pU = [e4(nc.psum_tensor(f"pU{i}", [128, 512], F32)) for i in range(2)]
            pD = [e4(nc.psum_tensor(f"pD{i}", [128, 512], F32)) for i in range(2)]
            pT2 = [e4(nc.psum_tensor(f"pT2{i}", [128, 128], F32)) for i in range(2)]
            x2buf = [x2s, x2s2]

            S = Sched(cnt, hwm)
            y_pend = []
            for s8 in range(8):
                xp = s8 % 2
                for j in range(8):
                    S.add("sync",
                          lambda e, j=j, s8=s8, xp=xp:
                          e.dma_start(x2buf[xp][j][:],
                                      d_xn2T[j][:, s8 * 512:(s8 + 1) * 512]),
                          waits=[("dg", 2 * 24 * (s8 - 1))],
                          inc="dx2l%d" % xp, incn=16)
                X2L = 16 * (S.n("dx2l%d" % xp) // 16)
                for h in range(24):
                    idx = 24 * s8 + h
                    S.add("sync", lambda e, h=h: e.dma_start(wgS[h % 4][:], d_wgT[h]),
                          waits=[("dg", 2 * (idx - 3)),
                                 ("dwg%d" % (h % 4), 16 * (idx // 4))],
                          inc="dwg%d" % (h % 4), incn=16)
                    S.add("gpsimd", lambda e, h=h: e.dma_start(wuS[h % 4][:], d_wuT[h]),
                          waits=[("dg", 2 * (idx - 3)),
                                 ("dwu%d" % (h % 4), 16 * (idx // 4))],
                          inc="dwu%d" % (h % 4), incn=16)
                    for j in range(8):
                        w = []
                        if j == 0:
                            w = [("dwg%d" % (h % 4), 16 * (idx // 4 + 1)),
                                 ("dsil", idx - 1),
                                 ("dx2l%d" % xp, X2L)]
                        S.add("tensor",
                              lambda e, h=h, j=j, xp=xp:
                              e.matmul(pG[h % 2][:], wgS[h % 4][:, j * 128:(j + 1) * 128],
                                       x2buf[xp][j][:],
                                       start=(j == 0), stop=(j == 7)),
                              waits=w, inc=("dg" if j == 7 else None))
                    S.add("scalar",
                          lambda e, h=h: e.activation(sg[h % 2][:], pG[h % 2][:],
                                                      AF.Silu, bias=0.0, scale=1.0),
                          waits=[("dg", S.n("dg")), ("dmul", idx - 1)], inc="dsil")
                    for j in range(8):
                        w = [("dmul", idx - 1)] if j == 0 else []
                        if j == 0:
                            w.append(("dwu%d" % (h % 4), 16 * (idx // 4 + 1)))
                        S.add("tensor",
                              lambda e, h=h, j=j, xp=xp:
                              e.matmul(pU[h % 2][:], wuS[h % 4][:, j * 128:(j + 1) * 128],
                                       x2buf[xp][j][:],
                                       start=(j == 0), stop=(j == 7)),
                              waits=w, inc=("dg" if j == 7 else None))
                    S.add("vector",
                          lambda e, h=h: e.tensor_mul(hseg[h][:], sg[h % 2][:],
                                                      pU[h % 2][:]),
                          waits=[("dsil", S.n("dsil")), ("dg", S.n("dg")),
                                 ("ddn", 8 * s8)],
                          inc="dmul")
                    if h == 1:
                        for fn in y_pend:
                            fn()
                        y_pend = []
                S.add("sync",
                      lambda e, s8=s8:
                      e.dma_start(x1s[s8 % 2][:].rearrange("p (t d) -> p t d",
                                                           d=1024),
                                  d_x1[s8 * 512:(s8 + 1) * 512, :]
                                  .rearrange("(t p) d -> p t d", p=128)),
                      waits=[("dadd", 32 * (s8 - 1)), ("dxl", 16 * s8)],
                      inc="dxl", incn=16)
                for c in range(8):
                    idx2 = 8 * s8 + c
                    S.add("gpsimd", lambda e, c=c: e.dma_start(wdC[c % 4][:], d_wdT[c]),
                          waits=[("ddn", idx2 - 3),
                                 ("dwd%d" % (c % 4), 16 * (idx2 // 4))],
                          inc="dwd%d" % (c % 4), incn=16)
                    for k in range(24):
                        w = []
                        if k == 0:
                            w = [("dwd%d" % (c % 4), 16 * (idx2 // 4 + 1)),
                                 ("dmul", 24 * (s8 + 1)),
                                 ("dev", idx2 - 1)]
                        S.add("tensor",
                              lambda e, c=c, k=k:
                              e.matmul(pD[c % 2][:], wdC[c % 4][:, k * 128:(k + 1) * 128],
                                       hseg[k][:], start=(k == 0), stop=(k == 23)),
                              waits=w, inc=("ddn" if k == 23 else None))
                    S.add("scalar",
                          lambda e, c=c: e.copy(rb2[c % 2][:], pD[c % 2][:]),
                          waits=[("ddn", S.n("ddn")), ("dtp", 4 * (idx2 - 1))],
                          inc="dev")
                    for tb in range(4):
                        S.add("tensor",
                              lambda e, c=c, tb=tb:
                              e.matmul(pT2[tb % 2][:],
                                       rb2[c % 2][:, tb * 128:(tb + 1) * 128],
                                       ident[:], start=True, stop=True),
                              waits=[("dev", S.n("dev")), ("dadd", S.n("dadd") - 1)],
                              inc="dtp")
                        S.add("vector",
                              lambda e, c=c, tb=tb, s8=s8:
                              e.tensor_add(yst[tb][:, c * 128:(c + 1) * 128],
                                           pT2[tb % 2][:],
                                           x1s[s8 % 2][:, 1024 * tb + 128 * c:
                                               1024 * tb + 128 * c + 128]),
                              waits=[("dtp", S.n("dtp")), ("dxl", 16 * (s8 + 1)),
                                     ("dy", 64 * s8)],
                              inc="dadd")
                for tb in range(4):
                    def mk(s8=s8, tb=tb):
                        def emit():
                            S.add("sync",
                                  lambda e, s8=s8, tb=tb:
                                  e.dma_start(y_out[s8 * 512 + tb * 128:
                                                    s8 * 512 + (tb + 1) * 128, :],
                                              yst[tb][:]),
                                  waits=[("dadd", 32 * (s8 + 1))],
                                  inc="dy", incn=16)
                        return emit
                    y_pend.append(mk())
            for fn in y_pend:
                fn()
            S.add("sync", lambda e: e.nop(), waits=[("dy", 64 * 8)])
            S.replay(nc, sems)
        barrier()

    return nc, tap_aps


def kernel(**inputs):
    x = np.asarray(inputs["L0_attn_x"], dtype=np.float32)
    Wq_ = np.asarray(inputs["L0_attn_Wq"], dtype=np.float32)
    Wk_ = np.asarray(inputs["L0_attn_Wk"], dtype=np.float32)
    Wv_ = np.asarray(inputs["L0_attn_Wv"], dtype=np.float32)
    Wfc_ = np.asarray(inputs["L0_attn_Wfc"], dtype=np.float32)
    wg_ = np.asarray(inputs["L0_ffn_w_gate"], dtype=np.int32)
    wu_ = np.asarray(inputs["L0_ffn_w_up"], dtype=np.int32)
    wd_ = np.asarray(inputs["L0_ffn_w_down"], dtype=np.int32)

    nc, _ = build_program()
    in_maps = []
    for c in range(8):
        b, h = c // 2, c % 2
        xc = np.ascontiguousarray(x[b, h * 4096:(h + 1) * 4096])
        in_maps.append({"x": xc, "Wq": Wq_, "Wk": Wk_, "Wv": Wv_, "Wfc": Wfc_,
                        "wg": wg_, "wu": wu_, "wd": wd_})
    res = run_bass_kernel_spmd(nc, in_maps, core_ids=list(range(8)))
    y = np.empty((4, 8192, 1024), dtype=np.float32)
    for c in range(8):
        b, h = c // 2, c % 2
        y[b, h * 4096:(h + 1) * 4096] = res.results[c]["y"]
    return y
